# version 06
# baseline (speedup 1.0000x reference)
import numpy as np
from contextlib import ExitStack

import concourse.bass as bass
import concourse.mybir as mybir
from concourse.bass_utils import run_bass_kernel_spmd

F32 = mybir.dt.float32
BF16 = mybir.dt.bfloat16
AF = mybir.ActivationFunctionType
ALU = mybir.AluOpType
AX = mybir.AxisListType

NCORES = 8
DBG_STAGES = ("ln", "attn", "ffn")
DBG_SUB = 99
DBG_DUMP = False
SEQ = 2048
NSEQ = 2
T = NSEQ * SEQ
D = 1024
NCH = D // 128
NEG = -1e30
SMC = 12.0



def _row(a):
    return np.ascontiguousarray(a.reshape(1, -1))


def _col(a, nch):
    return np.ascontiguousarray(a.reshape(nch, 128).T)


def _w1swz(a, nhc):
    c = a.shape[0] // 128
    return np.ascontiguousarray(
        a.reshape(c, 128, nhc, 128).transpose(2, 1, 0, 3).reshape(nhc, 128, c * 128))


def _prep_weights(inp):
    w = {}
    tr = np.ascontiguousarray
    w["l0_wq"] = tr(np.transpose(inp["L0_attn_wq"], (1, 0, 2)).reshape(D, D))
    w["l0_wk"] = tr(np.transpose(inp["L0_attn_wk"], (1, 0, 2)).reshape(D, D))
    w["l0_wv"] = tr(np.transpose(inp["L0_attn_wv"], (1, 0, 2)).reshape(D, D))
    w["l0_bq"] = _row(inp["L0_attn_bq"])
    w["l0_bk"] = _row(inp["L0_attn_bk"])
    w["l0_bv"] = _row(inp["L0_attn_bv"])
    w["l0_wproj"] = tr(inp["L0_attn_w_proj"])
    w["l0_bproj"] = _row(inp["L0_attn_b_proj"])
    w["l0_gate_w"] = tr(inp["L0_ffn_gate_w"])
    w["l0_gate_b"] = _row(inp["L0_ffn_gate_b"])
    w["l0_w1"] = tr(inp["L0_ffn_w1"])
    w["l0_b1"] = tr(inp["L0_ffn_b1"])
    w["l0_w2"] = tr(inp["L0_ffn_w2"])
    w["l0_b2"] = tr(inp["L0_ffn_b2"])
    w["l1_wqkv"] = tr(inp["L1_attn_w_qkv"])
    w["l1_wproj"] = tr(inp["L1_attn_w_proj"])
    w["l1_bproj"] = _row(inp["L1_attn_b_proj"])
    w["l1_WT"] = tr(inp["L1_ffn_W"].T)
    w["l1_b"] = _row(inp["L1_ffn_b"])
    w["l1_lora_down_T"] = tr(inp["L1_ffn_lora_down"].T)
    w["l1_lora_up_T"] = tr(inp["L1_ffn_lora_up"].T)
    w["l1_down_aux"] = tr(inp["L1_ffn_down_aux"])
    w["l1_up_aux_T"] = tr(inp["L1_ffn_up_aux"].T)
    w["l2_wq"] = tr(np.transpose(inp["L2_attn_W_Q"], (1, 0, 2)).reshape(D, 16))
    w["l2_wk"] = tr(np.transpose(inp["L2_attn_W_K"], (1, 0, 2)).reshape(D, 16))
    w["l2_wv"] = tr(np.transpose(inp["L2_attn_W_V"], (1, 0, 2)).reshape(D, D))
    w["l2_wo"] = tr(np.transpose(inp["L2_attn_W_O"], (2, 1, 0)).reshape(D, D))
    w["l2_gate_wT"] = tr(inp["L2_ffn_gate_w"].T)
    w["l2_gate_b"] = _row(inp["L2_ffn_gate_b"])
    w["l2_w1"] = tr(inp["L2_ffn_w1"])
    w["l2_b1"] = tr(inp["L2_ffn_b1"])
    w["l2_w3"] = tr(inp["L2_ffn_w3"])
    w["l2_b3"] = tr(inp["L2_ffn_b3"])
    w["l2_w2"] = tr(inp["L2_ffn_w2"])
    w["l2_b2"] = tr(inp["L2_ffn_b2"])
    w["l2_sw1"] = tr(inp["L2_ffn_sw1"])
    w["l2_sb1"] = _row(inp["L2_ffn_sb1"])
    w["l2_sw3"] = tr(inp["L2_ffn_sw3"])
    w["l2_sb3"] = _row(inp["L2_ffn_sb3"])
    w["l2_sw2"] = tr(inp["L2_ffn_sw2"])
    w["l2_sb2"] = _row(inp["L2_ffn_sb2"])
    for k in ("wq", "wk", "wv", "wgq", "wgk", "wgv", "r_out_w", "w_out"):
        w["l3_" + k] = tr(inp["L3_attn_" + k])

    def _rep_gate(gw_):
        rep = np.zeros((128, 512), np.float32)
        for h in range(16):
            rep[(h % 2) * 64:(h % 2) * 64 + 64, (h // 2) * 64:(h // 2) * 64 + 64] = \
                np.repeat(gw_[h][:, None], 64, axis=1)
        return rep
    w["l3_glrep"] = _rep_gate(inp["L3_attn_gate_local_w"])
    w["l3_grrep"] = _rep_gate(inp["L3_attn_gate_remote_w"])
    w["l3_w1"] = tr(inp["L3_ffn_w1"][0])
    w["l3_b1"] = _row(inp["L3_ffn_b1"][0])
    w["l3_w2"] = tr(inp["L3_ffn_w2"][0])
    w["l3_b2"] = _row(inp["L3_ffn_b2"][0])
    w["l3_gamma_c"] = tr(inp["L3_ffn_gamma"].reshape(8, 128).T)
    w["l3_beta_c"] = tr(inp["L3_ffn_beta"].reshape(8, 128).T)
    w["l0_bqc"] = _col(w["l0_bq"], 8)
    w["l0_bkc"] = _col(w["l0_bk"], 8)
    w["l0_bprojc"] = _col(w["l0_bproj"], 8)
    w["l1_bprojc"] = _col(w["l1_bproj"], 8)
    w["l1_bc"] = _col(w["l1_b"], 8)
    w["l0_b1c"] = np.stack([_col(inp["L0_ffn_b1"][e], 32) for e in range(8)])
    w["l2_b1c"] = np.stack([_col(inp["L2_ffn_b1"][e], 11) for e in range(8)])
    w["l2_b3c"] = np.stack([_col(inp["L2_ffn_b3"][e], 11) for e in range(8)])
    w["l2_sb1c"] = _col(inp["L2_ffn_sb1"], 11)
    w["l2_sb3c"] = _col(inp["L2_ffn_sb3"], 11)
    w["l3_b1c"] = _col(inp["L3_ffn_b1"][0], 32)
    w["l0_w1s"] = np.stack([_w1swz(inp["L0_ffn_w1"][e], 32) for e in range(8)])
    w["l2_w1s"] = np.stack([_w1swz(inp["L2_ffn_w1"][e], 11) for e in range(8)])
    w["l2_w3s"] = np.stack([_w1swz(inp["L2_ffn_w3"][e], 11) for e in range(8)])
    w["l2_sw1s"] = _w1swz(inp["L2_ffn_sw1"], 11)
    w["l2_sw3s"] = _w1swz(inp["L2_ffn_sw3"], 11)
    w["l3_w1s"] = _w1swz(inp["L3_ffn_w1"][0], 32)
    w["l3_w2s"] = _w1swz(inp["L3_ffn_w2"][0], 8)
    return w


def _prep_consts():
    c = {}
    i128 = np.arange(128)
    m = np.zeros((4, 128, 512), np.float32)
    for dd in range(4):
        for j in range(4):
            blk = m[dd][:, j * 128:(j + 1) * 128]
            d = dd - j
            if d > 0:
                blk[:] = NEG
            elif d == 0:
                blk[i128[:, None] > i128[None, :]] = NEG
    c["c_mask_causal"] = m.reshape(4 * 128, 512)
    c["c_mask_causal2"] = np.concatenate(
        [np.concatenate([m[2 * p], m[2 * p + 1]], axis=1) for p in range(2)], axis=0)
    mw = np.zeros((6, 128, 512), np.float32)
    for idx, rel in enumerate(range(-2, 4)):
        for j in range(4):
            d = rel - j
            blk = mw[idx][:, j * 128:(j + 1) * 128]
            if d == 0:
                blk[i128[:, None] > i128[None, :]] = NEG
            elif d == -1:
                pass
            elif d == -2:
                blk[i128[:, None] <= i128[None, :]] = NEG
            else:
                blk[:] = NEG
    c["c_mask_win"] = mw.reshape(6 * 128, 512)
    c["c_mask_win2"] = np.concatenate(
        [np.concatenate([mw[2 * p], mw[2 * p + 1]], axis=1) for p in range(3)], axis=0)
    lat = np.arange(64)[:, None]
    q = np.arange(2048)[None, :]
    c["c_mask_lat"] = np.where(lat <= q // 32, 0.0, NEG).astype(np.float32)
    PHI = (1.0 + np.sqrt(5.0)) / 2.0
    pos = (np.arange(2048, dtype=np.float32) * np.float32(PHI)) % np.float32(1.0)
    dist = np.abs(pos[:, None] - pos[None, :])
    c["c_phi"] = np.log(np.exp(-dist) + np.float32(1e-8)).astype(np.float32)
    c["c_ident"] = np.eye(128, dtype=np.float32)
    c["c_idx8"] = np.broadcast_to(np.arange(1, 9, dtype=np.float32), (128, 8)).copy()
    ind = np.zeros((16, 8 * 128), np.float32)
    for cc in range(8):
        ind[2 * cc, cc * 128:cc * 128 + 64] = 1.0
        ind[2 * cc + 1, cc * 128 + 64:(cc + 1) * 128] = 1.0
    c["c_l2ind"] = ind
    return c


def _shard_x(x):
    out = []
    for core in range(NCORES):
        xb = x[2 * core:2 * core + 2]
        out.append(np.ascontiguousarray(
            np.transpose(xb, (2, 0, 1)).reshape(D, T)))
    return out


def _unshard_out(shards):
    outs = []
    for s in shards:
        outs.append(np.transpose(np.asarray(s).reshape(D, NSEQ, SEQ), (1, 2, 0)))
    return np.ascontiguousarray(np.concatenate(outs, axis=0)).astype(np.float32)



class Ctx:
    def __init__(self, nc, stack):
        self.nc = nc
        self.stack = stack
        self.dram = {}
        self.g = {}
        self._uid = 0
        self._pool = []
        self._pool_idx = 0

    def uid(self):
        self._uid += 1
        return self._uid

    def sem(self, name):
        if self._pool_idx < len(self._pool):
            h = self._pool[self._pool_idx]
        else:
            h = self.stack.enter_context(self.nc.semaphore(f"sp{len(self._pool)}"))
            self._pool.append(h)
        self._pool_idx += 1
        return h

    def sem_global(self, name):
        return self.stack.enter_context(self.nc.semaphore(f"{name}_{self.uid()}"))

    def begin_stage(self):
        if self._pool:
            with self.nc.Block() as b:
                @b.gpsimd
                def _(e):
                    e.dma_reset()
            with self.nc.Block() as b:
                @b.sync
                def _(e):
                    for h in self._pool:
                        e.sem_clear(h)
        self._pool_idx = 0

    def sbuf(self, stk, name, shape, dt):
        return stk.enter_context(self.nc.sbuf_tensor(f"{name}_{self.uid()}", shape, dt))

    def psum(self, stk, name, shape, dt=F32):
        return stk.enter_context(self.nc.psum_tensor(f"{name}_{self.uid()}", shape, dt))

    def din(self, name, shape, dt=F32):
        h = self.nc.dram_tensor(name, list(shape), dt, kind="ExternalInput").ap()
        self.dram[name] = h
        return h

    def dout(self, name, shape, dt=F32):
        h = self.nc.dram_tensor(name, list(shape), dt, kind="ExternalOutput").ap()
        self.dram[name] = h
        return h

    def dtmp(self, name, shape, dt=F32):
        h = self.nc.dram_tensor(name, list(shape), dt, kind="Internal").ap()
        self.dram[name] = h
        return h


def emit_init(ctx):
    nc = ctx.nc
    g = ctx.g
    g["ones_row"] = ctx.sbuf(ctx.stack, "g_onesrow", [1, 512], BF16)
    g["ones_col"] = ctx.sbuf(ctx.stack, "g_onescol", [128, 64], BF16)
    g["ones128_f"] = ctx.sbuf(ctx.stack, "g_ones128f", [128, 1], F32)
    g["onesrow_f"] = ctx.sbuf(ctx.stack, "g_onesrowf", [1, 128], F32)
    g["ones_col128"] = ctx.sbuf(ctx.stack, "g_onescol128", [128, 128], BF16)
    g["neg_smc"] = ctx.sbuf(ctx.stack, "g_negsmc", [128, 1], F32)
    g["half_pi"] = ctx.sbuf(ctx.stack, "g_halfpi", [128, 1], F32)
    g["ident_bf"] = ctx.sbuf(ctx.stack, "g_identbf", [128, 128], BF16)
    g["ident_f"] = ctx.sbuf(ctx.stack, "g_identf", [128, 128], F32)
    gsem = ctx.sem_global("g_init")
    with nc.Block() as b:
        @b.vector
        def _(e):
            e.memset(g["ones_row"][:], 1.0)
            e.memset(g["ones_col"][:], 1.0)
            e.memset(g["ones128_f"][:], 1.0)
            e.memset(g["onesrow_f"][:], 1.0)
            e.memset(g["ones_col128"][:], 1.0)
            e.memset(g["neg_smc"][:], -SMC)
            e.memset(g["half_pi"][:], float(np.pi / 2))

        @b.gpsimd
        def _(e):
            e.dma_start(out=g["ident_bf"][:], in_=ctx.dram["c_ident"][:, :]).then_inc(gsem, 16)
            e.dma_start(out=g["ident_f"][:], in_=ctx.dram["c_ident"][:, :]).then_inc(gsem, 16)
            e.wait_ge(gsem, 32)



def emit_ln(ctx, x_src, xn_d):
    nc = ctx.nc
    ctx.begin_stage()
    with ExitStack() as st:
        xb = ctx.sbuf(st, "ln_x", [128, 8 * 4096], BF16)
        xsq = ctx.sbuf(st, "ln_xsq", [128, 2 * 512], F32)
        srow = ctx.sbuf(st, "ln_srow", [1, T], F32)
        qrow = ctx.sbuf(st, "ln_qrow", [1, T], F32)
        ones128 = ctx.g["ones128_f"]
        onesrow = ctx.g["onesrow_f"]
        mu_bc = ctx.sbuf(st, "ln_mubc", [128, T], F32)
        rs_bc = ctx.sbuf(st, "ln_rsbc", [128, T], F32)
        tmp = ctx.sbuf(st, "ln_tmp", [128, 4096], F32)
        xnb = ctx.sbuf(st, "ln_xnb", [128, 2 * 4096], BF16)
        ps_s = ctx.psum(st, "ln_ps_s", [128, 512])
        ps_q = ctx.psum(st, "ln_ps_q", [128, 512])

        ld = [ctx.sem(f"ln_ld{i}") for i in range(NCH)]
        sq = ctx.sem("ln_sq")
        mm = ctx.sem("ln_mm")
        ac = ctx.sem("ln_ac")

        with nc.Block() as b:
            @b.gpsimd
            def _(e):
                for c in range(NCH):
                    e.dma_start(out=xb[:, c * 4096:(c + 1) * 4096],
                                in_=x_src[c * 128:(c + 1) * 128, :]).then_inc(ld[c], 16)

            @b.scalar
            def _(e):
                n = 0
                for c in range(NCH):
                    e.wait_ge(ld[c], 16)
                    for tt in range(8):
                        n += 1
                        if n >= 3:
                            e.wait_ge(mm, n - 2)
                        e.activation(xsq[:, (n % 2) * 512:(n % 2 + 1) * 512],
                                     xb[:, c * 4096 + tt * 512:c * 4096 + (tt + 1) * 512],
                                     AF.Square).then_inc(sq, 1)

            @b.tensor
            def _(e):
                n = 0
                for c in range(NCH):
                    for tt in range(8):
                        n += 1
                        e.wait_ge(sq, n)
                        if n >= 2:
                            e.wait_ge(ac, n - 1)
                        e.matmul(ps_s[0:1, :], ctx.g["ones_col"][:, 0:1],
                                 xb[:, c * 4096 + tt * 512:c * 4096 + (tt + 1) * 512],
                                 start=True, stop=True)
                        e.matmul(ps_q[0:1, :], ones128[:],
                                 xsq[:, (n % 2) * 512:(n % 2 + 1) * 512],
                                 start=True, stop=True).then_inc(mm, 1)

            @b.vector
            def _(e):
                n = 0
                for c in range(NCH):
                    for tt in range(8):
                        n += 1
                        e.wait_ge(mm, n)
                        sl = slice(tt * 512, (tt + 1) * 512)
                        if c == 0:
                            e.tensor_copy(srow[:, sl], ps_s[0:1, :])
                            e.tensor_copy(qrow[:, sl], ps_q[0:1, :]).then_inc(ac, 1)
                        else:
                            e.tensor_tensor(srow[:, sl], srow[:, sl], ps_s[0:1, :], op=ALU.add)
                            e.tensor_tensor(qrow[:, sl], qrow[:, sl], ps_q[0:1, :],
                                            op=ALU.add).then_inc(ac, 1)

        with nc.Block() as b:
            @b.vector
            def _(e):
                e.tensor_scalar(srow[:], srow[:], 1.0 / D, None, op0=ALU.mult)
                e.tensor_scalar(qrow[:], qrow[:], 1.0 / D, None, op0=ALU.mult)
                e.drain()
                e.tensor_tensor(tmp[0:1, 0:T], srow[:], srow[:], op=ALU.mult)
                e.drain()
                e.tensor_tensor(qrow[:], qrow[:], tmp[0:1, 0:T], op=ALU.subtract)
                e.drain()
                e.tensor_scalar(qrow[:], qrow[:], 1e-5, None, op0=ALU.add)

        with nc.Block() as b:
            @b.scalar
            def _(e):
                e.activation(tmp[0:1, 0:T], qrow[:], AF.Ln)
                e.drain()
                e.activation(qrow[:], tmp[0:1, 0:T], AF.Exp, scale=-0.5)

        bc = ctx.sem("ln_bc")
        ev = ctx.sem("ln_ev")
        with nc.Block() as b:
            @b.tensor
            def _(e):
                for tt in range(8):
                    if tt >= 1:
                        e.wait_ge(ev, 2 * tt)
                    e.matmul(ps_s[:], onesrow[:], srow[:, tt * 512:(tt + 1) * 512],
                             start=True, stop=True)
                    e.matmul(ps_q[:], onesrow[:], qrow[:, tt * 512:(tt + 1) * 512],
                             start=True, stop=True).then_inc(bc, 1)

            @b.vector
            def _(e):
                for tt in range(8):
                    e.wait_ge(bc, tt + 1)
                    e.tensor_copy(mu_bc[:, tt * 512:(tt + 1) * 512], ps_s[:])
                    e.tensor_copy(rs_bc[:, tt * 512:(tt + 1) * 512], ps_q[:]).then_inc(ev, 2)

        xf = ctx.sbuf(st, "ln_xf", [128, 2 * 4096], F32)
        ld2 = [ctx.sem("ln_ld2a"), ctx.sem("ln_ld2b")]
        nm = ctx.sem("ln_nm")
        st2 = [ctx.sem("ln_st2a"), ctx.sem("ln_st2b")]
        with nc.Block() as b:
            @b.sync
            def _(e):
                for c in range(NCH):
                    if c >= 2:
                        e.wait_ge(nm, c - 1)
                    e.dma_start(out=xf[:, (c % 2) * 4096:(c % 2 + 1) * 4096],
                                in_=x_src[c * 128:(c + 1) * 128, :]).then_inc(ld2[c % 2], 16)

            @b.vector
            def _(e):
                for c in range(NCH):
                    e.wait_ge(ld2[c % 2], (c // 2 + 1) * 16)
                    if c >= 2:
                        e.wait_ge(st2[c % 2], (c // 2) * 16)
                    e.tensor_tensor(tmp[:], xf[:, (c % 2) * 4096:(c % 2 + 1) * 4096],
                                    mu_bc[:], op=ALU.subtract)
                    e.drain()
                    e.tensor_tensor(xnb[:, (c % 2) * 4096:(c % 2 + 1) * 4096],
                                    tmp[:], rs_bc[:], op=ALU.mult).then_inc(nm, 1)
                    e.drain()

            @b.scalar
            def _(e):
                for c in range(NCH):
                    e.wait_ge(nm, c + 1)
                    e.dma_start(out=xn_d[c * 128:(c + 1) * 128, :],
                                in_=xnb[:, (c % 2) * 4096:(c % 2 + 1) * 4096]).then_inc(st2[c % 2], 16)

            @b.gpsimd
            def _(e):
                e.wait_ge(st2[0], (NCH // 2) * 16)
                e.wait_ge(st2[1], (NCH // 2) * 16)



def emit_l1_attn(ctx, xn_d, s_d):
    nc = ctx.nc
    H = 16
    NKT = 16
    for seq in range(NSEQ):
        t0 = seq * SEQ
        ctx.begin_stage()
        with ExitStack() as st:
            qf = ctx.sbuf(st, "l1_q", [128, NCH * SEQ], BF16)
            kf = ctx.sbuf(st, "l1_k", [128, NCH * SEQ], BF16)
            vt = ctx.sbuf(st, "l1_v", [128, NKT * 1024], BF16)
            ao = ctx.sbuf(st, "l1_ao", [128, NCH * SEQ], BF16)

            stp = ExitStack()
            wqk = ctx.sbuf(stp, "l1_wqk", [128, NCH * 2048], BF16)
            wv = ctx.sbuf(stp, "l1_wv", [128, NCH * 1024], BF16)
            xs = ctx.sbuf(stp, "l1_xs", [128, 2 * NCH * 512], BF16)
            pss = [ctx.psum(st, f"l1_ps{i}", [128, 512]) for i in range(2)]

            wl = ctx.sem("l1_wl")
            xl = [ctx.sem("l1_xla"), ctx.sem("l1_xlb")]
            mm = ctx.sem("l1_mm")
            ev = ctx.sem("l1_ev")

            with nc.Block() as b:
                @b.gpsimd
                def _(e):
                    for c in range(NCH):
                        e.dma_start(out=wqk[:, c * 2048:(c + 1) * 2048],
                                    in_=ctx.dram["l1_wqkv"][c * 128:(c + 1) * 128, 0:2048]
                                    ).then_inc(wl, 16)
                        e.dma_start(out=wv[:, c * 1024:(c + 1) * 1024],
                                    in_=ctx.dram["l1_wqkv"][c * 128:(c + 1) * 128, 2048:3072]
                                    ).then_inc(wl, 16)

                @b.sync
                def _(e):
                    for s in range(4):
                        if s >= 2:
                            e.wait_ge(mm, (s - 1) * 24)
                        for c in range(NCH):
                            e.dma_start(
                                out=xs[:, ((s % 2) * NCH + c) * 512:((s % 2) * NCH + c + 1) * 512],
                                in_=xn_d[c * 128:(c + 1) * 128, t0 + s * 512:t0 + (s + 1) * 512]
                            ).then_inc(xl[s % 2], 16)

                @b.tensor
                def _(e):
                    e.wait_ge(wl, NCH * 2 * 16)
                    n = 0
                    for s in range(4):
                        e.wait_ge(xl[s % 2], (s // 2 + 1) * NCH * 16)
                        for m in range(16):
                            n += 1
                            if n >= 3:
                                e.wait_ge(ev, n - 2)
                            for c in range(NCH):
                                ii = e.matmul(pss[n % 2][:],
                                              wqk[:, c * 2048 + m * 128:c * 2048 + (m + 1) * 128],
                                              xs[:, ((s % 2) * NCH + c) * 512:((s % 2) * NCH + c + 1) * 512],
                                              start=(c == 0), stop=(c == NCH - 1))
                            ii.then_inc(mm, 1)
                        for tt in range(4):
                            for hf in range(2):
                                n += 1
                                if n >= 3:
                                    e.wait_ge(ev, n - 2)
                                for c in range(NCH):
                                    ii = e.matmul(
                                        pss[n % 2][:],
                                        xs[:, ((s % 2) * NCH + c) * 512 + tt * 128:
                                           ((s % 2) * NCH + c) * 512 + (tt + 1) * 128],
                                        wv[:, c * 1024 + hf * 512:c * 1024 + (hf + 1) * 512],
                                        start=(c == 0), stop=(c == NCH - 1))
                                ii.then_inc(mm, 1)

                @b.scalar
                def _(e):
                    n = 0
                    for s in range(4):
                        for m in range(16):
                            n += 1
                            e.wait_ge(mm, n)
                            dst = qf if m < 8 else kf
                            mc = m % 8
                            e.activation(dst[:, mc * SEQ + s * 512:mc * SEQ + (s + 1) * 512],
                                         pss[n % 2][:], AF.Copy).then_inc(ev, 1)
                        for tt in range(4):
                            for hf in range(2):
                                n += 1
                                e.wait_ge(mm, n)
                                kt = s * 4 + tt
                                e.activation(vt[:, kt * 1024 + hf * 512:kt * 1024 + (hf + 1) * 512],
                                             pss[n % 2][:], AF.Copy).then_inc(ev, 1)

            stp.close()
            if DBG_SUB < 1:
                continue
            phi_sb = ctx.sbuf(st, "l1_phi", [128, NKT * 512], F32)
            pt_sb = ctx.sbuf(st, "l1_pt", [128, 4 * 512], BF16)
            rln = ctx.sbuf(st, "l1_rln", [64, 2 * 512], F32)
            rin = ctx.sbuf(st, "l1_rin", [64, 2 * 512], F32)
            onc = ctx.g["ones_col"]
            ps_st = [ctx.psum(st, f"l1_pst{i}", [128, 1024]) for i in range(2)]
            ps_r = ctx.psum(st, "l1_psr", [64, 512])
            ps_av = ctx.psum(st, "l1_psav", [64, 512])

            pl = ctx.sem("l1_pl")
            sm = ctx.sem("l1_sm")
            pa = ctx.sem("l1_pa")
            ex = ctx.sem("l1_ex")
            cs = ctx.sem("l1_cs")
            rr = ctx.sem("l1_rr")
            fz = ctx.sem("l1_fz")

            groups = [(qt, h) for qt in range(4) for h in range(H)]
            NKP = NKT // 2

            with nc.Block() as b:
                @b.sync
                def _(e):
                    for qt in range(4):
                        if qt >= 1:
                            e.wait_ge(fz, qt * H)
                        for kt in range(NKT):
                            e.dma_start(
                                out=phi_sb[:, kt * 512:(kt + 1) * 512],
                                in_=ctx.dram["c_phi"][kt * 128:(kt + 1) * 128,
                                                      qt * 512:(qt + 1) * 512]).then_inc(pl, 16)

                @b.vector
                def _(e):
                    np_ = 0
                    for gi, (qt, h) in enumerate(groups):
                        for kp in range(NKP):
                            np_ += 1
                            e.wait_ge(sm, np_)
                            e.scalar_tensor_tensor(ps_st[np_ % 2][:], ps_st[np_ % 2][:],
                                                   0.125,
                                                   phi_sb[:, kp * 1024:(kp + 1) * 1024],
                                                   op0=ALU.mult, op1=ALU.add).then_inc(pa, 1)
                        e.wait_ge(rr, 2 * (gi + 1))
                        ch, hp = h // 2, (h % 2) * 64
                        e.tensor_tensor(ao[hp:hp + 64, ch * SEQ + qt * 512:ch * SEQ + (qt + 1) * 512],
                                        ps_av[:], rin[:, (gi % 2) * 512:(gi % 2 + 1) * 512],
                                        op=ALU.mult).then_inc(fz, 1)

                @b.tensor
                def _(e):
                    np_ = 0
                    for gi, (qt, h) in enumerate(groups):
                        e.wait_ge(pl, (qt + 1) * NKT * 16)
                        ch, hp = h // 2, (h % 2) * 64
                        for kp in range(NKP):
                            np_ += 1
                            if kp == 1 and gi >= 1:
                                e.wait_ge(fz, gi)
                            if np_ >= 3:
                                e.wait_ge(ex, np_ - 2)
                            for half in range(2):
                                kt = 2 * kp + half
                                ii = e.matmul(ps_st[np_ % 2][:, half * 512:(half + 1) * 512],
                                              kf[hp:hp + 64, ch * SEQ + kt * 128:ch * SEQ + (kt + 1) * 128],
                                              qf[hp:hp + 64, ch * SEQ + qt * 512:ch * SEQ + (qt + 1) * 512],
                                              start=True, stop=True)
                            ii.then_inc(sm, 1)
                            if kp >= 1:
                                e.wait_ge(ex, np_ - 1)
                                pb = ((np_ - 1) % 2) * 1024
                                for half in range(2):
                                    kt = 2 * (kp - 1) + half
                                    e.matmul(ps_r[:], onc[:],
                                             pt_sb[:, pb + half * 512:pb + (half + 1) * 512],
                                             start=(kp == 1 and half == 0), stop=False)
                                    e.matmul(ps_av[:],
                                             vt[:, kt * 1024 + h * 64:kt * 1024 + (h + 1) * 64],
                                             pt_sb[:, pb + half * 512:pb + (half + 1) * 512],
                                             start=(kp == 1 and half == 0),
                                             stop=False).then_inc(cs, 1)
                        e.wait_ge(ex, np_)
                        pb = (np_ % 2) * 1024
                        for half in range(2):
                            kt = 2 * (NKP - 1) + half
                            e.matmul(ps_r[:], onc[:],
                                     pt_sb[:, pb + half * 512:pb + (half + 1) * 512],
                                     start=False, stop=(half == 1))
                            e.matmul(ps_av[:],
                                     vt[:, kt * 1024 + h * 64:kt * 1024 + (h + 1) * 64],
                                     pt_sb[:, pb + half * 512:pb + (half + 1) * 512],
                                     start=False, stop=(half == 1)).then_inc(cs, 1)

                @b.scalar
                def _(e):
                    np_ = 0
                    for gi, (qt, h) in enumerate(groups):
                        for kp in range(NKP):
                            np_ += 1
                            e.wait_ge(pa, np_)
                            if np_ >= 3:
                                e.wait_ge(cs, 2 * (np_ - 2))
                            e.activation(pt_sb[:, (np_ % 2) * 1024:(np_ % 2 + 1) * 1024],
                                         ps_st[np_ % 2][:], AF.Exp,
                                         bias=ctx.g["neg_smc"][:]).then_inc(ex, 1)
                        e.wait_ge(cs, (gi + 1) * NKT)
                        if gi >= 2:
                            e.wait_ge(fz, gi - 1)
                        e.activation(rln[:, (gi % 2) * 512:(gi % 2 + 1) * 512],
                                     ps_r[:], AF.Ln).then_inc(rr, 1)
                        e.drain()
                        e.activation(rin[:, (gi % 2) * 512:(gi % 2 + 1) * 512],
                                     rln[:, (gi % 2) * 512:(gi % 2 + 1) * 512],
                                     AF.Exp, scale=-1.0).then_inc(rr, 1)

            if DBG_SUB < 2:
                continue
            wp = ctx.sbuf(st, "l1_wp", [128, NCH * 1024], BF16)
            bpc = ctx.sbuf(st, "l1_bpc", [128, 8], F32)
            orow = ctx.g["ones_row"]
            ob = ctx.sbuf(st, "l1_ob", [128, 2 * 512], F32)

            wl2 = ctx.sem("l1_wl2")
            mm2 = ctx.sem("l1_mm2")
            ev2 = ctx.sem("l1_ev2")
            so2 = [ctx.sem("l1_so2a"), ctx.sem("l1_so2b")]

            with nc.Block() as b:
                @b.gpsimd
                def _(e):
                    for c in range(NCH):
                        e.dma_start(out=wp[:, c * 1024:(c + 1) * 1024],
                                    in_=ctx.dram["l1_wproj"][c * 128:(c + 1) * 128, :]
                                    ).then_inc(wl2, 16)
                    e.dma_start(out=bpc[:], in_=ctx.dram["l1_bprojc"][:, :]).then_inc(wl2, 16)

                @b.tensor
                def _(e):
                    e.wait_ge(wl2, (NCH + 1) * 16)
                    n = 0
                    for s in range(4):
                        for m in range(NCH):
                            n += 1
                            if n >= 3:
                                e.wait_ge(ev2, n - 2)
                            for c in range(NCH):
                                ii = e.matmul(pss[n % 2][:],
                                              wp[:, c * 1024 + m * 128:c * 1024 + (m + 1) * 128],
                                              ao[:, c * SEQ + s * 512:c * SEQ + (s + 1) * 512],
                                              start=(c == 0), stop=(c == NCH - 1))
                            ii.then_inc(mm2, 1)

                @b.scalar
                def _(e):
                    n = 0
                    for s in range(4):
                        for m in range(NCH):
                            n += 1
                            e.wait_ge(mm2, n)
                            if n >= 3:
                                e.wait_ge(so2[n % 2], (n // 2 - 1 + (n % 2)) * 16)
                            e.activation(ob[:, (n % 2) * 512:(n % 2 + 1) * 512],
                                         pss[n % 2][:], AF.Identity,
                                         bias=bpc[:, m:m + 1]).then_inc(ev2, 1)

                @b.sync
                def _(e):
                    n = 0
                    for s in range(4):
                        for m in range(NCH):
                            n += 1
                            e.wait_ge(ev2, n)
                            e.dma_start(out=s_d[m * 128:(m + 1) * 128,
                                               t0 + s * 512:t0 + (s + 1) * 512],
                                        in_=ob[:, (n % 2) * 512:(n % 2 + 1) * 512]
                                        ).then_inc(so2[n % 2], 16)

                @b.gpsimd
                def _(e):
                    e.wait_ge(so2[0], (4 * NCH // 2) * 16)
                    e.wait_ge(so2[1], (4 * NCH // 2) * 16)



def emit_l1_ffn_combine(ctx, xn_d, s_d, x_src, x_dst):
    nc = ctx.nc
    ctx.begin_stage()
    with ExitStack() as st:
        weff = ctx.sbuf(st, "f1_weff", [128, NCH * 1024], BF16)
        t1 = ctx.sbuf(st, "f1_t1", [4, 1024], BF16)
        t2 = ctx.sbuf(st, "f1_t2", [50, 1024], BF16)
        ldn = ctx.sbuf(st, "f1_ldn", [100, 4], BF16)
        dax = ctx.sbuf(st, "f1_dax", [100, 1024], BF16)
        lup = ctx.sbuf(st, "f1_lup", [4, 50], BF16)
        uax = ctx.sbuf(st, "f1_uax", [50, 1024], BF16)
        wtc = ctx.sbuf(st, "f1_wtc", [128, 2 * 1024], BF16)
        bc1 = ctx.sbuf(st, "f1_bc", [128, 8], F32)
        orow = ctx.g["ones_row"]
        pc = [ctx.psum(st, f"f1_pc{i}", [128, 512]) for i in range(2)]

        wl = ctx.sem("f1_wl")
        wts = [ctx.sem("f1_wts0"), ctx.sem("f1_wts1")]
        wlb = ctx.sem("f1_wlb")
        m1 = ctx.sem("f1_m1")
        v1 = ctx.sem("f1_v1")

        with nc.Block() as b:
            @b.gpsimd
            def _(e):
                e.dma_start(out=ldn[:], in_=ctx.dram["l1_lora_down_T"][:, :]).then_inc(wl, 16)
                e.dma_start(out=dax[:], in_=ctx.dram["l1_down_aux"][:, :]).then_inc(wl, 16)
                e.dma_start(out=lup[:], in_=ctx.dram["l1_lora_up_T"][:, :]).then_inc(wl, 16)
                e.dma_start(out=uax[:], in_=ctx.dram["l1_up_aux_T"][:, :]).then_inc(wl, 16)
                for c in range(NCH):
                    if c >= 2:
                        e.wait_ge(v1, 3 + (c - 2))
                    e.dma_start(out=wtc[:, (c % 2) * 1024:(c % 2 + 1) * 1024],
                                in_=ctx.dram["l1_WT"][c * 128:(c + 1) * 128, :]
                                ).then_inc(wts[c % 2], 16)
                e.dma_start(out=bc1[:], in_=ctx.dram["l1_bc"][:, :]).then_inc(wlb, 16)

            @b.tensor
            def _(e):
                e.wait_ge(wl, 4 * 16)
                for hf in range(2):
                    e.matmul(pc[hf][0:4, :], ldn[:], dax[:, hf * 512:(hf + 1) * 512],
                             start=True, stop=True).then_inc(m1, 1)
                e.wait_ge(v1, 1)
                for hf in range(2):
                    e.matmul(pc[hf][0:50, :], lup[:], t1[:, hf * 512:(hf + 1) * 512],
                             start=True, stop=True).then_inc(m1, 1)
                e.wait_ge(v1, 2)
                for c in range(NCH):
                    for hf in range(2):
                        ii = e.matmul(pc[hf][:], t2[:, c * 128:(c + 1) * 128],
                                      uax[:, hf * 512:(hf + 1) * 512],
                                      start=True, stop=True)
                    ii.then_inc(m1, 1)
                    e.wait_ge(v1, 3 + c)

            @b.vector
            def _(e):
                e.wait_ge(m1, 2)
                e.tensor_copy(t1[:, 0:512], pc[0][0:4, :])
                e.tensor_copy(t1[:, 512:1024], pc[1][0:4, :]).then_inc(v1, 1)
                e.wait_ge(m1, 4)
                e.tensor_copy(t2[:, 0:512], pc[0][0:50, :])
                e.tensor_copy(t2[:, 512:1024], pc[1][0:50, :]).then_inc(v1, 1)
                for c in range(NCH):
                    e.wait_ge(m1, 5 + c)
                    e.wait_ge(wts[c % 2], (c // 2 + 1) * 16)
                    e.tensor_tensor(weff[:, c * 1024:c * 1024 + 512],
                                    pc[0][:], wtc[:, (c % 2) * 1024:(c % 2) * 1024 + 512],
                                    op=ALU.add)
                    e.tensor_tensor(weff[:, c * 1024 + 512:(c + 1) * 1024],
                                    pc[1][:], wtc[:, (c % 2) * 1024 + 512:(c % 2 + 1) * 1024],
                                    op=ALU.add).then_inc(v1, 1)

        xs = ctx.sbuf(st, "f1_xs", [128, 2 * NCH * 512], BF16)
        sx = ctx.sbuf(st, "f1_sx", [128, 2 * NCH * 512], F32)
        xr = ctx.sbuf(st, "f1_xr", [128, 2 * NCH * 512], F32)
        gb = ctx.sbuf(st, "f1_gb", [128, 2 * 512], F32)
        xo = ctx.sbuf(st, "f1_xo", [128, 2 * NCH * 512], F32)

        xl = [ctx.sem("f1_xl0"), ctx.sem("f1_xl1")]
        mm = ctx.sem("f1_mm")
        ev = ctx.sem("f1_ev")
        cb = ctx.sem("f1_cb")
        so = [ctx.sem("f1_so0"), ctx.sem("f1_so1")]

        NSL = T // 512
        with nc.Block() as b:
            @b.sync
            def _(e):
                for s in range(NSL):
                    if s >= 2:
                        e.wait_ge(cb, (s - 1) * NCH)
                    r = s % 2
                    for c in range(NCH):
                        e.dma_start(out=xs[:, (r * NCH + c) * 512:(r * NCH + c + 1) * 512],
                                    in_=xn_d[c * 128:(c + 1) * 128, s * 512:(s + 1) * 512]
                                    ).then_inc(xl[r], 16)
                    for c in range(NCH):
                        e.dma_start(out=sx[:, (r * NCH + c) * 512:(r * NCH + c + 1) * 512],
                                    in_=s_d[c * 128:(c + 1) * 128, s * 512:(s + 1) * 512]
                                    ).then_inc(xl[r], 16)
                    for c in range(NCH):
                        e.dma_start(out=xr[:, (r * NCH + c) * 512:(r * NCH + c + 1) * 512],
                                    in_=x_src[c * 128:(c + 1) * 128, s * 512:(s + 1) * 512]
                                    ).then_inc(xl[r], 16)

            @b.tensor
            def _(e):
                e.wait_ge(wlb, 16)
                n = 0
                for s in range(NSL):
                    e.wait_ge(xl[s % 2], (s // 2 + 1) * 3 * NCH * 16)
                    r = s % 2
                    for m in range(NCH):
                        n += 1
                        if n >= 3:
                            e.wait_ge(ev, n - 2)
                        for c in range(NCH):
                            ii = e.matmul(pc[n % 2][:],
                                          weff[:, c * 1024 + m * 128:c * 1024 + (m + 1) * 128],
                                          xs[:, (r * NCH + c) * 512:(r * NCH + c + 1) * 512],
                                          start=(c == 0), stop=(c == NCH - 1))
                        ii.then_inc(mm, 1)

            @b.scalar
            def _(e):
                n = 0
                for s in range(NSL):
                    for m in range(NCH):
                        n += 1
                        e.wait_ge(mm, n)
                        if n >= 3:
                            e.wait_ge(cb, n - 2)
                        e.activation(gb[:, (n % 2) * 512:(n % 2 + 1) * 512],
                                     pc[n % 2][:], AF.Silu,
                                     bias=bc1[:, m:m + 1]).then_inc(ev, 1)

            @b.vector
            def _(e):
                n = 0
                for s in range(NSL):
                    r = s % 2
                    if s >= 2:
                        e.wait_ge(so[r], (s // 2) * NCH * 16)
                    for m in range(NCH):
                        n += 1
                        e.wait_ge(ev, n)
                        sl = slice((r * NCH + m) * 512, (r * NCH + m + 1) * 512)
                        gsl = slice((n % 2) * 512, (n % 2 + 1) * 512)
                        e.tensor_tensor(gb[:, gsl], gb[:, gsl], sx[:, sl], op=ALU.mult)
                        e.drain()
                        e.tensor_tensor(xo[:, sl], xr[:, sl], gb[:, gsl],
                                        op=ALU.add).then_inc(cb, 1)

            @b.gpsimd
            def _(e):
                n = 0
                for s in range(NSL):
                    r = s % 2
                    for m in range(NCH):
                        n += 1
                        e.wait_ge(cb, n)
                        e.dma_start(out=x_dst[m * 128:(m + 1) * 128, s * 512:(s + 1) * 512],
                                    in_=xo[:, (r * NCH + m) * 512:(r * NCH + m + 1) * 512]
                                    ).then_inc(so[r], 16)
                e.wait_ge(so[0], (NSL // 2) * NCH * 16)
                e.wait_ge(so[1], (NSL // 2) * NCH * 16)



LAYER_WEIGHTS = {
    0: ["l0_wq", "l0_wk", "l0_wv", "l0_bqc", "l0_bkc", "l0_bv", "l0_wproj", "l0_bprojc",
        "l0_gate_w", "l0_gate_b", "l0_w1s", "l0_b1c", "l0_w2", "l0_b2"],
    1: ["l1_wqkv", "l1_wproj", "l1_bprojc", "l1_WT", "l1_bc", "l1_lora_down_T",
        "l1_lora_up_T", "l1_down_aux", "l1_up_aux_T"],
    2: ["l2_wq", "l2_wk", "l2_wv", "l2_wo", "l2_gate_wT", "l2_gate_b",
        "l2_w1s", "l2_b1c", "l2_w3s", "l2_b3c", "l2_w2", "l2_b2",
        "l2_sw1s", "l2_sb1c", "l2_sw3s", "l2_sb3c", "l2_sw2", "l2_sb2"],
    3: ["l3_wq", "l3_wk", "l3_wv", "l3_wgq", "l3_wgk", "l3_wgv", "l3_r_out_w",
        "l3_w_out", "l3_glrep", "l3_grrep", "l3_w1s", "l3_b1c",
        "l3_w2s", "l3_b2", "l3_gamma_c", "l3_beta_c"],
}
LAYER_CONSTS = {
    0: ["c_mask_causal2", "c_idx8", "c_ident"],
    1: ["c_phi", "c_ident"],
    2: ["c_idx8", "c_ident", "c_l2ind"],
    3: ["c_mask_win2", "c_mask_lat", "c_ident"],
}


def build(layers, weights_shapes, consts_shapes):
    nc = bass.Bass("TRN2", target_bir_lowering=False)
    with ExitStack() as stack:
        ctx = Ctx(nc, stack)
        x_in = ctx.din("x_in", [D, T])
        x_out = ctx.dout("x_out", [D, T])
        xd = ctx.dtmp("xd", [D, T])
        xn_d = ctx.dtmp("xn_d", [D, T], BF16)
        s_d = ctx.dtmp("s_d", [D, T])
        names = set()
        for L in layers:
            names.update(LAYER_WEIGHTS[L])
            names.update(LAYER_CONSTS[L])
        for n in sorted(names):
            shp = weights_shapes[n] if n in weights_shapes else consts_shapes[n]
            ctx.din(n, shp)

        emit_init(ctx)
        for i, L in enumerate(layers):
            src = x_in if i == 0 else xd
            dst = x_out if i == len(layers) - 1 else xd
            if "ln" in DBG_STAGES:
                emit_ln(ctx, src, xn_d)
            if "attn" in DBG_STAGES:
                (emit_l0_attn, emit_l1_attn, emit_l2_attn, emit_l3_attn)[L](ctx, xn_d, s_d)
            if "ffn" in DBG_STAGES:
                (emit_l0_ffn_combine, emit_l1_ffn_combine, emit_l2_ffn_combine,
                 emit_l3_ffn_combine)[L](ctx, xn_d, s_d, src, dst)
        if DBG_DUMP:
            dxn = ctx.dout("dbg_xn", [D, T], BF16)
            fin = ctx.sem("dbg_fin")
            with nc.Block() as b:
                @b.sync
                def _(e):
                    e.dma_start(out=dxn[:, :], in_=xn_d[:, :]).then_inc(fin, 16)
                    e.wait_ge(fin, 16)
            if "attn" in DBG_STAGES and DBG_SUB >= (3 if 3 in layers else 2):
                dss = ctx.dout("dbg_s", [D, T])
                fin2s = ctx.sem("dbg_fin2s")
                with nc.Block() as b:
                    @b.sync
                    def _(e):
                        e.dma_start(out=dss[:, :], in_=s_d[:, :]).then_inc(fin2s, 16)
                        e.wait_ge(fin2s, 16)
        if "ffn" not in DBG_STAGES:
            fin2 = ctx.sem("dbg_fin2")
            with nc.Block() as b:
                @b.sync
                def _(e):
                    e.dma_start(out=x_out[:, :], in_=x_in[:, :]).then_inc(fin2, 16)
                    e.wait_ge(fin2, 16)
    return nc


def emit_l0_attn(ctx, xn_d, s_d):
    nc = ctx.nc
    H = 16
    for seq in range(NSEQ):
        t0 = seq * SEQ
        ctx.begin_stage()
        with ExitStack() as st:
            qf = ctx.sbuf(st, "l0_q", [128, NCH * SEQ], BF16)
            kf = ctx.sbuf(st, "l0_k", [128, NCH * SEQ], BF16)
            vt = ctx.sbuf(st, "l0_v", [128, 16 * 1024], BF16)
            ao = ctx.sbuf(st, "l0_ao", [128, NCH * SEQ], BF16)
            orow = ctx.g["ones_row"]

            stp = ExitStack()
            wqc = ctx.sbuf(stp, "l0_wq", [128, NCH * 1024], BF16)
            wkc = ctx.sbuf(stp, "l0_wk", [128, NCH * 1024], BF16)
            wvc = ctx.sbuf(stp, "l0_wv", [128, NCH * 1024], BF16)
            bqc = ctx.sbuf(stp, "l0_bqc", [128, 8], F32)
            bkc = ctx.sbuf(stp, "l0_bkc", [128, 8], F32)
            bv = ctx.sbuf(stp, "l0_bv", [1, 1024], BF16)
            xs = ctx.sbuf(stp, "l0_xs", [128, 2 * NCH * 512], BF16)
            pss = [ctx.psum(st, f"l0_ps{i}", [128, 512]) for i in range(2)]

            wl = ctx.sem("l0_wl")
            xl = [ctx.sem("l0_xla"), ctx.sem("l0_xlb")]
            mm = ctx.sem("l0_mm")
            ev = ctx.sem("l0_ev")

            with nc.Block() as b:
                @b.gpsimd
                def _(e):
                    for c in range(NCH):
                        e.dma_start(out=wqc[:, c * 1024:(c + 1) * 1024],
                                    in_=ctx.dram["l0_wq"][c * 128:(c + 1) * 128, :]).then_inc(wl, 16)
                        e.dma_start(out=wkc[:, c * 1024:(c + 1) * 1024],
                                    in_=ctx.dram["l0_wk"][c * 128:(c + 1) * 128, :]).then_inc(wl, 16)
                        e.dma_start(out=wvc[:, c * 1024:(c + 1) * 1024],
                                    in_=ctx.dram["l0_wv"][c * 128:(c + 1) * 128, :]).then_inc(wl, 16)
                    e.dma_start(out=bqc[:], in_=ctx.dram["l0_bqc"][:, :]).then_inc(wl, 16)
                    e.dma_start(out=bkc[:], in_=ctx.dram["l0_bkc"][:, :]).then_inc(wl, 16)
                    e.dma_start(out=bv[:], in_=ctx.dram["l0_bv"][:, :]).then_inc(wl, 16)

                @b.sync
                def _(e):
                    for s in range(4):
                        if s >= 2:
                            e.wait_ge(mm, (s - 1) * 24)
                        for c in range(NCH):
                            e.dma_start(
                                out=xs[:, ((s % 2) * NCH + c) * 512:((s % 2) * NCH + c + 1) * 512],
                                in_=xn_d[c * 128:(c + 1) * 128, t0 + s * 512:t0 + (s + 1) * 512]
                            ).then_inc(xl[s % 2], 16)

                @b.tensor
                def _(e):
                    e.wait_ge(wl, (3 * NCH + 3) * 16)
                    n = 0
                    for s in range(4):
                        e.wait_ge(xl[s % 2], (s // 2 + 1) * NCH * 16)
                        for m in range(16):
                            n += 1
                            if n >= 3:
                                e.wait_ge(ev, n - 2)
                            wmat = wqc if m < 8 else wkc
                            mc = m % 8
                            for c in range(NCH):
                                ii = e.matmul(pss[n % 2][:],
                                              wmat[:, c * 1024 + mc * 128:c * 1024 + (mc + 1) * 128],
                                              xs[:, ((s % 2) * NCH + c) * 512:((s % 2) * NCH + c + 1) * 512],
                                              start=(c == 0), stop=(c == NCH - 1))
                            ii.then_inc(mm, 1)
                        for tt in range(4):
                            for hf in range(2):
                                n += 1
                                if n >= 3:
                                    e.wait_ge(ev, n - 2)
                                for c in range(NCH):
                                    e.matmul(
                                        pss[n % 2][:],
                                        xs[:, ((s % 2) * NCH + c) * 512 + tt * 128:
                                           ((s % 2) * NCH + c) * 512 + (tt + 1) * 128],
                                        wvc[:, c * 1024 + hf * 512:c * 1024 + (hf + 1) * 512],
                                        start=(c == 0), stop=False)
                                e.matmul(pss[n % 2][:], orow[:, 0:128],
                                         bv[:, hf * 512:(hf + 1) * 512],
                                         start=False, stop=True).then_inc(mm, 1)

                @b.scalar
                def _(e):
                    n = 0
                    for s in range(4):
                        for m in range(16):
                            n += 1
                            e.wait_ge(mm, n)
                            dst = qf if m < 8 else kf
                            bcol_t = bqc if m < 8 else bkc
                            mc = m % 8
                            e.activation(dst[:, mc * SEQ + s * 512:mc * SEQ + (s + 1) * 512],
                                         pss[n % 2][:], AF.Identity,
                                         bias=bcol_t[:, mc:mc + 1]).then_inc(ev, 1)
                        for tt in range(4):
                            for hf in range(2):
                                n += 1
                                e.wait_ge(mm, n)
                                kt = s * 4 + tt
                                e.activation(vt[:, kt * 1024 + hf * 512:kt * 1024 + (hf + 1) * 512],
                                             pss[n % 2][:], AF.Copy).then_inc(ev, 1)

            stp.close()
            if DBG_SUB < 1:
                continue
            msk = ctx.sbuf(st, "l0_msk", [128, 2 * 1024], F32)
            pt_sb = ctx.sbuf(st, "l0_pt", [128, 4 * 512], BF16)
            rln = ctx.sbuf(st, "l0_rln", [64, 2 * 512], F32)
            rin = ctx.sbuf(st, "l0_rin", [64, 2 * 512], F32)
            onc = ctx.g["ones_col"]
            ps_st = [ctx.psum(st, f"l0_pst{i}", [128, 1024]) for i in range(2)]
            ps_r = ctx.psum(st, "l0_psr", [64, 512])
            ps_av = ctx.psum(st, "l0_psav", [64, 512])

            ml = ctx.sem("l0_ml")
            sm = ctx.sem("l0_sm")
            pa = ctx.sem("l0_pa")
            ex = ctx.sem("l0_ex")
            cs = ctx.sem("l0_cs")
            rr = ctx.sem("l0_rr")
            fz = ctx.sem("l0_fz")

            groups = [(qt, h) for qt in range(4) for h in range(H)]
            nkp_of = {qt: 2 * qt + 2 for qt in range(4)}

            with nc.Block() as b:
                @b.sync
                def _(e):
                    for pp_ in range(2):
                        e.dma_start(out=msk[:, pp_ * 1024:(pp_ + 1) * 1024],
                                    in_=ctx.dram["c_mask_causal2"][pp_ * 128:(pp_ + 1) * 128, :]
                                    ).then_inc(ml, 16)

                @b.vector
                def _(e):
                    e.wait_ge(ml, 32)
                    np_ = 0
                    for gi, (qt, h) in enumerate(groups):
                        for kp in range(nkp_of[qt]):
                            np_ += 1
                            if kp >= 2 * qt:
                                e.wait_ge(sm, np_)
                                dp = kp - 2 * qt
                                e.tensor_tensor(ps_st[np_ % 2][:], ps_st[np_ % 2][:],
                                                msk[:, dp * 1024:(dp + 1) * 1024],
                                                op=ALU.add).then_inc(pa, 1)
                        e.wait_ge(rr, 2 * (gi + 1))
                        ch, hp = h // 2, (h % 2) * 64
                        e.tensor_tensor(ao[hp:hp + 64, ch * SEQ + qt * 512:ch * SEQ + (qt + 1) * 512],
                                        ps_av[:], rin[:, (gi % 2) * 512:(gi % 2 + 1) * 512],
                                        op=ALU.mult).then_inc(fz, 1)

                @b.tensor
                def _(e):
                    np_ = 0
                    csn = 0
                    for gi, (qt, h) in enumerate(groups):
                        ch, hp = h // 2, (h % 2) * 64
                        nkp = nkp_of[qt]
                        for kp in range(nkp):
                            np_ += 1
                            if kp == 1 and gi >= 1:
                                e.wait_ge(fz, gi)
                            if np_ >= 3:
                                e.wait_ge(ex, np_ - 2)
                            for half in range(2):
                                kt = 2 * kp + half
                                ii = e.matmul(ps_st[np_ % 2][:, half * 512:(half + 1) * 512],
                                              kf[hp:hp + 64, ch * SEQ + kt * 128:ch * SEQ + (kt + 1) * 128],
                                              qf[hp:hp + 64, ch * SEQ + qt * 512:ch * SEQ + (qt + 1) * 512],
                                              start=True, stop=True)
                            ii.then_inc(sm, 1)
                            if kp >= 1:
                                e.wait_ge(ex, np_ - 1)
                                pb = ((np_ - 1) % 2) * 1024
                                for half in range(2):
                                    kt = 2 * (kp - 1) + half
                                    e.matmul(ps_r[:], onc[:],
                                             pt_sb[:, pb + half * 512:pb + (half + 1) * 512],
                                             start=(kp == 1 and half == 0), stop=False)
                                    e.matmul(ps_av[:],
                                             vt[:, kt * 1024 + h * 64:kt * 1024 + (h + 1) * 64],
                                             pt_sb[:, pb + half * 512:pb + (half + 1) * 512],
                                             start=(kp == 1 and half == 0),
                                             stop=False).then_inc(cs, 1)
                        e.wait_ge(ex, np_)
                        pb = (np_ % 2) * 1024
                        for half in range(2):
                            kt = 2 * (nkp - 1) + half
                            e.matmul(ps_r[:], onc[:],
                                     pt_sb[:, pb + half * 512:pb + (half + 1) * 512],
                                     start=False, stop=(half == 1))
                            e.matmul(ps_av[:],
                                     vt[:, kt * 1024 + h * 64:kt * 1024 + (h + 1) * 64],
                                     pt_sb[:, pb + half * 512:pb + (half + 1) * 512],
                                     start=False, stop=(half == 1)).then_inc(cs, 1)
                        csn += 2 * nkp

                @b.scalar
                def _(e):
                    np_ = 0
                    csn = 0
                    npa = 0
                    for gi, (qt, h) in enumerate(groups):
                        for kp in range(nkp_of[qt]):
                            np_ += 1
                            e.wait_ge(sm, np_)
                            if kp >= 2 * qt:
                                npa += 1
                                e.wait_ge(pa, npa)
                            if np_ >= 3:
                                e.wait_ge(cs, 2 * (np_ - 2))
                            e.activation(pt_sb[:, (np_ % 2) * 1024:(np_ % 2 + 1) * 1024],
                                         ps_st[np_ % 2][:], AF.Exp, scale=0.125,
                                         bias=ctx.g["neg_smc"][:]).then_inc(ex, 1)
                        csn += 2 * nkp_of[qt]
                        e.wait_ge(cs, csn)
                        if gi >= 2:
                            e.wait_ge(fz, gi - 1)
                        e.activation(rln[:, (gi % 2) * 512:(gi % 2 + 1) * 512],
                                     ps_r[:], AF.Ln).then_inc(rr, 1)
                        e.drain()
                        e.activation(rin[:, (gi % 2) * 512:(gi % 2 + 1) * 512],
                                     rln[:, (gi % 2) * 512:(gi % 2 + 1) * 512],
                                     AF.Exp, scale=-1.0).then_inc(rr, 1)

            if DBG_SUB < 2:
                continue
            wp = ctx.sbuf(st, "l0_wp", [128, NCH * 1024], BF16)
            bpc = ctx.sbuf(st, "l0_bpc", [128, 8], F32)
            ob = ctx.sbuf(st, "l0_ob", [128, 2 * 512], F32)

            wl2 = ctx.sem("l0_wl2")
            mm2 = ctx.sem("l0_mm2")
            ev2 = ctx.sem("l0_ev2")
            so2 = [ctx.sem("l0_so2a"), ctx.sem("l0_so2b")]

            with nc.Block() as b:
                @b.gpsimd
                def _(e):
                    for c in range(NCH):
                        e.dma_start(out=wp[:, c * 1024:(c + 1) * 1024],
                                    in_=ctx.dram["l0_wproj"][c * 128:(c + 1) * 128, :]
                                    ).then_inc(wl2, 16)
                    e.dma_start(out=bpc[:], in_=ctx.dram["l0_bprojc"][:, :]).then_inc(wl2, 16)

                @b.tensor
                def _(e):
                    e.wait_ge(wl2, (NCH + 1) * 16)
                    n = 0
                    for s in range(4):
                        for m in range(NCH):
                            n += 1
                            if n >= 3:
                                e.wait_ge(ev2, n - 2)
                            for c in range(NCH):
                                ii = e.matmul(pss[n % 2][:],
                                              wp[:, c * 1024 + m * 128:c * 1024 + (m + 1) * 128],
                                              ao[:, c * SEQ + s * 512:c * SEQ + (s + 1) * 512],
                                              start=(c == 0), stop=(c == NCH - 1))
                            ii.then_inc(mm2, 1)

                @b.scalar
                def _(e):
                    n = 0
                    for s in range(4):
                        for m in range(NCH):
                            n += 1
                            e.wait_ge(mm2, n)
                            if n >= 3:
                                e.wait_ge(so2[n % 2], (n // 2 - 1 + (n % 2)) * 16)
                            e.activation(ob[:, (n % 2) * 512:(n % 2 + 1) * 512],
                                         pss[n % 2][:], AF.Identity,
                                         bias=bpc[:, m:m + 1]).then_inc(ev2, 1)

                @b.sync
                def _(e):
                    n = 0
                    for s in range(4):
                        for m in range(NCH):
                            n += 1
                            e.wait_ge(ev2, n)
                            e.dma_start(out=s_d[m * 128:(m + 1) * 128,
                                               t0 + s * 512:t0 + (s + 1) * 512],
                                        in_=ob[:, (n % 2) * 512:(n % 2 + 1) * 512]
                                        ).then_inc(so2[n % 2], 16)

                @b.gpsimd
                def _(e):
                    e.wait_ge(so2[0], (4 * NCH // 2) * 16)
                    e.wait_ge(so2[1], (4 * NCH // 2) * 16)


def emit_l0_ffn_combine(ctx, xn_d, s_d, x_src, x_dst):
    emit_moe_ffn_combine(ctx, xn_d, s_d, x_src, x_dst, kind="l0")


def emit_l2_ffn_combine(ctx, xn_d, s_d, x_src, x_dst):
    emit_moe_ffn_combine(ctx, xn_d, s_d, x_src, x_dst, kind="l2")


def emit_moe_routing(ctx, xn_d, rw_sb, kind, st):
    ctx.begin_stage()
    """Routing pre-pass: writes per-token per-expert combine weights as fp32
    columns rw_sb [128, NTILE*8] (tile-major, 8 expert cols per 128-token tile).

    l0: rw[:, e] = 2.0 * (e_star == e), e_star = max index of top-2 logits
        (logits = xn @ gate_w + gate_b); expert 0 never selected.
    l2: rw[:, e] = softmax(logits)[e] * (e in top2 of (softmax + gate_b)).
    """
    nc = ctx.nc
    NSL = T // 512
    gate_w_name = "l0_gate_w" if kind == "l0" else "l2_gate_wT"
    gate_b_name = "l0_gate_b" if kind == "l0" else "l2_gate_b"

    idx8p = ctx.sbuf(st, "mo_idx", [128, 8], F32)
    gbbc = ctx.sbuf(st, "mo_gbbc", [128, 8], F32)
    t8 = ctx.sbuf(st, "mo_t8", [128, 8], F32)
    p8 = ctx.sbuf(st, "mo_p8", [128, 8], F32)
    v8 = ctx.sbuf(st, "mo_v8", [128, 8], F32)
    col = ctx.sbuf(st, "mo_col", [128, 4], F32)
    gw = ctx.sbuf(st, "mo_gw", [128, NCH * 8], BF16)
    gbr = ctx.sbuf(st, "mo_gb", [1, 8], BF16)
    xs = ctx.sbuf(st, "mo_xs", [128, 2 * NCH * 512], BF16)
    orow = ctx.g["ones_row"]
    ps_l = [ctx.psum(st, f"mo_psl{i}", [128, 512]) for i in range(2)]

    wl = ctx.sem("mo_wl")
    xl = [ctx.sem("mo_xl0"), ctx.sem("mo_xl1")]
    mm = ctx.sem("mo_mm")
    va = ctx.sem("mo_va")
    sa = ctx.sem("mo_sa")
    vr = ctx.sem("mo_vr")
    gbc = ctx.sem("mo_gbc")

    with nc.Block() as b:
        @b.gpsimd
        def _(e):
            for c in range(NCH):
                e.dma_start(out=gw[:, c * 8:(c + 1) * 8],
                            in_=ctx.dram[gate_w_name][c * 128:(c + 1) * 128, :]
                            ).then_inc(wl, 16)
            e.dma_start(out=gbr[:], in_=ctx.dram[gate_b_name][:, :]).then_inc(wl, 16)
            e.dma_start(out=idx8p[:], in_=ctx.dram["c_idx8"][:, :]).then_inc(wl, 16)

        @b.sync
        def _(e):
            for s in range(NSL):
                if s >= 2:
                    e.wait_ge(vr, (s - 1) * 4)
                for c in range(NCH):
                    e.dma_start(
                        out=xs[:, ((s % 2) * NCH + c) * 512:((s % 2) * NCH + c + 1) * 512],
                        in_=xn_d[c * 128:(c + 1) * 128, s * 512:(s + 1) * 512]
                    ).then_inc(xl[s % 2], 16)

        @b.tensor
        def _(e):
            e.wait_ge(wl, (NCH + 2) * 16)
            if kind == "l2":
                e.matmul(ps_l[1][0:128, 0:8], orow[0:1, 0:128], gbr[:],
                         start=True, stop=True).then_inc(mm, 1)
            else:
                e.nop().then_inc(mm, 1)
            n = 0
            pfree = sa if kind == "l2" else vr
            for s in range(NSL):
                e.wait_ge(xl[s % 2], (s // 2 + 1) * NCH * 16)
                for tt in range(4):
                    n += 1
                    if n == 1 and kind == "l2":
                        e.wait_ge(gbc, 1)
                    if n >= 3:
                        e.wait_ge(pfree, n - 2)
                    for c in range(NCH):
                        ii = e.matmul(ps_l[n % 2][0:128, 0:8],
                                      xs[:, ((s % 2) * NCH + c) * 512 + tt * 128:
                                         ((s % 2) * NCH + c) * 512 + (tt + 1) * 128],
                                      gw[:, c * 8:(c + 1) * 8],
                                      start=(c == 0), stop=(c == NCH - 1 and kind == "l2"))
                    if kind == "l0":
                        ii = e.matmul(ps_l[n % 2][0:128, 0:8], orow[:, 0:128], gbr[:],
                                      start=False, stop=True)
                    ii.then_inc(mm, 1)

        if kind == "l2":
            @b.scalar
            def _(e):
                n = 0
                for s in range(NSL):
                    for tt in range(4):
                        n += 1
                        e.wait_ge(va, 2 * n)
                        e.activation(p8[:], ps_l[n % 2][0:128, 0:8], AF.Exp,
                                     bias=col[:, 0:1], accum_out=col[:, 1:2]
                                     ).then_inc(sa, 1)

        @b.vector
        def _(e):
            if kind == "l2":
                e.wait_ge(mm, 1)
                e.tensor_copy(gbbc[:], ps_l[1][0:128, 0:8]).then_inc(gbc, 1)
            n = 0
            for s in range(NSL):
                for tt in range(4):
                    n += 1
                    tile = s * 4 + tt
                    e.wait_ge(mm, n + 1)
                    if kind == "l0":
                        e.max(t8[:], ps_l[n % 2][0:128, 0:8])
                        e.drain()
                        e.tensor_scalar(p8[:], ps_l[n % 2][0:128, 0:8], t8[:, 1:2],
                                        None, op0=ALU.is_ge)
                        e.drain()
                        e.tensor_tensor(v8[:], p8[:], idx8p[:], op=ALU.mult)
                        e.drain()
                        e.reduce_max(col[:, 0:1], v8[:], axis=AX.X)
                        e.drain()
                        e.memset(rw_sb[:, tile * 8:tile * 8 + 1], 0.0)
                        for ex in range(1, 8):
                            ii = e.tensor_scalar(rw_sb[:, tile * 8 + ex:tile * 8 + ex + 1],
                                                 col[:, 0:1], float(ex + 1), 2.0,
                                                 op0=ALU.is_equal, op1=ALU.mult)
                        ii.then_inc(vr, 1)
                    else:
                        e.reduce_max(col[:, 0:1], ps_l[n % 2][0:128, 0:8],
                                     axis=AX.X, negate=True)
                        e.drain()
                        e.nop().then_inc(va, 2)
                        e.wait_ge(sa, n)
                        e.reciprocal(col[:, 2:3], col[:, 1:2])
                        e.drain()
                        e.tensor_scalar(p8[:], p8[:], col[:, 2:3], None, op0=ALU.mult)
                        e.drain()
                        e.tensor_tensor(t8[:], p8[:], gbbc[:], op=ALU.add)
                        e.drain()
                        e.max(v8[:], t8[:])
                        e.drain()
                        e.tensor_scalar(v8[:], t8[:], v8[:, 1:2], None, op0=ALU.is_ge)
                        e.drain()
                        e.tensor_tensor(rw_sb[:, tile * 8:(tile + 1) * 8], p8[:], v8[:],
                                        op=ALU.mult).then_inc(vr, 1)


def emit_moe_experts(ctx, xn_d, rw_sb, y_d, kind, st):
    ctx.begin_stage()
    """Expert sweeps: y_d[t, d] (token-major bf16 DRAM) = sum_e w_e(t) * y_e(t,:).

    Token groups of TG; per (group, expert): stream w1 (and w3) column blocks,
    full w2; h = act(xn@w1+b1) feature-major in SBUF; y accumulated token-major
    in PSUM over hidden chunks then combined into yac with routing columns.
    """
    nc = ctx.nc
    if kind == "l0":
        experts = [("l0_w1s", "l0_b1c", "l0_w2", "l0_b2", e, "rw") for e in range(1, 8)]
        nhc, TG = 32, 1024
    else:
        experts = [("l2_w1s", "l2_b1c", "l2_w2", "l2_b2", e, "rw") for e in range(8)]
        experts.append(("l2_sw1s", "l2_sb1c", "l2_sw2", "l2_sb2", None, "one"))
        nhc, TG = 11, 2048
    NE = len(experts)
    NG = T // TG
    NTS = TG // 512
    NTT = TG // 128
    l2 = kind == "l2"

    xg = ctx.sbuf(st, "me_xg", [128, NCH * TG], BF16)
    hsb = ctx.sbuf(st, "me_h", [128, nhc * TG], BF16)
    w1t = ctx.sbuf(st, "me_w1t", [128, 4 * NCH * 128], BF16)
    w3t = ctx.sbuf(st, "me_w3t", [128, 4 * NCH * 128], BF16) if l2 else None
    w2sb = ctx.sbuf(st, "me_w2sb", [128, nhc * 1024], BF16)
    b1c = ctx.sbuf(st, "me_b1c", [128, 2 * nhc], F32)
    b3c = ctx.sbuf(st, "me_b3c", [128, 2 * nhc], F32) if l2 else None
    b2r = ctx.sbuf(st, "me_b2r", [1, 2 * 1024], BF16)
    yac = ctx.sbuf(st, "me_yac", [128, NTT * 1024], BF16)
    asb = ctx.sbuf(st, "me_asb", [128, 2 * 512], F32) if l2 else None
    psb = ctx.sbuf(st, "me_psb", [128, 2 * 512], F32) if l2 else None
    orow = ctx.g["ones_row"]
    ps_h = [ctx.psum(st, f"me_psh{i}", [128, 512]) for i in range(2)]
    ps_g = [ctx.psum(st, f"me_psg{i}", [128, 512]) for i in range(2)] if l2 else None
    ps_y = [ctx.psum(st, f"me_psy{i}", [128, 512]) for i in range(2)]

    xgl = ctx.sem("me_xgl")
    w1l = [ctx.sem(f"me_w1l{i}") for i in range(4)]
    w2l = ctx.sem("me_w2l")
    b1l = ctx.sem("me_b1l")
    hm = ctx.sem("me_hm")
    he = ctx.sem("me_he")
    hv = ctx.sem("me_hv") if l2 else None
    ym = ctx.sem("me_ym")
    yv = ctx.sem("me_yv")
    yst = [ctx.sem("me_yst0"), ctx.sem("me_yst1")]

    HEPI = 2 if l2 else 1
    NBW = 3 if l2 else 2
    NW1 = 2 if l2 else 1

    def wslices(w1n, b1n, w2n, b2n, enum):
        w1d = ctx.dram[w1n] if enum is None else ctx.dram[w1n][enum]
        b1d = ctx.dram[b1n] if enum is None else ctx.dram[b1n][enum]
        w2d = ctx.dram[w2n] if enum is None else ctx.dram[w2n][enum]
        b2d = ctx.dram[b2n] if enum is None else ctx.dram[b2n][enum:enum + 1]
        return w1d, b1d, w2d, b2d

    with nc.Block() as b:
        @b.gpsimd
        def _(e):
            for gi in range(NG):
                if gi >= 1:
                    e.wait_ge(ym, gi * NE * NTT * 2)
                for c in range(NCH):
                    e.dma_start(out=xg[:, c * TG:(c + 1) * TG],
                                in_=xn_d[c * 128:(c + 1) * 128, gi * TG:(gi + 1) * TG]
                                ).then_inc(xgl, 16)
                for ei, (w1n, b1n, w2n, b2n, enum, _) in enumerate(experts):
                    gg = gi * NE + ei
                    w1d, b1d, w2d, b2d = wslices(w1n, b1n, w2n, b2n, enum)
                    if gg >= 2:
                        e.wait_ge(ym, (gg - 1) * NTT * 2)
                    bo = (gg % 2) * nhc
                    e.dma_start(out=b1c[:, bo:bo + nhc], in_=b1d[:, :]).then_inc(b1l, 16)
                    if l2:
                        b3d = ctx.dram["l2_b3c"][enum] if enum is not None \
                            else ctx.dram["l2_sb3c"]
                        e.dma_start(out=b3c[:, bo:bo + nhc], in_=b3d[:, :]).then_inc(b1l, 16)
                    e.dma_start(out=b2r[:, (gg % 2) * 1024:(gg % 2 + 1) * 1024],
                                in_=b2d[:, :]).then_inc(b1l, 16)
                    for fc in range(nhc):
                        gfc = gg * nhc + fc
                        if fc == 0:
                            if gg >= 1:
                                e.wait_ge(ym, gg * NTT * 2)
                            e.dma_start(out=w2sb[:, 0:nhc * 1024],
                                        in_=w2d[:, :].rearrange("(c p) d -> p c d", p=128)
                                        ).then_inc(w2l, 16)
                        if gfc >= 4:
                            e.wait_ge(hm, (gfc - 3) * NTS)
                        e.dma_start(
                            out=w1t[:, (gfc % 4) * NCH * 128:(gfc % 4 + 1) * NCH * 128],
                            in_=w1d[fc]).then_inc(w1l[gfc % 4], 16)
                        if l2:
                            w3d = ctx.dram["l2_w3s"][enum] if enum is not None \
                                else ctx.dram["l2_sw3s"]
                            e.dma_start(
                                out=w3t[:, (gfc % 4) * NCH * 128:(gfc % 4 + 1) * NCH * 128],
                                in_=w3d[fc]).then_inc(w1l[gfc % 4], 16)

        @b.tensor
        def _(e):
            nh = 0
            ny = 0
            for gi in range(NG):
                e.wait_ge(xgl, (gi + 1) * NCH * 16)
                for ei, (w1n, b1n, w2n, b2n, enum, wsel) in enumerate(experts):
                    gg = gi * NE + ei
                    for fc in range(nhc):
                        gfc = gg * nhc + fc
                        for ts in range(NTS):
                            nh += 1
                            if ts == 0:
                                e.wait_ge(w1l[gfc % 4], (gfc // 4 + 1) * NW1 * 16)
                                if fc == 0:
                                    e.wait_ge(b1l, (gg + 1) * NBW * 16)
                            if nh >= 3:
                                e.wait_ge(hv if l2 else he, (nh - 2) * (1 if l2 else HEPI))
                            for c in range(NCH):
                                ii = e.matmul(ps_h[nh % 2][:],
                                              w1t[:, (gfc % 4) * NCH * 128 + c * 128:
                                                  (gfc % 4) * NCH * 128 + (c + 1) * 128],
                                              xg[:, c * TG + ts * 512:c * TG + (ts + 1) * 512],
                                              start=(c == 0), stop=(c == NCH - 1))
                            if l2:
                                for c in range(NCH):
                                    ii = e.matmul(ps_g[nh % 2][:],
                                                  w3t[:, (gfc % 4) * NCH * 128 + c * 128:
                                                      (gfc % 4) * NCH * 128 + (c + 1) * 128],
                                                  xg[:, c * TG + ts * 512:c * TG + (ts + 1) * 512],
                                                  start=(c == 0), stop=(c == NCH - 1))
                            ii.then_inc(hm, 1)
                    e.wait_ge(w2l, (gg + 1) * 16)
                    e.wait_ge(he, (gg + 1) * nhc * NTS * HEPI)
                    for tt in range(NTT):
                        for hf in range(2):
                            ny += 1
                            if ny >= 3:
                                e.wait_ge(yv, ny - 2)
                            for fc2 in range(nhc):
                                e.matmul(ps_y[ny % 2][:],
                                         hsb[:, fc2 * TG + tt * 128:fc2 * TG + (tt + 1) * 128],
                                         w2sb[:, fc2 * 1024 + hf * 512:fc2 * 1024 + (hf + 1) * 512],
                                         start=(fc2 == 0), stop=False)
                            e.matmul(ps_y[ny % 2][:], orow[0:1, 0:128],
                                     b2r[:, (gg % 2) * 1024 + hf * 512:
                                         (gg % 2) * 1024 + (hf + 1) * 512],
                                     start=False, stop=True).then_inc(ym, 1)

        @b.scalar
        def _(e):
            nh = 0
            for gi in range(NG):
                for ei in range(NE):
                    gg = gi * NE + ei
                    if gg >= 1:
                        e.wait_ge(ym, gg * NTT * 2)
                    for fc in range(nhc):
                        for ts in range(NTS):
                            nh += 1
                            e.wait_ge(hm, nh)
                            if kind == "l0":
                                e.activation(hsb[:, fc * TG + ts * 512:fc * TG + (ts + 1) * 512],
                                             ps_h[nh % 2][:], AF.Gelu,
                                             bias=b1c[:, (gg % 2) * nhc + fc:
                                                       (gg % 2) * nhc + fc + 1]).then_inc(he, 1)
                            else:
                                e.activation(asb[:, (nh % 2) * 512:(nh % 2 + 1) * 512],
                                             ps_h[nh % 2][:], AF.Identity,
                                             bias=b1c[:, (gg % 2) * nhc + fc:
                                                       (gg % 2) * nhc + fc + 1]).then_inc(he, 1)
                                e.wait_ge(hv, nh)
                                e.activation(hsb[:, fc * TG + ts * 512:fc * TG + (ts + 1) * 512],
                                             psb[:, (nh % 2) * 512:(nh % 2 + 1) * 512],
                                             AF.Silu).then_inc(he, 1)

        if l2:
            @b.vector
            def _(e):
                nh = 0
                ny = 0
                for gi in range(NG):
                    if gi >= 1:
                        e.wait_ge(yst[(gi - 1) % 2], (gi // 2 + (gi % 2)) * 16)
                    e.memset(yac[:, 0:NTT * 1024], 0.0)
                    e.drain()
                    for ei in range(NE):
                        gg = gi * NE + ei
                        for fc in range(nhc):
                            for ts in range(NTS):
                                nh += 1
                                e.wait_ge(he, 2 * nh - 1)
                                e.scalar_tensor_tensor(
                                    psb[:, (nh % 2) * 512:(nh % 2 + 1) * 512],
                                    ps_g[nh % 2][:],
                                    b3c[:, (gg % 2) * nhc + fc:(gg % 2) * nhc + fc + 1],
                                    asb[:, (nh % 2) * 512:(nh % 2 + 1) * 512],
                                    op0=ALU.add, op1=ALU.mult).then_inc(hv, 1)
                        wsel = experts[ei][5]
                        enum = experts[ei][4]
                        for tt in range(NTT):
                            for hf in range(2):
                                ny += 1
                                e.wait_ge(ym, ny)
                                ysl = slice(tt * 1024 + hf * 512, tt * 1024 + (hf + 1) * 512)
                                tile = gi * NTT + tt
                                if wsel == "one":
                                    e.scalar_tensor_tensor(
                                        yac[:, ysl], ps_y[ny % 2][:], 1.0, yac[:, ysl],
                                        op0=ALU.mult, op1=ALU.add).then_inc(yv, 1)
                                else:
                                    e.scalar_tensor_tensor(
                                        yac[:, ysl], ps_y[ny % 2][:],
                                        rw_sb[:, tile * 8 + enum:tile * 8 + enum + 1],
                                        yac[:, ysl], op0=ALU.mult, op1=ALU.add).then_inc(yv, 1)
        else:
            @b.vector
            def _(e):
                ny = 0
                for gi in range(NG):
                    if gi >= 1:
                        e.wait_ge(yst[(gi - 1) % 2], (gi // 2 + (gi % 2)) * 16)
                    e.memset(yac[:, 0:NTT * 1024], 0.0)
                    e.drain()
                    for ei in range(NE):
                        enum = experts[ei][4]
                        for tt in range(NTT):
                            for hf in range(2):
                                ny += 1
                                e.wait_ge(ym, ny)
                                ysl = slice(tt * 1024 + hf * 512, tt * 1024 + (hf + 1) * 512)
                                tile = gi * NTT + tt
                                e.scalar_tensor_tensor(
                                    yac[:, ysl], ps_y[ny % 2][:],
                                    rw_sb[:, tile * 8 + enum:tile * 8 + enum + 1],
                                    yac[:, ysl], op0=ALU.mult, op1=ALU.add).then_inc(yv, 1)

        @b.sync
        def _(e):
            for gi in range(NG):
                e.wait_ge(yv, (gi + 1) * NE * NTT * 2)
                e.dma_start(out=y_d[gi * TG:(gi + 1) * TG, :].rearrange(
                                "(tt p) d -> p tt d", p=128),
                            in_=yac[:, 0:NTT * 1024]).then_inc(yst[gi % 2], 16)
            e.wait_ge(yst[0], ((NG + 1) // 2) * 16)
            if NG > 1:
                e.wait_ge(yst[1], (NG // 2) * 16)


def emit_moe_finalize(ctx, y_d, s_d, x_src, x_dst, st):
    ctx.begin_stage()
    """g = silu(y_d) (token-major) -> PE transpose to feature-major ->
    x_dst = x_src + s_d * g."""
    nc = ctx.nc
    NTILE = T // 128
    NSL = T // 512
    yt = ctx.sbuf(st, "mf_yt", [128, 2 * 1024], BF16)
    gtm = ctx.sbuf(st, "mf_gtm", [128, 2 * 1024], BF16)
    gfm = ctx.sbuf(st, "mf_gfm", [128, NCH * 512], BF16)
    sxr = ctx.sbuf(st, "mf_sxr", [128, NCH * 512], F32)
    xxr = ctx.sbuf(st, "mf_xxr", [128, NCH * 512], F32)
    xov = ctx.sbuf(st, "mf_xov", [128, 2 * NCH * 512], F32)
    identb = ctx.g["ident_bf"]
    ps_t = [ctx.psum(st, f"mf_pst{i}", [128, 128], BF16) for i in range(2)]

    yl = [ctx.sem("mf_yl0"), ctx.sem("mf_yl1")]
    gs = ctx.sem("mf_gs")
    tm = ctx.sem("mf_tm")
    ge = ctx.sem("mf_ge")
    cl = ctx.sem("mf_cl")
    cb = ctx.sem("mf_cb")
    so = [ctx.sem("mf_so0"), ctx.sem("mf_so1")]

    with nc.Block() as b:
        @b.sync
        def _(e):
            for s in range(NSL):
                if s >= 1:
                    e.wait_ge(cb, s * NCH)
                    for c in range(NCH):
                        e.dma_start(out=x_dst[c * 128:(c + 1) * 128,
                                              (s - 1) * 512:s * 512],
                                    in_=xov[:, ((s - 1) % 2) * NCH * 512 + c * 512:
                                            ((s - 1) % 2) * NCH * 512 + (c + 1) * 512]
                                    ).then_inc(so[(s - 1) % 2], 16)
                for tt in range(4 * s, 4 * s + 4):
                    if tt >= 2:
                        e.wait_ge(gs, tt - 1)
                    e.dma_start(out=yt[:, (tt % 2) * 1024:(tt % 2 + 1) * 1024],
                                in_=y_d[tt * 128:(tt + 1) * 128, :]).then_inc(yl[tt % 2], 16)
                for c in range(NCH):
                    e.dma_start(out=sxr[:, c * 512:(c + 1) * 512],
                                in_=s_d[c * 128:(c + 1) * 128, s * 512:(s + 1) * 512]
                                ).then_inc(cl, 16)
                for c in range(NCH):
                    e.dma_start(out=xxr[:, c * 512:(c + 1) * 512],
                                in_=x_src[c * 128:(c + 1) * 128, s * 512:(s + 1) * 512]
                                ).then_inc(cl, 16)
            e.wait_ge(cb, NSL * NCH)
            for c in range(NCH):
                e.dma_start(out=x_dst[c * 128:(c + 1) * 128, (NSL - 1) * 512:NSL * 512],
                            in_=xov[:, ((NSL - 1) % 2) * NCH * 512 + c * 512:
                                    ((NSL - 1) % 2) * NCH * 512 + (c + 1) * 512]
                            ).then_inc(so[(NSL - 1) % 2], 16)
            e.wait_ge(so[0], (NSL // 2) * 16 * NCH)
            e.wait_ge(so[1], (NSL // 2) * 16 * NCH)

        @b.scalar
        def _(e):
            nt = 0
            for tt in range(NTILE):
                e.wait_ge(yl[tt % 2], (tt // 2 + 1) * 16)
                if tt >= 2:
                    e.wait_ge(tm, (tt - 1) * NCH)
                e.activation(gtm[:, (tt % 2) * 1024:(tt % 2 + 1) * 1024],
                             yt[:, (tt % 2) * 1024:(tt % 2 + 1) * 1024],
                             AF.Silu).then_inc(gs, 1)
                for c in range(NCH):
                    nt += 1
                    e.wait_ge(tm, nt)
                    e.activation(gfm[:, c * 512 + (tt % 4) * 128:
                                     c * 512 + (tt % 4) * 128 + 128],
                                 ps_t[nt % 2][:], AF.Copy).then_inc(ge, 1)

        @b.tensor
        def _(e):
            nt = 0
            for tt in range(NTILE):
                e.wait_ge(gs, tt + 1)
                if tt % 4 == 0 and tt >= 4:
                    e.wait_ge(cb, (tt // 4) * NCH)
                for c in range(NCH):
                    nt += 1
                    if nt >= 3:
                        e.wait_ge(ge, nt - 2)
                    e.transpose(ps_t[nt % 2][:],
                                gtm[:, (tt % 2) * 1024 + c * 128:
                                    (tt % 2) * 1024 + (c + 1) * 128],
                                identb[:]).then_inc(tm, 1)

        @b.vector
        def _(e):
            for s in range(NSL):
                e.wait_ge(ge, (s + 1) * 4 * NCH)
                e.wait_ge(cl, (s + 1) * 2 * NCH * 16)
                if s >= 2:
                    e.wait_ge(so[s % 2], (s // 2) * NCH * 16)
                for c in range(NCH):
                    e.tensor_tensor(gfm[:, c * 512:(c + 1) * 512],
                                    gfm[:, c * 512:(c + 1) * 512],
                                    sxr[:, c * 512:(c + 1) * 512], op=ALU.mult)
                    e.drain()
                    e.tensor_tensor(xov[:, (s % 2) * NCH * 512 + c * 512:
                                        (s % 2) * NCH * 512 + (c + 1) * 512],
                                    xxr[:, c * 512:(c + 1) * 512],
                                    gfm[:, c * 512:(c + 1) * 512],
                                    op=ALU.add).then_inc(cb, 1)


def emit_moe_ffn_combine(ctx, xn_d, s_d, x_src, x_dst, kind):
    nc = ctx.nc
    NTILE = T // 128
    y_d = ctx.dtmp(f"y_{kind}_{ctx.uid()}", [T, 1024], BF16)
    with ExitStack() as st:
        rw_sb = ctx.sbuf(st, "mo_rwsb", [128, NTILE * 8], F32)
        with ExitStack() as str_:
            emit_moe_routing(ctx, xn_d, rw_sb, kind, str_)
        with ExitStack() as ste:
            emit_moe_experts(ctx, xn_d, rw_sb, y_d, kind, ste)
        with ExitStack() as stf:
            emit_moe_finalize(ctx, y_d, s_d, x_src, x_dst, stf)


def emit_l2_attn(ctx, xn_d, s_d):
    nc = ctx.nc
    ctx.begin_stage()
    NSL = T // 512
    with ExitStack() as st:
        vsb = ctx.sbuf(st, "l2_v", [128, NCH * T], BF16)
        ckb = ctx.sbuf(st, "l2_ckb", [16, T], BF16)
        skb = ctx.sbuf(st, "l2_skb", [16, T], BF16)
        cqb = ctx.sbuf(st, "l2_cqb", [16, T], BF16)
        sqb = ctx.sbuf(st, "l2_sqb", [16, T], BF16)
        ind = ctx.sbuf(st, "l2_ind", [16, NCH * 128], BF16)
        pss = [ctx.psum(st, f"l2_ps{i}", [128, 512]) for i in range(2)]

        stp2 = ExitStack()
        tq = ctx.sbuf(stp2, "l2_tq", [16, T], F32)
        tk = ctx.sbuf(stp2, "l2_tk", [16, T], F32)
        stp1 = ExitStack()
        wv = ctx.sbuf(stp1, "l2_wv", [128, NCH * 1024], BF16)
        wqk = ctx.sbuf(stp1, "l2_wqk", [128, NCH * 32], BF16)
        xs = ctx.sbuf(stp1, "l2_xs", [128, 2 * NCH * 512], BF16)

        wl = ctx.sem("l2_wl")
        xl = [ctx.sem("l2_xla"), ctx.sem("l2_xlb")]
        mm = ctx.sem("l2_mm")
        ev = ctx.sem("l2_ev")

        with nc.Block() as b:
            @b.gpsimd
            def _(e):
                for c in range(NCH):
                    e.dma_start(out=wv[:, c * 1024:(c + 1) * 1024],
                                in_=ctx.dram["l2_wv"][c * 128:(c + 1) * 128, :]).then_inc(wl, 16)
                    e.dma_start(out=wqk[:, c * 32:c * 32 + 16],
                                in_=ctx.dram["l2_wq"][c * 128:(c + 1) * 128, :]).then_inc(wl, 16)
                    e.dma_start(out=wqk[:, c * 32 + 16:(c + 1) * 32],
                                in_=ctx.dram["l2_wk"][c * 128:(c + 1) * 128, :]).then_inc(wl, 16)
                e.dma_start(out=ind[:], in_=ctx.dram["c_l2ind"][:, :]).then_inc(wl, 16)

            @b.sync
            def _(e):
                for s in range(NSL):
                    if s >= 2:
                        e.wait_ge(mm, (s - 1) * 10)
                    for c in range(NCH):
                        e.dma_start(
                            out=xs[:, ((s % 2) * NCH + c) * 512:((s % 2) * NCH + c + 1) * 512],
                            in_=xn_d[c * 128:(c + 1) * 128, s * 512:(s + 1) * 512]
                        ).then_inc(xl[s % 2], 16)

            @b.tensor
            def _(e):
                e.wait_ge(wl, (3 * NCH + 1) * 16)
                n = 0
                for s in range(NSL):
                    e.wait_ge(xl[s % 2], (s // 2 + 1) * NCH * 16)
                    for m in range(NCH):
                        n += 1
                        if n >= 3:
                            e.wait_ge(ev, n - 2)
                        for c in range(NCH):
                            ii = e.matmul(pss[n % 2][:],
                                          wv[:, c * 1024 + m * 128:c * 1024 + (m + 1) * 128],
                                          xs[:, ((s % 2) * NCH + c) * 512:((s % 2) * NCH + c + 1) * 512],
                                          start=(c == 0), stop=(c == NCH - 1))
                        ii.then_inc(mm, 1)
                    for qk in range(2):
                        n += 1
                        if n >= 3:
                            e.wait_ge(ev, n - 2)
                        for c in range(NCH):
                            ii = e.matmul(pss[n % 2][0:16, :],
                                          wqk[:, c * 32 + qk * 16:c * 32 + qk * 16 + 16],
                                          xs[:, ((s % 2) * NCH + c) * 512:((s % 2) * NCH + c + 1) * 512],
                                          start=(c == 0), stop=(c == NCH - 1))
                        ii.then_inc(mm, 1)

            @b.scalar
            def _(e):
                n = 0
                for s in range(NSL):
                    for m in range(NCH):
                        n += 1
                        e.wait_ge(mm, n)
                        e.activation(vsb[:, m * T + s * 512:m * T + (s + 1) * 512],
                                     pss[n % 2][:], AF.Copy).then_inc(ev, 1)
                    for qk in range(2):
                        n += 1
                        e.wait_ge(mm, n)
                        dst = tq if qk == 0 else tk
                        e.activation(dst[:, s * 512:(s + 1) * 512],
                                     pss[n % 2][0:16, :], AF.Tanh).then_inc(ev, 1)

        stp1.close()

        stA = ExitStack()
        cq = ctx.sbuf(stA, "l2_cq", [16, SEQ], F32)
        sq = ctx.sbuf(stA, "l2_sq", [16, SEQ], F32)
        ck = ctx.sbuf(stA, "l2_ck", [16, SEQ], F32)
        sk = ctx.sbuf(stA, "l2_sk", [16, SEQ], F32)
        den = ctx.sbuf(stA, "l2_den", [16, SEQ], F32)
        t1 = ctx.sbuf(stA, "l2_t1", [16, SEQ], F32)

        PI4 = float(np.pi / 4.0)
        tg = ctx.sem("l2_tg")
        sc = ctx.sem("l2_sc")
        with nc.Block() as b:
            @b.scalar
            def _(e):
                for s in range(NSEQ):
                    sl = slice(s * SEQ, (s + 1) * SEQ)
                    if s >= 1:
                        e.wait_ge(sc, 2 * s)
                    e.activation(cq[:], tq[:, sl], AF.Sin, scale=PI4,
                                 bias=ctx.g["half_pi"][0:16, :])
                    e.activation(sq[:], tq[:, sl], AF.Sin, scale=PI4)
                    e.activation(ck[:], tk[:, sl], AF.Sin, scale=PI4,
                                 bias=ctx.g["half_pi"][0:16, :])
                    e.activation(sk[:], tk[:, sl], AF.Sin, scale=PI4).then_inc(tg, 2)
                    e.wait_ge(sc, 2 * s + 1)
                    e.activation(t1[:], den[:], AF.Ln)
                    e.drain()
                    e.activation(den[:], t1[:], AF.Exp, scale=-1.0).then_inc(tg, 1)

            @b.vector
            def _(e):
                for s in range(NSEQ):
                    sl = slice(s * SEQ, (s + 1) * SEQ)
                    e.wait_ge(tg, 3 * s + 2)
                    e.tensor_tensor_scan(t1[:], ck[:], ck[:], 0.0,
                                         op0=ALU.add, op1=ALU.bypass)
                    e.drain()
                    e.tensor_tensor(den[:], cq[:], t1[:], op=ALU.mult)
                    e.drain()
                    e.tensor_tensor_scan(t1[:], sk[:], sk[:], 0.0,
                                         op0=ALU.add, op1=ALU.bypass)
                    e.drain()
                    e.tensor_tensor(t1[:], sq[:], t1[:], op=ALU.mult)
                    e.drain()
                    e.tensor_tensor(den[:], den[:], t1[:], op=ALU.add)
                    e.drain()
                    e.tensor_scalar(den[:], den[:], 1e-6, None, op0=ALU.add).then_inc(sc, 1)
                    e.wait_ge(tg, 3 * s + 3)
                    e.tensor_tensor(cq[:], cq[:], den[:], op=ALU.mult)
                    e.tensor_tensor(sq[:], sq[:], den[:], op=ALU.mult)
                    e.drain()
                    e.tensor_copy(ckb[:, sl], ck[:])
                    e.tensor_copy(skb[:, sl], sk[:])
                    e.tensor_copy(cqb[:, sl], cq[:])
                    e.tensor_copy(sqb[:, sl], sq[:]).then_inc(sc, 1)

        stA.close()
        stp2.close()

        hd_d = ctx.dtmp(f"l2_hd_{ctx.uid()}", [D, T], BF16)
        hrb = ctx.sbuf(st, "l2_hrb", [128, 2 * 512], BF16)
        bck = ctx.sbuf(st, "l2_bck", [128, 2 * 4 * 512], F32)
        vc = ctx.sbuf(st, "l2_vc", [128, 512], F32)
        vs = ctx.sbuf(st, "l2_vs", [128, 512], F32)
        cvc = ctx.sbuf(st, "l2_cvc", [128, 2 * 512], F32)
        cvs = ctx.sbuf(st, "l2_cvs", [128, 2 * 512], F32)
        th = ctx.sbuf(st, "l2_th", [128, 512], F32)
        ps_b = [ctx.psum(st, f"l2_psb{i}", [128, 512]) for i in range(4)]

        bm = ctx.sem("l2_bm")
        be = ctx.sem("l2_be")
        hv = ctx.sem("l2_hv")
        hst = [ctx.sem("l2_hst0"), ctx.sem("l2_hst1")]

        with nc.Block() as b:
            @b.gpsimd
            def _(e):
                nb = 0
                for c in range(NCH):
                    for s in range(NSEQ):
                        for tt in range(SEQ // 512):
                            t0 = s * SEQ + tt * 512
                            nb += 1
                            e.wait_ge(hv, nb * 5)
                            e.dma_start(out=hd_d[c * 128:(c + 1) * 128, t0:t0 + 512],
                                        in_=hrb[:, (nb % 2) * 512:(nb % 2 + 1) * 512]
                                        ).then_inc(hst[nb % 2], 16)
                e.wait_ge(hst[0], 32 * 16)
                e.wait_ge(hst[1], 32 * 16)

            @b.tensor
            def _(e):
                nb = 0
                for c in range(NCH):
                    for s in range(NSEQ):
                        for tt in range(SEQ // 512):
                            t0 = s * SEQ + tt * 512
                            nb += 1
                            if nb >= 2:
                                e.wait_ge(be, (nb - 1) * 4)
                            for j, row in enumerate((ckb, skb, cqb, sqb)):
                                e.matmul(ps_b[j][:], ind[:, c * 128:(c + 1) * 128],
                                         row[:, t0:t0 + 512],
                                         start=True, stop=True).then_inc(bm, 1)

            @b.scalar
            def _(e):
                nb = 0
                for c in range(NCH):
                    for s in range(NSEQ):
                        for tt in range(SEQ // 512):
                            nb += 1
                            e.wait_ge(bm, nb * 4)
                            if nb >= 3:
                                e.wait_ge(hv, (nb - 2) * 5)
                            par = nb % 2
                            for j in range(4):
                                e.activation(bck[:, par * 2048 + j * 512:par * 2048 + (j + 1) * 512],
                                             ps_b[j][:], AF.Copy).then_inc(be, 1)

            @b.vector
            def _(e):
                nb = 0
                for c in range(NCH):
                    for s in range(NSEQ):
                        for tt in range(SEQ // 512):
                            t0 = s * SEQ + tt * 512
                            nb += 1
                            e.wait_ge(be, nb * 4)
                            e.drain()
                            par = nb % 2
                            bb = par * 2048
                            e.tensor_tensor(vc[:], vsb[:, c * T + t0:c * T + t0 + 512],
                                            bck[:, bb:bb + 512], op=ALU.mult)
                            e.tensor_tensor(vs[:], vsb[:, c * T + t0:c * T + t0 + 512],
                                            bck[:, bb + 512:bb + 1024], op=ALU.mult)
                            e.drain()
                            init_c = 0.0 if tt == 0 else cvc[:, (1 - par) * 512 + 511:(1 - par) * 512 + 512]
                            init_s = 0.0 if tt == 0 else cvs[:, (1 - par) * 512 + 511:(1 - par) * 512 + 512]
                            e.tensor_tensor_scan(cvc[:, par * 512:(par + 1) * 512],
                                                 vc[:], vc[:], init_c,
                                                 op0=ALU.add, op1=ALU.bypass)
                            e.tensor_tensor_scan(cvs[:, par * 512:(par + 1) * 512],
                                                 vs[:], vs[:], init_s,
                                                 op0=ALU.add, op1=ALU.bypass)
                            e.drain()
                            e.tensor_tensor(th[:], bck[:, bb + 1024:bb + 1536],
                                            cvc[:, par * 512:(par + 1) * 512], op=ALU.mult)
                            e.tensor_tensor(vc[:], bck[:, bb + 1536:bb + 2048],
                                            cvs[:, par * 512:(par + 1) * 512], op=ALU.mult)
                            e.drain()
                            if nb >= 3:
                                e.wait_ge(hst[nb % 2], (nb // 2 - 1 + (nb % 2)) * 16)
                            e.tensor_tensor(hrb[:, (nb % 2) * 512:(nb % 2 + 1) * 512],
                                            th[:], vc[:], op=ALU.add).then_inc(hv, 5)

        wo = ctx.sbuf(st, "l2_wo", [128, NCH * 1024], BF16)
        ob = ctx.sbuf(st, "l2_ob", [128, 2 * 512], F32)
        hxs = ctx.sbuf(st, "l2_hxs", [128, 2 * NCH * 512], BF16)
        wl2 = ctx.sem("l2_wl2")
        hxl = [ctx.sem("l2_hxl0"), ctx.sem("l2_hxl1")]
        mm2 = ctx.sem("l2_mm2")
        ev2 = ctx.sem("l2_ev2")
        so2 = [ctx.sem("l2_so2a"), ctx.sem("l2_so2b")]

        with nc.Block() as b:
            @b.gpsimd
            def _(e):
                for c in range(NCH):
                    e.dma_start(out=wo[:, c * 1024:(c + 1) * 1024],
                                in_=ctx.dram["l2_wo"][c * 128:(c + 1) * 128, :]).then_inc(wl2, 16)
                n = 0
                for s in range(NSL):
                    for m in range(NCH):
                        n += 1
                        e.wait_ge(ev2, n)
                        e.dma_start(out=s_d[m * 128:(m + 1) * 128, s * 512:(s + 1) * 512],
                                    in_=ob[:, (n % 2) * 512:(n % 2 + 1) * 512]
                                    ).then_inc(so2[n % 2], 16)
                e.wait_ge(so2[0], (NSL * NCH // 2) * 16)
                e.wait_ge(so2[1], (NSL * NCH // 2) * 16)

            @b.sync
            def _(e):
                for s in range(NSL):
                    if s >= 2:
                        e.wait_ge(mm2, (s - 1) * NCH)
                    for c in range(NCH):
                        e.dma_start(
                            out=hxs[:, ((s % 2) * NCH + c) * 512:((s % 2) * NCH + c + 1) * 512],
                            in_=hd_d[c * 128:(c + 1) * 128, s * 512:(s + 1) * 512]
                        ).then_inc(hxl[s % 2], 16)

            @b.tensor
            def _(e):
                e.wait_ge(wl2, NCH * 16)
                n = 0
                for s in range(NSL):
                    e.wait_ge(hxl[s % 2], (s // 2 + 1) * NCH * 16)
                    for m in range(NCH):
                        n += 1
                        if n >= 3:
                            e.wait_ge(ev2, n - 2)
                        for c in range(NCH):
                            ii = e.matmul(pss[n % 2][:],
                                          wo[:, c * 1024 + m * 128:c * 1024 + (m + 1) * 128],
                                          hxs[:, ((s % 2) * NCH + c) * 512:((s % 2) * NCH + c + 1) * 512],
                                          start=(c == 0), stop=(c == NCH - 1))
                        ii.then_inc(mm2, 1)

            @b.scalar
            def _(e):
                n = 0
                for s in range(NSL):
                    for m in range(NCH):
                        n += 1
                        e.wait_ge(mm2, n)
                        if n >= 3:
                            e.wait_ge(so2[n % 2], (n // 2 - 1 + (n % 2)) * 16)
                        e.activation(ob[:, (n % 2) * 512:(n % 2 + 1) * 512],
                                     pss[n % 2][:], AF.Copy).then_inc(ev2, 1)




def emit_l3_attn(ctx, xn_d, s_d):
    nc = ctx.nc
    H = 16
    LH = 8
    for seq in range(NSEQ):
        t0 = seq * SEQ
        ctx.begin_stage()
        with ExitStack() as st:
            qf = ctx.sbuf(st, "l3_q", [128, NCH * SEQ], BF16)
            kf = ctx.sbuf(st, "l3_k", [128, NCH * SEQ], BF16)
            vt = ctx.sbuf(st, "l3_v", [128, 16 * 1024], BF16)
            gqf = ctx.sbuf(st, "l3_gq", [128, 4 * SEQ], BF16)
            xpb = ctx.sbuf(st, "l3_xp", [128, NCH * 64], BF16)
            gkb = ctx.sbuf(st, "l3_gk", [128, 4 * 64], BF16)
            gvb = ctx.sbuf(st, "l3_gv", [64, 512], BF16)
            glr = ctx.sbuf(st, "l3_glr", [128, 512], BF16)
            grr = ctx.sbuf(st, "l3_grr", [128, 512], BF16)
            stPP = ExitStack()
            pss = [ctx.psum(stPP, f"l3_ps{i}", [128, 512]) for i in range(2)]

            stp = ExitStack()
            wbuf = ctx.sbuf(stp, "l3_wbuf", [128, NCH * 1024], BF16)
            xs = ctx.sbuf(stp, "l3_xs", [128, 2 * NCH * 512], BF16)
            xpr = ctx.sbuf(stp, "l3_xpr", [128, NCH * 64], F32)

            wl = ctx.sem("l3_wl")
            xl = [ctx.sem("l3_xla"), ctx.sem("l3_xlb")]
            mm = ctx.sem("l3_mm")
            ev = ctx.sem("l3_ev")
            pr = ctx.sem("l3_pr")
            pp = ctx.sem("l3_pp")
            cnt = {"wl": 0, "xs": [0, 0], "mm": 0, "ev": 0, "pr": 0, "sl": 0}

            def phase(wname, wcols, mode, dst, pool=False):
                mchunks = wcols // 128
                wl0 = cnt["wl"]
                cnt["wl"] += NCH * 16
                with nc.Block() as b:
                    @b.gpsimd
                    def _(e):
                        for c in range(NCH):
                            e.dma_start(out=wbuf[:, c * wcols:(c + 1) * wcols],
                                        in_=ctx.dram[wname][c * 128:(c + 1) * 128, :]
                                        ).then_inc(wl, 16)

                    @b.sync
                    def _(e):
                        for s in range(4):
                            gsl = cnt["sl"] + s
                            if gsl >= 2:
                                e.wait_ge(mm, cnt["mm0"](gsl - 1))
                                if pool:
                                    e.wait_ge(pr, s - 1)
                            for c in range(NCH):
                                e.dma_start(
                                    out=xs[:, ((gsl % 2) * NCH + c) * 512:((gsl % 2) * NCH + c + 1) * 512],
                                    in_=xn_d[c * 128:(c + 1) * 128, t0 + s * 512:t0 + (s + 1) * 512]
                                ).then_inc(xl[gsl % 2], 16)

                    if pool:
                        @b.vector
                        def _(e):
                            for s in range(4):
                                gsl = cnt["sl"] + s
                                e.wait_ge(xl[gsl % 2], (gsl // 2 + 1) * NCH * 16)
                                for c in range(NCH):
                                    srcv = xs[:, ((gsl % 2) * NCH + c) * 512:((gsl % 2) * NCH + c + 1) * 512]
                                    ii = e.reduce_sum(xpr[:, c * 64 + s * 16:c * 64 + (s + 1) * 16],
                                                      srcv.rearrange("p (l w) -> p l w", w=32),
                                                      axis=AX.X)
                                ii.then_inc(pr, 1)
                            e.drain()
                            for c in range(NCH):
                                e.tensor_scalar(xpb[:, c * 64:(c + 1) * 64],
                                                xpr[:, c * 64:(c + 1) * 64],
                                                1.0 / 32.0, None, op0=ALU.mult)
                            e.drain()
                            e.nop().then_inc(pp, 1)

                    @b.tensor
                    def _(e):
                        e.wait_ge(wl, wl0 + NCH * 16)
                        for s in range(4):
                            gsl = cnt["sl"] + s
                            e.wait_ge(xl[gsl % 2], (gsl // 2 + 1) * NCH * 16)
                            if mode == "fm":
                                for m in range(mchunks):
                                    cnt["mm"] += 1
                                    n = cnt["mm"]
                                    if n >= 3:
                                        e.wait_ge(ev, n - 2)
                                    for c in range(NCH):
                                        ii = e.matmul(pss[n % 2][:],
                                                      wbuf[:, c * wcols + m * 128:c * wcols + (m + 1) * 128],
                                                      xs[:, ((gsl % 2) * NCH + c) * 512:((gsl % 2) * NCH + c + 1) * 512],
                                                      start=(c == 0), stop=(c == NCH - 1))
                                    ii.then_inc(mm, 1)
                            else:
                                for tt in range(4):
                                    for hf in range(2):
                                        cnt["mm"] += 1
                                        n = cnt["mm"]
                                        if n >= 3:
                                            e.wait_ge(ev, n - 2)
                                        for c in range(NCH):
                                            ii = e.matmul(
                                                pss[n % 2][:],
                                                xs[:, ((gsl % 2) * NCH + c) * 512 + tt * 128:
                                                   ((gsl % 2) * NCH + c) * 512 + (tt + 1) * 128],
                                                wbuf[:, c * wcols + hf * 512:c * wcols + (hf + 1) * 512],
                                                start=(c == 0), stop=(c == NCH - 1))
                                        ii.then_inc(mm, 1)

                    @b.scalar
                    def _(e):
                        for s in range(4):
                            if mode == "fm":
                                for m in range(mchunks):
                                    cnt["ev"] += 1
                                    n = cnt["ev"]
                                    e.wait_ge(mm, n)
                                    e.activation(dst[:, m * SEQ + s * 512:m * SEQ + (s + 1) * 512],
                                                 pss[n % 2][:], AF.Copy).then_inc(ev, 1)
                            else:
                                for tt in range(4):
                                    for hf in range(2):
                                        cnt["ev"] += 1
                                        n = cnt["ev"]
                                        e.wait_ge(mm, n)
                                        kt = s * 4 + tt
                                        e.activation(vt[:, kt * 1024 + hf * 512:kt * 1024 + (hf + 1) * 512],
                                                     pss[n % 2][:], AF.Copy).then_inc(ev, 1)
                cnt["sl"] += 4

            per_sl = {"l3_wq": NCH, "l3_wk": NCH, "l3_wv": NCH, "l3_wgq": 4}

            def _mm0(gsl):
                total = 0
                order = [("l3_wq", NCH), ("l3_wk", NCH), ("l3_wv", NCH), ("l3_wgq", 4)]
                g = 0
                for name, per in order:
                    for s in range(4):
                        total += per
                        if g == gsl:
                            return total
                        g += 1
                return total
            cnt["mm0"] = _mm0

            phase("l3_wq", 1024, "fm", qf, pool=True)
            phase("l3_wk", 1024, "fm", kf)
            phase("l3_wv", 1024, "tm", None)
            phase("l3_wgq", 512, "fm", gqf)

            gkv = ctx.sem("l3_gkv")
            with nc.Block() as b:
                @b.gpsimd
                def _(e):
                    for c in range(NCH):
                        e.dma_start(out=wbuf[:, c * 512:(c + 1) * 512],
                                    in_=ctx.dram["l3_wgk"][c * 128:(c + 1) * 128, :]).then_inc(gkv, 16)
                    for c in range(NCH):
                        e.dma_start(out=wbuf[:, NCH * 512 + c * 512:NCH * 512 + (c + 1) * 512],
                                    in_=ctx.dram["l3_wgv"][c * 128:(c + 1) * 128, :]).then_inc(gkv, 16)
                    e.dma_start(out=glr[:], in_=ctx.dram["l3_glrep"][:, :]).then_inc(gkv, 16)
                    e.dma_start(out=grr[:], in_=ctx.dram["l3_grrep"][:, :]).then_inc(gkv, 16)

                @b.tensor
                def _(e):
                    e.wait_ge(gkv, (2 * NCH + 2) * 16)
                    e.wait_ge(pp, 1)
                    for m in range(4):
                        for c in range(NCH):
                            ii = e.matmul(pss[0][0:128, 0:64],
                                          wbuf[:, c * 512 + m * 128:c * 512 + (m + 1) * 128],
                                          xpb[:, c * 64:(c + 1) * 64],
                                          start=(c == 0), stop=(c == NCH - 1))
                        ii.then_inc(mm, 1)
                        e.wait_ge(ev, cnt["ev"] + m + 1)
                    for c in range(NCH):
                        ii = e.matmul(pss[1][0:64, :],
                                      xpb[:, c * 64:(c + 1) * 64],
                                      wbuf[:, NCH * 512 + c * 512:NCH * 512 + (c + 1) * 512],
                                      start=(c == 0), stop=(c == NCH - 1))
                    ii.then_inc(mm, 1)

                @b.scalar
                def _(e):
                    for m in range(4):
                        e.wait_ge(mm, cnt["mm"] + m + 1)
                        e.activation(gkb[:, m * 64:(m + 1) * 64],
                                     pss[0][0:128, 0:64], AF.Copy).then_inc(ev, 1)
                    e.wait_ge(mm, cnt["mm"] + 5)
                    e.activation(gvb[:], pss[1][0:64, :], AF.Copy).then_inc(ev, 1)

            stp.close()
            st2 = st.enter_context(ExitStack())
            att = ctx.sbuf(st2, "l3_att", [128, NCH * SEQ], BF16)
            ra = ctx.sbuf(st2, "l3_ra", [128, 4 * SEQ], BF16)

            stPP.close()
            if DBG_SUB < 1:
                continue
            stFb = ExitStack()
            msk = ctx.sbuf(stFb, "l3_msk", [128, 3 * 1024], F32)
            lmsk = ctx.sbuf(stFb, "l3_lmsk", [64, SEQ], F32)
            pt_sb = ctx.sbuf(stFb, "l3_pt", [128, 4 * 512], BF16)
            rln = ctx.sbuf(stFb, "l3_rln", [64, 2 * 512], F32)
            rin = ctx.sbuf(stFb, "l3_rin", [64, 2 * 512], F32)
            gle = ctx.sbuf(stFb, "l3_gle", [64, 2 * 512], F32)
            onc = ctx.g["ones_col"]
            ps_r = ctx.psum(st, "l3_psr", [64, 512])
            ps_av = ctx.psum(st, "l3_psav", [64, 512])
            stF = ExitStack()
            ps_st = [ctx.psum(stF, f"l3_pst{i}", [128, 1024]) for i in range(2)]
            ps_gl = ctx.psum(stF, "l3_psgl", [64, 512])

            ml = ctx.sem("l3_ml")
            sm = ctx.sem("l3_sm")
            pa = ctx.sem("l3_pa")
            ex = ctx.sem("l3_ex")
            cs = ctx.sem("l3_cs")
            rr = ctx.sem("l3_rr")
            gv2 = ctx.sem("l3_gv2")
            fz = ctx.sem("l3_fz")

            groups = [(qt, h) for qt in range(4) for h in range(H)]

            def kps_of(qt):
                if qt == 0:
                    return [(0, 1), (2, 2)]
                return [(4 * qt - 2, 0), (4 * qt, 1), (4 * qt + 2, 2)]

            with nc.Block() as b:
                @b.sync
                def _(e):
                    for pp_ in range(3):
                        e.dma_start(out=msk[:, pp_ * 1024:(pp_ + 1) * 1024],
                                    in_=ctx.dram["c_mask_win2"][pp_ * 128:(pp_ + 1) * 128, :]
                                    ).then_inc(ml, 16)

                @b.vector
                def _(e):
                    np_ = 0
                    e.wait_ge(ml, 3 * 16)
                    for gi, (qt, h) in enumerate(groups):
                        for k0, mp in kps_of(qt):
                            np_ += 1
                            e.wait_ge(sm, np_)
                            e.tensor_tensor(ps_st[np_ % 2][:], ps_st[np_ % 2][:],
                                            msk[:, mp * 1024:(mp + 1) * 1024],
                                            op=ALU.add).then_inc(pa, 1)
                        e.wait_ge(rr, 3 * (gi + 1))
                        e.tensor_tensor(gle[:, (gi % 2) * 512:(gi % 2 + 1) * 512],
                                        gle[:, (gi % 2) * 512:(gi % 2 + 1) * 512],
                                        rin[:, (gi % 2) * 512:(gi % 2 + 1) * 512],
                                        op=ALU.mult).then_inc(gv2, 1)
                        e.drain()
                        ch, hp = h // 2, (h % 2) * 64
                        e.tensor_tensor(att[hp:hp + 64, ch * SEQ + qt * 512:ch * SEQ + (qt + 1) * 512],
                                        ps_av[:], gle[:, (gi % 2) * 512:(gi % 2 + 1) * 512],
                                        op=ALU.mult).then_inc(fz, 1)

                @b.tensor
                def _(e):
                    np_ = 0
                    for gi, (qt, h) in enumerate(groups):
                        ch, hp = h // 2, (h % 2) * 64
                        kps = kps_of(qt)
                        nkp = len(kps)
                        if gi >= 1:
                            e.wait_ge(rr, 3 * (gi - 1) + 1)
                        e.matmul(ps_gl[:], glr[(h % 2) * 64:(h % 2) * 64 + 64,
                                              (h // 2) * 64:(h // 2) * 64 + 64],
                                 qf[hp:hp + 64, ch * SEQ + qt * 512:ch * SEQ + (qt + 1) * 512],
                                 start=True, stop=True)
                        for ki, (k0, mp) in enumerate(kps):
                            np_ += 1
                            if ki == 1 and gi >= 1:
                                e.wait_ge(fz, gi)
                            if np_ >= 3:
                                e.wait_ge(ex, np_ - 2)
                            for half in range(2):
                                kt = k0 + half
                                ii = e.matmul(ps_st[np_ % 2][:, half * 512:(half + 1) * 512],
                                              kf[hp:hp + 64, ch * SEQ + kt * 128:ch * SEQ + (kt + 1) * 128],
                                              qf[hp:hp + 64, ch * SEQ + qt * 512:ch * SEQ + (qt + 1) * 512],
                                              start=True, stop=True)
                            ii.then_inc(sm, 1)
                            if ki >= 1:
                                e.wait_ge(ex, np_ - 1)
                                pb = ((np_ - 1) % 2) * 1024
                                pk0 = kps[ki - 1][0]
                                for half in range(2):
                                    kt = pk0 + half
                                    e.matmul(ps_r[:], onc[:],
                                             pt_sb[:, pb + half * 512:pb + (half + 1) * 512],
                                             start=(ki == 1 and half == 0), stop=False)
                                    e.matmul(ps_av[:],
                                             vt[:, kt * 1024 + h * 64:kt * 1024 + (h + 1) * 64],
                                             pt_sb[:, pb + half * 512:pb + (half + 1) * 512],
                                             start=(ki == 1 and half == 0),
                                             stop=False).then_inc(cs, 1)
                        e.wait_ge(ex, np_)
                        pb = (np_ % 2) * 1024
                        for half in range(2):
                            kt = kps[-1][0] + half
                            e.matmul(ps_r[:], onc[:],
                                     pt_sb[:, pb + half * 512:pb + (half + 1) * 512],
                                     start=False, stop=(half == 1))
                            e.matmul(ps_av[:],
                                     vt[:, kt * 1024 + h * 64:kt * 1024 + (h + 1) * 64],
                                     pt_sb[:, pb + half * 512:pb + (half + 1) * 512],
                                     start=False, stop=(half == 1)).then_inc(cs, 1)

                @b.scalar
                def _(e):
                    np_ = 0
                    csn = 0
                    for gi, (qt, h) in enumerate(groups):
                        kps = kps_of(qt)
                        for k0, mp in kps:
                            np_ += 1
                            e.wait_ge(pa, np_)
                            if np_ >= 3:
                                e.wait_ge(cs, 2 * (np_ - 2))
                            e.activation(pt_sb[:, (np_ % 2) * 1024:(np_ % 2 + 1) * 1024],
                                         ps_st[np_ % 2][:], AF.Exp, scale=0.125,
                                         bias=ctx.g["neg_smc"][:]).then_inc(ex, 1)
                        csn += 2 * len(kps)
                        e.wait_ge(cs, csn)
                        if gi >= 2:
                            e.wait_ge(fz, gi - 1)
                        e.wait_ge(sm, np_)
                        e.activation(gle[:, (gi % 2) * 512:(gi % 2 + 1) * 512],
                                     ps_gl[:], AF.Sigmoid).then_inc(rr, 1)
                        e.activation(rln[:, (gi % 2) * 512:(gi % 2 + 1) * 512],
                                     ps_r[:], AF.Ln).then_inc(rr, 1)
                        e.drain()
                        e.activation(rin[:, (gi % 2) * 512:(gi % 2 + 1) * 512],
                                     rln[:, (gi % 2) * 512:(gi % 2 + 1) * 512],
                                     AF.Exp, scale=-1.0).then_inc(rr, 1)

            stF.close()
            if DBG_SUB < 2:
                continue
            stL = ExitStack()
            ps_stl = [ctx.psum(stL, f"l3_pstl{i}", [64, 512]) for i in range(2)]

            lml = ctx.sem("l3_lml")
            sml = ctx.sem("l3_sml")
            pal = ctx.sem("l3_pal")
            exl = ctx.sem("l3_exl")
            csl = ctx.sem("l3_csl")
            rrl = ctx.sem("l3_rrl")
            fzl = ctx.sem("l3_fzl")

            lgroups = [(qt, lh) for qt in range(4) for lh in range(LH)]

            with nc.Block() as b:
                @b.sync
                def _(e):
                    e.dma_start(out=lmsk[:], in_=ctx.dram["c_mask_lat"][:, :]).then_inc(lml, 16)

                @b.vector
                def _(e):
                    e.wait_ge(lml, 16)
                    for gi, (qt, lh) in enumerate(lgroups):
                        e.wait_ge(sml, gi + 1)
                        e.tensor_tensor(ps_stl[gi % 2][:], ps_stl[gi % 2][:],
                                        lmsk[:, qt * 512:(qt + 1) * 512],
                                        op=ALU.add).then_inc(pal, 1)
                        e.wait_ge(rrl, 2 * (gi + 1))
                        ch, hp = lh // 2, (lh % 2) * 64
                        e.tensor_tensor(ra[hp:hp + 64, ch * SEQ + qt * 512:ch * SEQ + (qt + 1) * 512],
                                        ps_av[:], rin[:, (gi % 2) * 512:(gi % 2 + 1) * 512],
                                        op=ALU.mult).then_inc(fzl, 1)

                @b.tensor
                def _(e):
                    for gi, (qt, lh) in enumerate(lgroups):
                        if gi >= 2:
                            e.wait_ge(exl, gi - 1)
                        ch, hp = lh // 2, (lh % 2) * 64
                        e.matmul(ps_stl[gi % 2][:],
                                 gkb[hp:hp + 64, ch * 64:(ch + 1) * 64],
                                 gqf[hp:hp + 64, ch * SEQ + qt * 512:ch * SEQ + (qt + 1) * 512],
                                 start=True, stop=True).then_inc(sml, 1)
                        e.wait_ge(exl, gi + 1)
                        if gi >= 1:
                            e.wait_ge(fzl, gi)
                        e.matmul(ps_r[:], onc[0:64, :],
                                 pt_sb[0:64, (gi % 4) * 512:(gi % 4 + 1) * 512],
                                 start=True, stop=True)
                        e.matmul(ps_av[:],
                                 gvb[:, lh * 64:(lh + 1) * 64],
                                 pt_sb[0:64, (gi % 4) * 512:(gi % 4 + 1) * 512],
                                 start=True, stop=True).then_inc(csl, 1)

                @b.scalar
                def _(e):
                    for gi, (qt, lh) in enumerate(lgroups):
                        e.wait_ge(pal, gi + 1)
                        if gi >= 4:
                            e.wait_ge(csl, gi - 3)
                        e.activation(pt_sb[0:64, (gi % 4) * 512:(gi % 4 + 1) * 512],
                                     ps_stl[gi % 2][:], AF.Exp, scale=0.125,
                                     bias=ctx.g["neg_smc"][0:64, :]).then_inc(exl, 1)
                        e.wait_ge(csl, gi + 1)
                        if gi >= 2:
                            e.wait_ge(fzl, gi - 1)
                        e.activation(rln[:, (gi % 2) * 512:(gi % 2 + 1) * 512],
                                     ps_r[:], AF.Ln).then_inc(rrl, 1)
                        e.drain()
                        e.activation(rin[:, (gi % 2) * 512:(gi % 2 + 1) * 512],
                                     rln[:, (gi % 2) * 512:(gi % 2 + 1) * 512],
                                     AF.Exp, scale=-1.0).then_inc(rrl, 1)

            stL.close()
            stFb.close()
            if DBG_SUB < 3:
                continue
            wro = ctx.sbuf(st, "l3_wro", [128, 4 * 1024], BF16)
            grs = ctx.sbuf(st, "l3_grs", [128, 512], F32)
            tmpv = ctx.sbuf(st, "l3_tmpv", [128, 512], F32)
            stR = ExitStack()
            ps_rm = [ctx.psum(stR, f"l3_psrm{i}", [128, 512]) for i in range(2)]
            ps_g2 = ctx.psum(stR, "l3_psg2", [128, 512])

            wl3 = ctx.sem("l3_wl3")
            rm = ctx.sem("l3_rm")
            gm = ctx.sem("l3_gm")
            ge3 = ctx.sem("l3_ge3")
            cv3 = ctx.sem("l3_cv3")

            with nc.Block() as b:
                @b.gpsimd
                def _(e):
                    for c in range(4):
                        e.dma_start(out=wro[:, c * 1024:(c + 1) * 1024],
                                    in_=ctx.dram["l3_r_out_w"][c * 128:(c + 1) * 128, :]
                                    ).then_inc(wl3, 16)

                @b.tensor
                def _(e):
                    e.wait_ge(wl3, 4 * 16)
                    n = 0
                    for sl in range(4):
                        for m in range(NCH):
                            n += 1
                            if n >= 3:
                                e.wait_ge(cv3, n - 2)
                            for c in range(4):
                                e.matmul(ps_rm[n % 2][:],
                                         wro[:, c * 1024 + m * 128:c * 1024 + (m + 1) * 128],
                                         ra[:, c * SEQ + sl * 512:c * SEQ + (sl + 1) * 512],
                                         start=(c == 0), stop=(c == 3))
                            if n >= 2:
                                e.wait_ge(ge3, n - 1)
                            for sub in range(2):
                                h = 2 * m + sub
                                ii = e.matmul(ps_g2[sub * 64:(sub + 1) * 64, :],
                                              grr[sub * 64:(sub + 1) * 64,
                                                  (h // 2) * 64:(h // 2) * 64 + 64],
                                              qf[sub * 64:(sub + 1) * 64,
                                                 m * SEQ + sl * 512:m * SEQ + (sl + 1) * 512],
                                              start=True, stop=True)
                            ii.then_inc(rm, 1)

                @b.scalar
                def _(e):
                    n = 0
                    for sl in range(4):
                        for m in range(NCH):
                            n += 1
                            e.wait_ge(rm, n)
                            if n >= 2:
                                e.wait_ge(cv3, n - 1)
                            e.activation(grs[:], ps_g2[:], AF.Sigmoid).then_inc(ge3, 1)

                @b.vector
                def _(e):
                    n = 0
                    for sl in range(4):
                        for m in range(NCH):
                            n += 1
                            e.wait_ge(ge3, n)
                            e.tensor_tensor(tmpv[:], ps_rm[n % 2][:], grs[:], op=ALU.mult)
                            e.drain()
                            asl = slice(m * SEQ + sl * 512, m * SEQ + (sl + 1) * 512)
                            e.tensor_tensor(att[:, asl], att[:, asl], tmpv[:],
                                            op=ALU.add).then_inc(cv3, 1)
                            e.drain()

            stR.close()
            wo3 = ctx.sbuf(st, "l3_wo3", [128, NCH * 1024], BF16)
            ob3 = ctx.sbuf(st, "l3_ob3", [128, 2 * 512], F32)
            pss = [ctx.psum(st, f"l3_pso{i}", [128, 512]) for i in range(2)]
            wl4 = ctx.sem("l3_wl4")
            mm4 = ctx.sem("l3_mm4")
            ev4 = ctx.sem("l3_ev4")
            so4 = [ctx.sem("l3_so4a"), ctx.sem("l3_so4b")]

            with nc.Block() as b:
                @b.gpsimd
                def _(e):
                    for c in range(NCH):
                        e.dma_start(out=wo3[:, c * 1024:(c + 1) * 1024],
                                    in_=ctx.dram["l3_w_out"][c * 128:(c + 1) * 128, :]
                                    ).then_inc(wl4, 16)

                @b.tensor
                def _(e):
                    e.wait_ge(wl4, NCH * 16)
                    n = 0
                    for s in range(4):
                        for m in range(NCH):
                            n += 1
                            if n >= 3:
                                e.wait_ge(ev4, n - 2)
                            for c in range(NCH):
                                ii = e.matmul(pss[n % 2][:],
                                              wo3[:, c * 1024 + m * 128:c * 1024 + (m + 1) * 128],
                                              att[:, c * SEQ + s * 512:c * SEQ + (s + 1) * 512],
                                              start=(c == 0), stop=(c == NCH - 1))
                            ii.then_inc(mm4, 1)

                @b.scalar
                def _(e):
                    n = 0
                    for s in range(4):
                        for m in range(NCH):
                            n += 1
                            e.wait_ge(mm4, n)
                            if n >= 3:
                                e.wait_ge(so4[n % 2], (n // 2 - 1 + (n % 2)) * 16)
                            e.activation(ob3[:, (n % 2) * 512:(n % 2 + 1) * 512],
                                         pss[n % 2][:], AF.Copy).then_inc(ev4, 1)

                @b.sync
                def _(e):
                    n = 0
                    for s in range(4):
                        for m in range(NCH):
                            n += 1
                            e.wait_ge(ev4, n)
                            e.dma_start(out=s_d[m * 128:(m + 1) * 128,
                                               t0 + s * 512:t0 + (s + 1) * 512],
                                        in_=ob3[:, (n % 2) * 512:(n % 2 + 1) * 512]
                                        ).then_inc(so4[n % 2], 16)

                @b.gpsimd
                def _(e):
                    e.wait_ge(so4[0], (4 * NCH // 2) * 16)
                    e.wait_ge(so4[1], (4 * NCH // 2) * 16)


def emit_l3_ffn_combine(ctx, xn_d, s_d, x_src, x_dst):
    nc = ctx.nc
    ctx.begin_stage()
    NSL = T // 512
    NHC = 32
    with ExitStack() as st:
        xsn = ctx.sbuf(st, "f3_xsn", [128, 2 * NCH * 512], BF16)
        w1t = ctx.sbuf(st, "f3_w1t", [128, 2 * NCH * 128], BF16)
        w2t = ctx.sbuf(st, "f3_w2t", [128, 2 * NHC * 128], BF16)
        hsb = ctx.sbuf(st, "f3_h", [128, NHC * 512], BF16)
        b1c3 = ctx.sbuf(st, "f3_b1c", [128, NHC], F32)
        b2r = ctx.sbuf(st, "f3_b2r", [1, 1024], BF16)
        x2 = ctx.sbuf(st, "f3_x2", [128, NCH * 512], F32)
        xsq = ctx.sbuf(st, "f3_xsq", [128, 2 * 512], F32)
        mrow = ctx.sbuf(st, "f3_mrow", [1, 512], F32)
        qrow = ctx.sbuf(st, "f3_qrow", [1, 512], F32)
        rrow = ctx.sbuf(st, "f3_rrow", [1, 512], F32)
        mu_bc = ctx.sbuf(st, "f3_mubc", [128, 512], F32)
        rs_bc = ctx.sbuf(st, "f3_rsbc", [128, 512], F32)
        gcol = ctx.sbuf(st, "f3_gcol", [128, 8], F32)
        bcol = ctx.sbuf(st, "f3_bcol", [128, 8], F32)
        xhat = ctx.sbuf(st, "f3_xhat", [128, 512], F32)
        gsb = ctx.sbuf(st, "f3_gsb", [128, NCH * 512], BF16)
        sxr = ctx.sbuf(st, "f3_sxr", [128, NCH * 512], F32)
        xxr = ctx.sbuf(st, "f3_xxr", [128, NCH * 512], F32)
        xov = ctx.sbuf(st, "f3_xov", [128, 2 * NCH * 512], F32)
        orow = ctx.g["ones_row"]
        ones128 = ctx.g["ones128_f"]
        onesrow = ctx.g["onesrow_f"]
        ps_h = [ctx.psum(st, f"f3_psh{i}", [128, 512]) for i in range(2)]
        ps_y = [ctx.psum(st, f"f3_psy{i}", [128, 512]) for i in range(2)]
        ps_s1 = ctx.psum(st, "f3_pss1", [128, 512])
        ps_s2 = ctx.psum(st, "f3_pss2", [128, 512])

        wl = ctx.sem("f3_wl")
        w1l = [ctx.sem("f3_w1l0"), ctx.sem("f3_w1l1")]
        w2l = [ctx.sem("f3_w2l0"), ctx.sem("f3_w2l1")]
        xl = [ctx.sem("f3_xl0"), ctx.sem("f3_xl1")]
        cl = ctx.sem("f3_cl")
        hm = ctx.sem("f3_hm")
        he = ctx.sem("f3_he")
        ym = ctx.sem("f3_ym")
        x2v = ctx.sem("f3_x2v")
        sqs = ctx.sem("f3_sqs")
        stm = ctx.sem("f3_stm")
        rowv = ctx.sem("f3_rowv")
        rows = ctx.sem("f3_rows")
        bcm = ctx.sem("f3_bcm")
        bcv = ctx.sem("f3_bcv")
        xhv = ctx.sem("f3_xhv")
        gss = ctx.sem("f3_gss")
        cb = ctx.sem("f3_cb")
        so = [ctx.sem("f3_so0"), ctx.sem("f3_so1")]

        with nc.Block() as b:
            @b.gpsimd
            def _(e):
                e.dma_start(out=b1c3[:], in_=ctx.dram["l3_b1c"][:, :]).then_inc(wl, 16)
                e.dma_start(out=b2r[:], in_=ctx.dram["l3_b2"][:, :]).then_inc(wl, 16)
                e.dma_start(out=gcol[:], in_=ctx.dram["l3_gamma_c"][:, :]).then_inc(wl, 16)
                e.dma_start(out=bcol[:], in_=ctx.dram["l3_beta_c"][:, :]).then_inc(wl, 16)
                for s in range(NSL):
                    for fc in range(NHC):
                        gfc = s * NHC + fc
                        if gfc >= 2:
                            e.wait_ge(hm, gfc - 1)
                        e.dma_start(
                            out=w1t[:, (gfc % 2) * NCH * 128:(gfc % 2 + 1) * NCH * 128],
                            in_=ctx.dram["l3_w1s"][fc]).then_inc(w1l[gfc % 2], 16)
                    for m in range(NCH):
                        gm = s * NCH + m
                        if gm >= 2:
                            e.wait_ge(ym, gm - 1)
                        e.dma_start(
                            out=w2t[:, (gm % 2) * NHC * 128:(gm % 2 + 1) * NHC * 128],
                            in_=ctx.dram["l3_w2s"][m]).then_inc(w2l[gm % 2], 16)

            @b.sync
            def _(e):
                for s in range(NSL):
                    if s >= 1:
                        e.wait_ge(cb, s * NCH)
                        for c in range(NCH):
                            e.dma_start(out=x_dst[c * 128:(c + 1) * 128,
                                                  (s - 1) * 512:s * 512],
                                        in_=xov[:, ((s - 1) % 2) * NCH * 512 + c * 512:
                                                ((s - 1) % 2) * NCH * 512 + (c + 1) * 512]
                                        ).then_inc(so[(s - 1) % 2], 16)
                    if s >= 2:
                        e.wait_ge(x2v, (s - 1) * NCH)
                    for c in range(NCH):
                        e.dma_start(
                            out=xsn[:, ((s % 2) * NCH + c) * 512:((s % 2) * NCH + c + 1) * 512],
                            in_=xn_d[c * 128:(c + 1) * 128, s * 512:(s + 1) * 512]
                        ).then_inc(xl[s % 2], 16)
                    for c in range(NCH):
                        e.dma_start(out=sxr[:, c * 512:(c + 1) * 512],
                                    in_=s_d[c * 128:(c + 1) * 128, s * 512:(s + 1) * 512]
                                    ).then_inc(cl, 16)
                    for c in range(NCH):
                        e.dma_start(out=xxr[:, c * 512:(c + 1) * 512],
                                    in_=x_src[c * 128:(c + 1) * 128, s * 512:(s + 1) * 512]
                                    ).then_inc(cl, 16)
                e.wait_ge(cb, NSL * NCH)
                for c in range(NCH):
                    e.dma_start(out=x_dst[c * 128:(c + 1) * 128, (NSL - 1) * 512:NSL * 512],
                                in_=xov[:, ((NSL - 1) % 2) * NCH * 512 + c * 512:
                                        ((NSL - 1) % 2) * NCH * 512 + (c + 1) * 512]
                                ).then_inc(so[(NSL - 1) % 2], 16)
                e.wait_ge(so[0], (NSL // 2) * NCH * 16)
                e.wait_ge(so[1], (NSL // 2) * NCH * 16)

            @b.tensor
            def _(e):
                e.wait_ge(wl, 4 * 16)
                nh = 0
                for s in range(NSL):
                    e.wait_ge(xl[s % 2], (s // 2 + 1) * NCH * 16)
                    for fc in range(NHC):
                        gfc = s * NHC + fc
                        nh += 1
                        e.wait_ge(w1l[gfc % 2], (gfc // 2 + 1) * 16)
                        if nh >= 3:
                            e.wait_ge(he, nh - 2)
                        for c in range(NCH):
                            ii = e.matmul(ps_h[nh % 2][:],
                                          w1t[:, (gfc % 2) * NCH * 128 + c * 128:
                                              (gfc % 2) * NCH * 128 + (c + 1) * 128],
                                          xsn[:, ((s % 2) * NCH + c) * 512:((s % 2) * NCH + c + 1) * 512],
                                          start=(c == 0), stop=(c == NCH - 1))
                        ii.then_inc(hm, 1)
                    e.wait_ge(he, (s + 1) * NHC)
                    for m in range(NCH):
                        nm = s * NCH + m
                        e.wait_ge(w2l[nm % 2], (nm // 2 + 1) * 16)
                        if nm >= 2:
                            e.wait_ge(x2v, nm - 1)
                        for fc in range(NHC):
                            e.matmul(ps_y[nm % 2][:],
                                     w2t[:, (nm % 2) * NHC * 128 + fc * 128:
                                         (nm % 2) * NHC * 128 + (fc + 1) * 128],
                                     hsb[:, fc * 512:(fc + 1) * 512],
                                     start=(fc == 0), stop=False)
                        e.matmul(ps_y[nm % 2][:], b2r[:, m * 128:(m + 1) * 128],
                                 orow[:], start=False, stop=True).then_inc(ym, 1)
                        e.wait_ge(sqs, nm + 1)
                        e.matmul(ps_s1[0:1, :], ones128[:], x2[:, m * 512:(m + 1) * 512],
                                 start=(m == 0), stop=(m == NCH - 1))
                        e.matmul(ps_s2[0:1, :], ones128[:], xsq[:, (nm % 2) * 512:(nm % 2 + 1) * 512],
                                 start=(m == 0), stop=(m == NCH - 1)).then_inc(stm, 1)
                    e.wait_ge(rows, s + 1)
                    e.matmul(ps_h[0][:], onesrow[:], mrow[:, :], start=True, stop=True)
                    e.matmul(ps_h[1][:], onesrow[:], rrow[:, :],
                             start=True, stop=True).then_inc(bcm, 1)
                    e.wait_ge(bcv, s + 1)

            @b.scalar
            def _(e):
                for s in range(NSL):
                    nh0 = s * NHC
                    for fc in range(NHC):
                        nh = nh0 + fc + 1
                        e.wait_ge(hm, nh)
                        e.activation(hsb[:, fc * 512:(fc + 1) * 512],
                                     ps_h[nh % 2][:], AF.Relu,
                                     bias=b1c3[:, fc:fc + 1]).then_inc(he, 1)
                    for m in range(NCH):
                        nm = s * NCH + m
                        e.wait_ge(x2v, nm + 1)
                        e.activation(xsq[:, (nm % 2) * 512:(nm % 2 + 1) * 512],
                                     x2[:, m * 512:(m + 1) * 512], AF.Square).then_inc(sqs, 1)
                    e.wait_ge(rowv, s * 2 + 2)
                    e.activation(rrow[:], qrow[:], AF.Ln)
                    e.drain()
                    e.activation(rrow[:], rrow[:], AF.Exp, scale=-0.5).then_inc(rows, 1)
                    for c in range(NCH):
                        e.wait_ge(xhv, s * NCH + c + 1)
                        e.activation(gsb[:, c * 512:(c + 1) * 512], xhat[:],
                                     AF.Silu, scale=gcol[:, c:c + 1],
                                     bias=bcol[:, c:c + 1]).then_inc(gss, 1)

            @b.vector
            def _(e):
                for s in range(NSL):
                    for m in range(NCH):
                        nm = s * NCH + m
                        e.wait_ge(ym, nm + 1)
                        e.tensor_tensor(x2[:, m * 512:(m + 1) * 512], ps_y[nm % 2][:],
                                        xsn[:, ((s % 2) * NCH + m) * 512:((s % 2) * NCH + m + 1) * 512],
                                        op=ALU.add).then_inc(x2v, 1)
                    e.wait_ge(stm, (s + 1) * NCH)
                    e.tensor_scalar(mrow[:], ps_s1[0:1, :], 1.0 / D, None, op0=ALU.mult)
                    e.tensor_scalar(qrow[:], ps_s2[0:1, :], 1.0 / D, None, op0=ALU.mult)
                    e.drain()
                    e.tensor_tensor(rrow[:], mrow[:], mrow[:], op=ALU.mult)
                    e.drain()
                    e.tensor_tensor(qrow[:], qrow[:], rrow[:], op=ALU.subtract)
                    e.drain()
                    e.tensor_scalar(qrow[:], qrow[:], 1e-5, None, op0=ALU.add).then_inc(rowv, 2)
                    e.wait_ge(bcm, s + 1)
                    e.tensor_copy(mu_bc[:], ps_h[0][:])
                    e.tensor_copy(rs_bc[:], ps_h[1][:]).then_inc(bcv, 1)
                    e.drain()
                    e.wait_ge(cl, (s + 1) * 2 * NCH * 16)
                    if s >= 2:
                        e.wait_ge(so[s % 2], (s // 2) * NCH * 16)
                    for c in range(NCH):
                        e.tensor_tensor(xhat[:], x2[:, c * 512:(c + 1) * 512],
                                        mu_bc[:], op=ALU.subtract)
                        e.drain()
                        e.tensor_tensor(xhat[:], xhat[:], rs_bc[:],
                                        op=ALU.mult).then_inc(xhv, 1)
                        e.drain()
                        e.wait_ge(gss, s * NCH + c + 1)
                        e.tensor_tensor(xhat[:], gsb[:, c * 512:(c + 1) * 512],
                                        sxr[:, c * 512:(c + 1) * 512], op=ALU.mult)
                        e.drain()
                        e.tensor_tensor(xov[:, (s % 2) * NCH * 512 + c * 512:
                                            (s % 2) * NCH * 512 + (c + 1) * 512],
                                        xxr[:, c * 512:(c + 1) * 512],
                                        xhat[:], op=ALU.add).then_inc(cb, 1)
                        e.drain()



def _run_layers(layers, x_shards, w, c):
    wshapes = {k: list(v.shape) for k, v in w.items()}
    cshapes = {k: list(v.shape) for k, v in c.items()}
    nc = build(layers, wshapes, cshapes)
    names = set()
    for L in layers:
        names.update(LAYER_WEIGHTS[L])
        names.update(LAYER_CONSTS[L])
    shared = {}
    for n in sorted(names):
        shared[n] = w[n] if n in w else c[n]
    in_maps = [dict(shared, x_in=x_shards[i]) for i in range(NCORES)]
    res = run_bass_kernel_spmd(nc, in_maps, core_ids=list(range(NCORES)))
    return [r["x_out"] for r in res.results]


LAYER_GROUPS = [[0, 1, 2, 3]]


def kernel(**inputs):
    w = _prep_weights(inputs)
    c = _prep_consts()
    x_shards = _shard_x(np.asarray(inputs["L0_attn_x"], dtype=np.float32))
    for group in LAYER_GROUPS:
        x_shards = _run_layers(group, x_shards, w, c)
    return _unshard_out(x_shards)


# version 07
# speedup vs baseline: 1.0126x; 1.0126x over previous
import numpy as np
from contextlib import ExitStack

import concourse.bass as bass
import concourse.mybir as mybir
from concourse.bass_utils import run_bass_kernel_spmd

F32 = mybir.dt.float32
BF16 = mybir.dt.bfloat16
AF = mybir.ActivationFunctionType
ALU = mybir.AluOpType
AX = mybir.AxisListType

NCORES = 8
DBG_STAGES = ("ln", "attn", "ffn")
DBG_SUB = 99
DBG_DUMP = False
SEQ = 2048
NSEQ = 2
T = NSEQ * SEQ
D = 1024
NCH = D // 128
NEG = -1e30
SMC = 12.0



def _row(a):
    return np.ascontiguousarray(a.reshape(1, -1))


def _col(a, nch):
    return np.ascontiguousarray(a.reshape(nch, 128).T)


def _w1swz(a, nhc):
    c = a.shape[0] // 128
    return np.ascontiguousarray(
        a.reshape(c, 128, nhc, 128).transpose(2, 1, 0, 3).reshape(nhc, 128, c * 128))


def _prep_weights(inp):
    w = {}
    tr = np.ascontiguousarray
    w["l0_wq"] = tr(np.transpose(inp["L0_attn_wq"], (1, 0, 2)).reshape(D, D))
    w["l0_wk"] = tr(np.transpose(inp["L0_attn_wk"], (1, 0, 2)).reshape(D, D))
    w["l0_wv"] = tr(np.transpose(inp["L0_attn_wv"], (1, 0, 2)).reshape(D, D))
    w["l0_bq"] = _row(inp["L0_attn_bq"])
    w["l0_bk"] = _row(inp["L0_attn_bk"])
    w["l0_bv"] = _row(inp["L0_attn_bv"])
    w["l0_wproj"] = tr(inp["L0_attn_w_proj"])
    w["l0_bproj"] = _row(inp["L0_attn_b_proj"])
    w["l0_gate_w"] = tr(inp["L0_ffn_gate_w"])
    w["l0_gate_b"] = _row(inp["L0_ffn_gate_b"])
    w["l0_w1"] = tr(inp["L0_ffn_w1"])
    w["l0_b1"] = tr(inp["L0_ffn_b1"])
    w["l0_w2"] = tr(inp["L0_ffn_w2"])
    w["l0_b2"] = tr(inp["L0_ffn_b2"])
    w["l1_wqkv"] = tr(inp["L1_attn_w_qkv"])
    w["l1_wproj"] = tr(inp["L1_attn_w_proj"])
    w["l1_bproj"] = _row(inp["L1_attn_b_proj"])
    w["l1_WT"] = tr(inp["L1_ffn_W"].T)
    w["l1_b"] = _row(inp["L1_ffn_b"])
    w["l1_lora_down_T"] = tr(inp["L1_ffn_lora_down"].T)
    w["l1_lora_up_T"] = tr(inp["L1_ffn_lora_up"].T)
    w["l1_down_aux"] = tr(inp["L1_ffn_down_aux"])
    w["l1_up_aux_T"] = tr(inp["L1_ffn_up_aux"].T)
    w["l2_wq"] = tr(np.transpose(inp["L2_attn_W_Q"], (1, 0, 2)).reshape(D, 16))
    w["l2_wk"] = tr(np.transpose(inp["L2_attn_W_K"], (1, 0, 2)).reshape(D, 16))
    w["l2_wv"] = tr(np.transpose(inp["L2_attn_W_V"], (1, 0, 2)).reshape(D, D))
    w["l2_wo"] = tr(np.transpose(inp["L2_attn_W_O"], (2, 1, 0)).reshape(D, D))
    w["l2_gate_wT"] = tr(inp["L2_ffn_gate_w"].T)
    w["l2_gate_b"] = _row(inp["L2_ffn_gate_b"])
    w["l2_w1"] = tr(inp["L2_ffn_w1"])
    w["l2_b1"] = tr(inp["L2_ffn_b1"])
    w["l2_w3"] = tr(inp["L2_ffn_w3"])
    w["l2_b3"] = tr(inp["L2_ffn_b3"])
    w["l2_w2"] = tr(inp["L2_ffn_w2"])
    w["l2_b2"] = tr(inp["L2_ffn_b2"])
    w["l2_sw1"] = tr(inp["L2_ffn_sw1"])
    w["l2_sb1"] = _row(inp["L2_ffn_sb1"])
    w["l2_sw3"] = tr(inp["L2_ffn_sw3"])
    w["l2_sb3"] = _row(inp["L2_ffn_sb3"])
    w["l2_sw2"] = tr(inp["L2_ffn_sw2"])
    w["l2_sb2"] = _row(inp["L2_ffn_sb2"])
    for k in ("wq", "wk", "wv", "wgq", "wgk", "wgv", "r_out_w", "w_out"):
        w["l3_" + k] = tr(inp["L3_attn_" + k])

    def _rep_gate(gw_):
        rep = np.zeros((128, 512), np.float32)
        for h in range(16):
            rep[(h % 2) * 64:(h % 2) * 64 + 64, (h // 2) * 64:(h // 2) * 64 + 64] = \
                np.repeat(gw_[h][:, None], 64, axis=1)
        return rep
    w["l3_glrep"] = _rep_gate(inp["L3_attn_gate_local_w"])
    w["l3_grrep"] = _rep_gate(inp["L3_attn_gate_remote_w"])
    w["l3_w1"] = tr(inp["L3_ffn_w1"][0])
    w["l3_b1"] = _row(inp["L3_ffn_b1"][0])
    w["l3_w2"] = tr(inp["L3_ffn_w2"][0])
    w["l3_b2"] = _row(inp["L3_ffn_b2"][0])
    w["l3_gamma_c"] = tr(inp["L3_ffn_gamma"].reshape(8, 128).T)
    w["l3_beta_c"] = tr(inp["L3_ffn_beta"].reshape(8, 128).T)
    w["l0_bqc"] = _col(w["l0_bq"], 8)
    w["l0_bkc"] = _col(w["l0_bk"], 8)
    w["l0_bprojc"] = _col(w["l0_bproj"], 8)
    w["l1_bprojc"] = _col(w["l1_bproj"], 8)
    w["l1_bc"] = _col(w["l1_b"], 8)
    w["l0_b1c"] = np.stack([_col(inp["L0_ffn_b1"][e], 32) for e in range(8)])
    w["l2_b1c"] = np.stack([_col(inp["L2_ffn_b1"][e], 11) for e in range(8)])
    w["l2_b3c"] = np.stack([_col(inp["L2_ffn_b3"][e], 11) for e in range(8)])
    w["l2_sb1c"] = _col(inp["L2_ffn_sb1"], 11)
    w["l2_sb3c"] = _col(inp["L2_ffn_sb3"], 11)
    w["l3_b1c"] = _col(inp["L3_ffn_b1"][0], 32)
    w["l0_w1s"] = np.stack([_w1swz(inp["L0_ffn_w1"][e], 32) for e in range(8)])
    w["l2_w1s"] = np.stack([_w1swz(inp["L2_ffn_w1"][e], 11) for e in range(8)])
    w["l2_w3s"] = np.stack([_w1swz(inp["L2_ffn_w3"][e], 11) for e in range(8)])
    w["l2_sw1s"] = _w1swz(inp["L2_ffn_sw1"], 11)
    w["l2_sw3s"] = _w1swz(inp["L2_ffn_sw3"], 11)
    w["l3_w1s"] = _w1swz(inp["L3_ffn_w1"][0], 32)
    w["l3_w2s"] = _w1swz(inp["L3_ffn_w2"][0], 8)
    return w


def _prep_consts():
    c = {}
    i128 = np.arange(128)
    m = np.zeros((4, 128, 512), np.float32)
    for dd in range(4):
        for j in range(4):
            blk = m[dd][:, j * 128:(j + 1) * 128]
            d = dd - j
            if d > 0:
                blk[:] = NEG
            elif d == 0:
                blk[i128[:, None] > i128[None, :]] = NEG
    c["c_mask_causal"] = m.reshape(4 * 128, 512)
    c["c_mask_causal2"] = np.concatenate(
        [np.concatenate([m[2 * p], m[2 * p + 1]], axis=1) for p in range(2)], axis=0)
    mw = np.zeros((6, 128, 512), np.float32)
    for idx, rel in enumerate(range(-2, 4)):
        for j in range(4):
            d = rel - j
            blk = mw[idx][:, j * 128:(j + 1) * 128]
            if d == 0:
                blk[i128[:, None] > i128[None, :]] = NEG
            elif d == -1:
                pass
            elif d == -2:
                blk[i128[:, None] <= i128[None, :]] = NEG
            else:
                blk[:] = NEG
    c["c_mask_win"] = mw.reshape(6 * 128, 512)
    c["c_mask_win2"] = np.concatenate(
        [np.concatenate([mw[2 * p], mw[2 * p + 1]], axis=1) for p in range(3)], axis=0)
    lat = np.arange(64)[:, None]
    q = np.arange(2048)[None, :]
    c["c_mask_lat"] = np.where(lat <= q // 32, 0.0, NEG).astype(np.float32)
    PHI = (1.0 + np.sqrt(5.0)) / 2.0
    pos = (np.arange(2048, dtype=np.float32) * np.float32(PHI)) % np.float32(1.0)
    dist = np.abs(pos[:, None] - pos[None, :])
    c["c_phi"] = np.log(np.exp(-dist) + np.float32(1e-8)).astype(np.float32)
    c["c_ident"] = np.eye(128, dtype=np.float32)
    c["c_idx8"] = np.broadcast_to(np.arange(1, 9, dtype=np.float32), (128, 8)).copy()
    ind = np.zeros((16, 8 * 128), np.float32)
    for cc in range(8):
        ind[2 * cc, cc * 128:cc * 128 + 64] = 1.0
        ind[2 * cc + 1, cc * 128 + 64:(cc + 1) * 128] = 1.0
    c["c_l2ind"] = ind
    return c


def _shard_x(x):
    out = []
    for core in range(NCORES):
        xb = x[2 * core:2 * core + 2]
        out.append(np.ascontiguousarray(
            np.transpose(xb, (2, 0, 1)).reshape(D, T)))
    return out


def _unshard_out(shards):
    outs = []
    for s in shards:
        outs.append(np.transpose(np.asarray(s).reshape(D, NSEQ, SEQ), (1, 2, 0)))
    return np.ascontiguousarray(np.concatenate(outs, axis=0)).astype(np.float32)



class Ctx:
    def __init__(self, nc, stack):
        self.nc = nc
        self.stack = stack
        self.dram = {}
        self.g = {}
        self._uid = 0
        self._pool = []
        self._pool_idx = 0

    def uid(self):
        self._uid += 1
        return self._uid

    def sem(self, name):
        if self._pool_idx < len(self._pool):
            h = self._pool[self._pool_idx]
        else:
            h = self.stack.enter_context(self.nc.semaphore(f"sp{len(self._pool)}"))
            self._pool.append(h)
        self._pool_idx += 1
        return h

    def sem_global(self, name):
        return self.stack.enter_context(self.nc.semaphore(f"{name}_{self.uid()}"))

    def begin_stage(self):
        if self._pool:
            with self.nc.Block() as b:
                @b.gpsimd
                def _(e):
                    e.dma_reset()
            with self.nc.Block() as b:
                @b.sync
                def _(e):
                    for h in self._pool:
                        e.sem_clear(h)
        self._pool_idx = 0

    def sbuf(self, stk, name, shape, dt):
        return stk.enter_context(self.nc.sbuf_tensor(f"{name}_{self.uid()}", shape, dt))

    def psum(self, stk, name, shape, dt=F32):
        return stk.enter_context(self.nc.psum_tensor(f"{name}_{self.uid()}", shape, dt))

    def din(self, name, shape, dt=F32):
        h = self.nc.dram_tensor(name, list(shape), dt, kind="ExternalInput").ap()
        self.dram[name] = h
        return h

    def dout(self, name, shape, dt=F32):
        h = self.nc.dram_tensor(name, list(shape), dt, kind="ExternalOutput").ap()
        self.dram[name] = h
        return h

    def dtmp(self, name, shape, dt=F32):
        h = self.nc.dram_tensor(name, list(shape), dt, kind="Internal").ap()
        self.dram[name] = h
        return h


def emit_init(ctx):
    nc = ctx.nc
    g = ctx.g
    g["ones_row"] = ctx.sbuf(ctx.stack, "g_onesrow", [1, 512], BF16)
    g["ones_col"] = ctx.sbuf(ctx.stack, "g_onescol", [128, 64], BF16)
    g["ones128_f"] = ctx.sbuf(ctx.stack, "g_ones128f", [128, 1], F32)
    g["onesrow_f"] = ctx.sbuf(ctx.stack, "g_onesrowf", [1, 128], F32)
    g["ones_col128"] = ctx.sbuf(ctx.stack, "g_onescol128", [128, 128], BF16)
    g["neg_smc"] = ctx.sbuf(ctx.stack, "g_negsmc", [128, 1], F32)
    g["half_pi"] = ctx.sbuf(ctx.stack, "g_halfpi", [128, 1], F32)
    g["ident_bf"] = ctx.sbuf(ctx.stack, "g_identbf", [128, 128], BF16)
    g["ident_f"] = ctx.sbuf(ctx.stack, "g_identf", [128, 128], F32)
    gsem = ctx.sem_global("g_init")
    with nc.Block() as b:
        @b.vector
        def _(e):
            e.memset(g["ones_row"][:], 1.0)
            e.memset(g["ones_col"][:], 1.0)
            e.memset(g["ones128_f"][:], 1.0)
            e.memset(g["onesrow_f"][:], 1.0)
            e.memset(g["ones_col128"][:], 1.0)
            e.memset(g["neg_smc"][:], -SMC)
            e.memset(g["half_pi"][:], float(np.pi / 2))

        @b.gpsimd
        def _(e):
            e.dma_start(out=g["ident_bf"][:], in_=ctx.dram["c_ident"][:, :]).then_inc(gsem, 16)
            e.dma_start(out=g["ident_f"][:], in_=ctx.dram["c_ident"][:, :]).then_inc(gsem, 16)
            e.wait_ge(gsem, 32)



def emit_ln(ctx, x_src, xn_d):
    nc = ctx.nc
    ctx.begin_stage()
    with ExitStack() as st:
        xb = ctx.sbuf(st, "ln_x", [128, 8 * 4096], BF16)
        xsq = ctx.sbuf(st, "ln_xsq", [128, 2 * 512], F32)
        srow = ctx.sbuf(st, "ln_srow", [1, T], F32)
        qrow = ctx.sbuf(st, "ln_qrow", [1, T], F32)
        ones128 = ctx.g["ones128_f"]
        onesrow = ctx.g["onesrow_f"]
        mu_bc = ctx.sbuf(st, "ln_mubc", [128, T], F32)
        rs_bc = ctx.sbuf(st, "ln_rsbc", [128, T], F32)
        tmp = ctx.sbuf(st, "ln_tmp", [128, 4096], F32)
        xnb = ctx.sbuf(st, "ln_xnb", [128, 2 * 4096], BF16)
        ps_s = ctx.psum(st, "ln_ps_s", [128, 512])
        ps_q = ctx.psum(st, "ln_ps_q", [128, 512])

        ld = [ctx.sem(f"ln_ld{i}") for i in range(NCH)]
        sq = ctx.sem("ln_sq")
        mm = ctx.sem("ln_mm")
        ac = ctx.sem("ln_ac")

        with nc.Block() as b:
            @b.gpsimd
            def _(e):
                for c in range(NCH):
                    e.dma_start(out=xb[:, c * 4096:(c + 1) * 4096],
                                in_=x_src[c * 128:(c + 1) * 128, :]).then_inc(ld[c], 16)

            @b.scalar
            def _(e):
                n = 0
                for c in range(NCH):
                    e.wait_ge(ld[c], 16)
                    for tt in range(8):
                        n += 1
                        if n >= 3:
                            e.wait_ge(mm, n - 2)
                        e.activation(xsq[:, (n % 2) * 512:(n % 2 + 1) * 512],
                                     xb[:, c * 4096 + tt * 512:c * 4096 + (tt + 1) * 512],
                                     AF.Square).then_inc(sq, 1)

            @b.tensor
            def _(e):
                n = 0
                for c in range(NCH):
                    for tt in range(8):
                        n += 1
                        e.wait_ge(sq, n)
                        if n >= 2:
                            e.wait_ge(ac, n - 1)
                        e.matmul(ps_s[0:1, :], ctx.g["ones_col"][:, 0:1],
                                 xb[:, c * 4096 + tt * 512:c * 4096 + (tt + 1) * 512],
                                 start=True, stop=True)
                        e.matmul(ps_q[0:1, :], ones128[:],
                                 xsq[:, (n % 2) * 512:(n % 2 + 1) * 512],
                                 start=True, stop=True).then_inc(mm, 1)

            @b.vector
            def _(e):
                n = 0
                for c in range(NCH):
                    for tt in range(8):
                        n += 1
                        e.wait_ge(mm, n)
                        sl = slice(tt * 512, (tt + 1) * 512)
                        if c == 0:
                            e.tensor_copy(srow[:, sl], ps_s[0:1, :])
                            e.tensor_copy(qrow[:, sl], ps_q[0:1, :]).then_inc(ac, 1)
                        else:
                            e.tensor_tensor(srow[:, sl], srow[:, sl], ps_s[0:1, :], op=ALU.add)
                            e.tensor_tensor(qrow[:, sl], qrow[:, sl], ps_q[0:1, :],
                                            op=ALU.add).then_inc(ac, 1)

        with nc.Block() as b:
            @b.vector
            def _(e):
                e.tensor_scalar(srow[:], srow[:], 1.0 / D, None, op0=ALU.mult)
                e.tensor_scalar(qrow[:], qrow[:], 1.0 / D, None, op0=ALU.mult)
                e.drain()
                e.tensor_tensor(tmp[0:1, 0:T], srow[:], srow[:], op=ALU.mult)
                e.drain()
                e.tensor_tensor(qrow[:], qrow[:], tmp[0:1, 0:T], op=ALU.subtract)
                e.drain()
                e.tensor_scalar(qrow[:], qrow[:], 1e-5, None, op0=ALU.add)

        with nc.Block() as b:
            @b.scalar
            def _(e):
                e.activation(tmp[0:1, 0:T], qrow[:], AF.Ln)
                e.drain()
                e.activation(qrow[:], tmp[0:1, 0:T], AF.Exp, scale=-0.5)

        bc = ctx.sem("ln_bc")
        ev = ctx.sem("ln_ev")
        with nc.Block() as b:
            @b.tensor
            def _(e):
                for tt in range(8):
                    if tt >= 1:
                        e.wait_ge(ev, 2 * tt)
                    e.matmul(ps_s[:], onesrow[:], srow[:, tt * 512:(tt + 1) * 512],
                             start=True, stop=True)
                    e.matmul(ps_q[:], onesrow[:], qrow[:, tt * 512:(tt + 1) * 512],
                             start=True, stop=True).then_inc(bc, 1)

            @b.vector
            def _(e):
                for tt in range(8):
                    e.wait_ge(bc, tt + 1)
                    e.tensor_copy(mu_bc[:, tt * 512:(tt + 1) * 512], ps_s[:])
                    e.tensor_copy(rs_bc[:, tt * 512:(tt + 1) * 512], ps_q[:]).then_inc(ev, 2)

        xf = ctx.sbuf(st, "ln_xf", [128, 2 * 4096], F32)
        ld2 = [ctx.sem("ln_ld2a"), ctx.sem("ln_ld2b")]
        nm = ctx.sem("ln_nm")
        st2 = [ctx.sem("ln_st2a"), ctx.sem("ln_st2b")]
        with nc.Block() as b:
            @b.sync
            def _(e):
                for c in range(NCH):
                    if c >= 2:
                        e.wait_ge(nm, c - 1)
                    e.dma_start(out=xf[:, (c % 2) * 4096:(c % 2 + 1) * 4096],
                                in_=x_src[c * 128:(c + 1) * 128, :]).then_inc(ld2[c % 2], 16)

            @b.vector
            def _(e):
                for c in range(NCH):
                    e.wait_ge(ld2[c % 2], (c // 2 + 1) * 16)
                    if c >= 2:
                        e.wait_ge(st2[c % 2], (c // 2) * 16)
                    e.tensor_tensor(tmp[:], xf[:, (c % 2) * 4096:(c % 2 + 1) * 4096],
                                    mu_bc[:], op=ALU.subtract)
                    e.drain()
                    e.tensor_tensor(xnb[:, (c % 2) * 4096:(c % 2 + 1) * 4096],
                                    tmp[:], rs_bc[:], op=ALU.mult).then_inc(nm, 1)
                    e.drain()

            @b.scalar
            def _(e):
                for c in range(NCH):
                    e.wait_ge(nm, c + 1)
                    e.dma_start(out=xn_d[c * 128:(c + 1) * 128, :],
                                in_=xnb[:, (c % 2) * 4096:(c % 2 + 1) * 4096]).then_inc(st2[c % 2], 16)

            @b.gpsimd
            def _(e):
                e.wait_ge(st2[0], (NCH // 2) * 16)
                e.wait_ge(st2[1], (NCH // 2) * 16)



def emit_l1_attn(ctx, xn_d, s_d):
    nc = ctx.nc
    H = 16
    NKT = 16
    for seq in range(NSEQ):
        t0 = seq * SEQ
        ctx.begin_stage()
        with ExitStack() as st:
            qf = ctx.sbuf(st, "l1_q", [128, NCH * SEQ], BF16)
            kf = ctx.sbuf(st, "l1_k", [128, NCH * SEQ], BF16)
            vt = ctx.sbuf(st, "l1_v", [128, NKT * 1024], BF16)
            ao = ctx.sbuf(st, "l1_ao", [128, NCH * SEQ], BF16)

            stp = ExitStack()
            wqk = ctx.sbuf(stp, "l1_wqk", [128, NCH * 2048], BF16)
            wv = ctx.sbuf(stp, "l1_wv", [128, NCH * 1024], BF16)
            xs = ctx.sbuf(stp, "l1_xs", [128, 2 * NCH * 512], BF16)
            pss = [ctx.psum(st, f"l1_ps{i}", [128, 512]) for i in range(2)]

            wl = ctx.sem("l1_wl")
            xl = [ctx.sem("l1_xla"), ctx.sem("l1_xlb")]
            mm = ctx.sem("l1_mm")
            ev = ctx.sem("l1_ev")

            with nc.Block() as b:
                @b.gpsimd
                def _(e):
                    for c in range(NCH):
                        e.dma_start(out=wqk[:, c * 2048:(c + 1) * 2048],
                                    in_=ctx.dram["l1_wqkv"][c * 128:(c + 1) * 128, 0:2048]
                                    ).then_inc(wl, 16)
                        e.dma_start(out=wv[:, c * 1024:(c + 1) * 1024],
                                    in_=ctx.dram["l1_wqkv"][c * 128:(c + 1) * 128, 2048:3072]
                                    ).then_inc(wl, 16)

                @b.sync
                def _(e):
                    for s in range(4):
                        if s >= 2:
                            e.wait_ge(mm, (s - 1) * 24)
                        for c in range(NCH):
                            e.dma_start(
                                out=xs[:, ((s % 2) * NCH + c) * 512:((s % 2) * NCH + c + 1) * 512],
                                in_=xn_d[c * 128:(c + 1) * 128, t0 + s * 512:t0 + (s + 1) * 512]
                            ).then_inc(xl[s % 2], 16)

                @b.tensor
                def _(e):
                    e.wait_ge(wl, NCH * 2 * 16)
                    n = 0
                    for s in range(4):
                        e.wait_ge(xl[s % 2], (s // 2 + 1) * NCH * 16)
                        for m in range(16):
                            n += 1
                            if n >= 3:
                                e.wait_ge(ev, n - 2)
                            for c in range(NCH):
                                ii = e.matmul(pss[n % 2][:],
                                              wqk[:, c * 2048 + m * 128:c * 2048 + (m + 1) * 128],
                                              xs[:, ((s % 2) * NCH + c) * 512:((s % 2) * NCH + c + 1) * 512],
                                              start=(c == 0), stop=(c == NCH - 1))
                            ii.then_inc(mm, 1)
                        for tt in range(4):
                            for hf in range(2):
                                n += 1
                                if n >= 3:
                                    e.wait_ge(ev, n - 2)
                                for c in range(NCH):
                                    ii = e.matmul(
                                        pss[n % 2][:],
                                        xs[:, ((s % 2) * NCH + c) * 512 + tt * 128:
                                           ((s % 2) * NCH + c) * 512 + (tt + 1) * 128],
                                        wv[:, c * 1024 + hf * 512:c * 1024 + (hf + 1) * 512],
                                        start=(c == 0), stop=(c == NCH - 1))
                                ii.then_inc(mm, 1)

                @b.scalar
                def _(e):
                    n = 0
                    for s in range(4):
                        for m in range(16):
                            n += 1
                            e.wait_ge(mm, n)
                            dst = qf if m < 8 else kf
                            mc = m % 8
                            e.activation(dst[:, mc * SEQ + s * 512:mc * SEQ + (s + 1) * 512],
                                         pss[n % 2][:], AF.Copy).then_inc(ev, 1)
                        for tt in range(4):
                            for hf in range(2):
                                n += 1
                                e.wait_ge(mm, n)
                                kt = s * 4 + tt
                                e.activation(vt[:, kt * 1024 + hf * 512:kt * 1024 + (hf + 1) * 512],
                                             pss[n % 2][:], AF.Copy).then_inc(ev, 1)

            stp.close()
            if DBG_SUB < 1:
                continue
            phi_sb = ctx.sbuf(st, "l1_phi", [128, NKT * 512], F32)
            pt_sb = ctx.sbuf(st, "l1_pt", [128, 4 * 512], BF16)
            rln = ctx.sbuf(st, "l1_rln", [64, 2 * 512], F32)
            rin = ctx.sbuf(st, "l1_rin", [64, 2 * 512], F32)
            onc = ctx.g["ones_col"]
            ps_st = [ctx.psum(st, f"l1_pst{i}", [128, 1024]) for i in range(2)]
            ps_r = ctx.psum(st, "l1_psr", [64, 512])
            ps_av = ctx.psum(st, "l1_psav", [64, 512])

            pl = ctx.sem("l1_pl")
            sm = ctx.sem("l1_sm")
            pa = ctx.sem("l1_pa")
            ex = ctx.sem("l1_ex")
            cs = ctx.sem("l1_cs")
            rr = ctx.sem("l1_rr")
            fz = ctx.sem("l1_fz")

            groups = [(qt, h) for qt in range(4) for h in range(H)]
            NKP = NKT // 2

            with nc.Block() as b:
                @b.sync
                def _(e):
                    for qt in range(4):
                        if qt >= 1:
                            e.wait_ge(fz, qt * H)
                        for kt in range(NKT):
                            e.dma_start(
                                out=phi_sb[:, kt * 512:(kt + 1) * 512],
                                in_=ctx.dram["c_phi"][kt * 128:(kt + 1) * 128,
                                                      qt * 512:(qt + 1) * 512]).then_inc(pl, 16)

                @b.vector
                def _(e):
                    np_ = 0
                    for gi, (qt, h) in enumerate(groups):
                        for kp in range(NKP):
                            np_ += 1
                            e.wait_ge(sm, np_)
                            e.scalar_tensor_tensor(ps_st[np_ % 2][:], ps_st[np_ % 2][:],
                                                   0.125,
                                                   phi_sb[:, kp * 1024:(kp + 1) * 1024],
                                                   op0=ALU.mult, op1=ALU.add).then_inc(pa, 1)
                        e.wait_ge(rr, 2 * (gi + 1))
                        ch, hp = h // 2, (h % 2) * 64
                        e.tensor_tensor(ao[hp:hp + 64, ch * SEQ + qt * 512:ch * SEQ + (qt + 1) * 512],
                                        ps_av[:], rin[:, (gi % 2) * 512:(gi % 2 + 1) * 512],
                                        op=ALU.mult).then_inc(fz, 1)

                @b.tensor
                def _(e):
                    np_ = 0
                    for gi, (qt, h) in enumerate(groups):
                        e.wait_ge(pl, (qt + 1) * NKT * 16)
                        ch, hp = h // 2, (h % 2) * 64
                        for kp in range(NKP):
                            np_ += 1
                            if kp == 1 and gi >= 1:
                                e.wait_ge(fz, gi)
                            if np_ >= 3:
                                e.wait_ge(ex, np_ - 2)
                            for half in range(2):
                                kt = 2 * kp + half
                                ii = e.matmul(ps_st[np_ % 2][:, half * 512:(half + 1) * 512],
                                              kf[hp:hp + 64, ch * SEQ + kt * 128:ch * SEQ + (kt + 1) * 128],
                                              qf[hp:hp + 64, ch * SEQ + qt * 512:ch * SEQ + (qt + 1) * 512],
                                              start=True, stop=True)
                            ii.then_inc(sm, 1)
                            if kp >= 1:
                                e.wait_ge(ex, np_ - 1)
                                pb = ((np_ - 1) % 2) * 1024
                                for half in range(2):
                                    kt = 2 * (kp - 1) + half
                                    e.matmul(ps_r[:], onc[:],
                                             pt_sb[:, pb + half * 512:pb + (half + 1) * 512],
                                             start=(kp == 1 and half == 0), stop=False)
                                    e.matmul(ps_av[:],
                                             vt[:, kt * 1024 + h * 64:kt * 1024 + (h + 1) * 64],
                                             pt_sb[:, pb + half * 512:pb + (half + 1) * 512],
                                             start=(kp == 1 and half == 0),
                                             stop=False).then_inc(cs, 1)
                        e.wait_ge(ex, np_)
                        pb = (np_ % 2) * 1024
                        for half in range(2):
                            kt = 2 * (NKP - 1) + half
                            e.matmul(ps_r[:], onc[:],
                                     pt_sb[:, pb + half * 512:pb + (half + 1) * 512],
                                     start=False, stop=(half == 1))
                            e.matmul(ps_av[:],
                                     vt[:, kt * 1024 + h * 64:kt * 1024 + (h + 1) * 64],
                                     pt_sb[:, pb + half * 512:pb + (half + 1) * 512],
                                     start=False, stop=(half == 1)).then_inc(cs, 1)

                @b.scalar
                def _(e):
                    np_ = 0
                    for gi, (qt, h) in enumerate(groups):
                        for kp in range(NKP):
                            np_ += 1
                            e.wait_ge(pa, np_)
                            if np_ >= 3:
                                e.wait_ge(cs, 2 * (np_ - 2))
                            e.activation(pt_sb[:, (np_ % 2) * 1024:(np_ % 2 + 1) * 1024],
                                         ps_st[np_ % 2][:], AF.Exp,
                                         bias=ctx.g["neg_smc"][:]).then_inc(ex, 1)
                        e.wait_ge(cs, (gi + 1) * NKT)
                        if gi >= 2:
                            e.wait_ge(fz, gi - 1)
                        e.activation(rln[:, (gi % 2) * 512:(gi % 2 + 1) * 512],
                                     ps_r[:], AF.Ln).then_inc(rr, 1)
                        e.drain()
                        e.activation(rin[:, (gi % 2) * 512:(gi % 2 + 1) * 512],
                                     rln[:, (gi % 2) * 512:(gi % 2 + 1) * 512],
                                     AF.Exp, scale=-1.0).then_inc(rr, 1)

            if DBG_SUB < 2:
                continue
            wp = ctx.sbuf(st, "l1_wp", [128, NCH * 1024], BF16)
            bpc = ctx.sbuf(st, "l1_bpc", [128, 8], F32)
            orow = ctx.g["ones_row"]
            ob = ctx.sbuf(st, "l1_ob", [128, 2 * 512], F32)

            wl2 = ctx.sem("l1_wl2")
            mm2 = ctx.sem("l1_mm2")
            ev2 = ctx.sem("l1_ev2")
            so2 = [ctx.sem("l1_so2a"), ctx.sem("l1_so2b")]

            with nc.Block() as b:
                @b.gpsimd
                def _(e):
                    for c in range(NCH):
                        e.dma_start(out=wp[:, c * 1024:(c + 1) * 1024],
                                    in_=ctx.dram["l1_wproj"][c * 128:(c + 1) * 128, :]
                                    ).then_inc(wl2, 16)
                    e.dma_start(out=bpc[:], in_=ctx.dram["l1_bprojc"][:, :]).then_inc(wl2, 16)

                @b.tensor
                def _(e):
                    e.wait_ge(wl2, (NCH + 1) * 16)
                    n = 0
                    for s in range(4):
                        for m in range(NCH):
                            n += 1
                            if n >= 3:
                                e.wait_ge(ev2, n - 2)
                            for c in range(NCH):
                                ii = e.matmul(pss[n % 2][:],
                                              wp[:, c * 1024 + m * 128:c * 1024 + (m + 1) * 128],
                                              ao[:, c * SEQ + s * 512:c * SEQ + (s + 1) * 512],
                                              start=(c == 0), stop=(c == NCH - 1))
                            ii.then_inc(mm2, 1)

                @b.scalar
                def _(e):
                    n = 0
                    for s in range(4):
                        for m in range(NCH):
                            n += 1
                            e.wait_ge(mm2, n)
                            if n >= 3:
                                e.wait_ge(so2[n % 2], (n // 2 - 1 + (n % 2)) * 16)
                            e.activation(ob[:, (n % 2) * 512:(n % 2 + 1) * 512],
                                         pss[n % 2][:], AF.Identity,
                                         bias=bpc[:, m:m + 1]).then_inc(ev2, 1)

                @b.sync
                def _(e):
                    n = 0
                    for s in range(4):
                        for m in range(NCH):
                            n += 1
                            e.wait_ge(ev2, n)
                            e.dma_start(out=s_d[m * 128:(m + 1) * 128,
                                               t0 + s * 512:t0 + (s + 1) * 512],
                                        in_=ob[:, (n % 2) * 512:(n % 2 + 1) * 512]
                                        ).then_inc(so2[n % 2], 16)

                @b.gpsimd
                def _(e):
                    e.wait_ge(so2[0], (4 * NCH // 2) * 16)
                    e.wait_ge(so2[1], (4 * NCH // 2) * 16)



def emit_l1_ffn_combine(ctx, xn_d, s_d, x_src, x_dst):
    nc = ctx.nc
    ctx.begin_stage()
    with ExitStack() as st:
        weff = ctx.sbuf(st, "f1_weff", [128, NCH * 1024], BF16)
        t1 = ctx.sbuf(st, "f1_t1", [4, 1024], BF16)
        t2 = ctx.sbuf(st, "f1_t2", [50, 1024], BF16)
        ldn = ctx.sbuf(st, "f1_ldn", [100, 4], BF16)
        dax = ctx.sbuf(st, "f1_dax", [100, 1024], BF16)
        lup = ctx.sbuf(st, "f1_lup", [4, 50], BF16)
        uax = ctx.sbuf(st, "f1_uax", [50, 1024], BF16)
        wtc = ctx.sbuf(st, "f1_wtc", [128, 2 * 1024], BF16)
        bc1 = ctx.sbuf(st, "f1_bc", [128, 8], F32)
        orow = ctx.g["ones_row"]
        pc = [ctx.psum(st, f"f1_pc{i}", [128, 512]) for i in range(2)]

        wl = ctx.sem("f1_wl")
        wts = [ctx.sem("f1_wts0"), ctx.sem("f1_wts1")]
        wlb = ctx.sem("f1_wlb")
        m1 = ctx.sem("f1_m1")
        v1 = ctx.sem("f1_v1")

        with nc.Block() as b:
            @b.gpsimd
            def _(e):
                e.dma_start(out=ldn[:], in_=ctx.dram["l1_lora_down_T"][:, :]).then_inc(wl, 16)
                e.dma_start(out=dax[:], in_=ctx.dram["l1_down_aux"][:, :]).then_inc(wl, 16)
                e.dma_start(out=lup[:], in_=ctx.dram["l1_lora_up_T"][:, :]).then_inc(wl, 16)
                e.dma_start(out=uax[:], in_=ctx.dram["l1_up_aux_T"][:, :]).then_inc(wl, 16)
                for c in range(NCH):
                    if c >= 2:
                        e.wait_ge(v1, 3 + (c - 2))
                    e.dma_start(out=wtc[:, (c % 2) * 1024:(c % 2 + 1) * 1024],
                                in_=ctx.dram["l1_WT"][c * 128:(c + 1) * 128, :]
                                ).then_inc(wts[c % 2], 16)
                e.dma_start(out=bc1[:], in_=ctx.dram["l1_bc"][:, :]).then_inc(wlb, 16)

            @b.tensor
            def _(e):
                e.wait_ge(wl, 4 * 16)
                for hf in range(2):
                    e.matmul(pc[hf][0:4, :], ldn[:], dax[:, hf * 512:(hf + 1) * 512],
                             start=True, stop=True).then_inc(m1, 1)
                e.wait_ge(v1, 1)
                for hf in range(2):
                    e.matmul(pc[hf][0:50, :], lup[:], t1[:, hf * 512:(hf + 1) * 512],
                             start=True, stop=True).then_inc(m1, 1)
                e.wait_ge(v1, 2)
                for c in range(NCH):
                    for hf in range(2):
                        ii = e.matmul(pc[hf][:], t2[:, c * 128:(c + 1) * 128],
                                      uax[:, hf * 512:(hf + 1) * 512],
                                      start=True, stop=True)
                    ii.then_inc(m1, 1)
                    e.wait_ge(v1, 3 + c)

            @b.vector
            def _(e):
                e.wait_ge(m1, 2)
                e.tensor_copy(t1[:, 0:512], pc[0][0:4, :])
                e.tensor_copy(t1[:, 512:1024], pc[1][0:4, :]).then_inc(v1, 1)
                e.wait_ge(m1, 4)
                e.tensor_copy(t2[:, 0:512], pc[0][0:50, :])
                e.tensor_copy(t2[:, 512:1024], pc[1][0:50, :]).then_inc(v1, 1)
                for c in range(NCH):
                    e.wait_ge(m1, 5 + c)
                    e.wait_ge(wts[c % 2], (c // 2 + 1) * 16)
                    e.tensor_tensor(weff[:, c * 1024:c * 1024 + 512],
                                    pc[0][:], wtc[:, (c % 2) * 1024:(c % 2) * 1024 + 512],
                                    op=ALU.add)
                    e.tensor_tensor(weff[:, c * 1024 + 512:(c + 1) * 1024],
                                    pc[1][:], wtc[:, (c % 2) * 1024 + 512:(c % 2 + 1) * 1024],
                                    op=ALU.add).then_inc(v1, 1)

        xs = ctx.sbuf(st, "f1_xs", [128, 2 * NCH * 512], BF16)
        sx = ctx.sbuf(st, "f1_sx", [128, 2 * NCH * 512], F32)
        xr = ctx.sbuf(st, "f1_xr", [128, 2 * NCH * 512], F32)
        gb = ctx.sbuf(st, "f1_gb", [128, 2 * 512], F32)
        xo = ctx.sbuf(st, "f1_xo", [128, 2 * NCH * 512], F32)

        xl = [ctx.sem("f1_xl0"), ctx.sem("f1_xl1")]
        mm = ctx.sem("f1_mm")
        ev = ctx.sem("f1_ev")
        cb = ctx.sem("f1_cb")
        so = [ctx.sem("f1_so0"), ctx.sem("f1_so1")]

        NSL = T // 512
        with nc.Block() as b:
            @b.sync
            def _(e):
                for s in range(NSL):
                    if s >= 2:
                        e.wait_ge(cb, (s - 1) * NCH)
                    r = s % 2
                    for c in range(NCH):
                        e.dma_start(out=xs[:, (r * NCH + c) * 512:(r * NCH + c + 1) * 512],
                                    in_=xn_d[c * 128:(c + 1) * 128, s * 512:(s + 1) * 512]
                                    ).then_inc(xl[r], 16)
                    for c in range(NCH):
                        e.dma_start(out=sx[:, (r * NCH + c) * 512:(r * NCH + c + 1) * 512],
                                    in_=s_d[c * 128:(c + 1) * 128, s * 512:(s + 1) * 512]
                                    ).then_inc(xl[r], 16)
                    for c in range(NCH):
                        e.dma_start(out=xr[:, (r * NCH + c) * 512:(r * NCH + c + 1) * 512],
                                    in_=x_src[c * 128:(c + 1) * 128, s * 512:(s + 1) * 512]
                                    ).then_inc(xl[r], 16)

            @b.tensor
            def _(e):
                e.wait_ge(wlb, 16)
                n = 0
                for s in range(NSL):
                    e.wait_ge(xl[s % 2], (s // 2 + 1) * 3 * NCH * 16)
                    r = s % 2
                    for m in range(NCH):
                        n += 1
                        if n >= 3:
                            e.wait_ge(ev, n - 2)
                        for c in range(NCH):
                            ii = e.matmul(pc[n % 2][:],
                                          weff[:, c * 1024 + m * 128:c * 1024 + (m + 1) * 128],
                                          xs[:, (r * NCH + c) * 512:(r * NCH + c + 1) * 512],
                                          start=(c == 0), stop=(c == NCH - 1))
                        ii.then_inc(mm, 1)

            @b.scalar
            def _(e):
                n = 0
                for s in range(NSL):
                    for m in range(NCH):
                        n += 1
                        e.wait_ge(mm, n)
                        if n >= 3:
                            e.wait_ge(cb, n - 2)
                        e.activation(gb[:, (n % 2) * 512:(n % 2 + 1) * 512],
                                     pc[n % 2][:], AF.Silu,
                                     bias=bc1[:, m:m + 1]).then_inc(ev, 1)

            @b.vector
            def _(e):
                n = 0
                for s in range(NSL):
                    r = s % 2
                    if s >= 2:
                        e.wait_ge(so[r], (s // 2) * NCH * 16)
                    for m in range(NCH):
                        n += 1
                        e.wait_ge(ev, n)
                        sl = slice((r * NCH + m) * 512, (r * NCH + m + 1) * 512)
                        gsl = slice((n % 2) * 512, (n % 2 + 1) * 512)
                        e.tensor_tensor(gb[:, gsl], gb[:, gsl], sx[:, sl], op=ALU.mult)
                        e.drain()
                        e.tensor_tensor(xo[:, sl], xr[:, sl], gb[:, gsl],
                                        op=ALU.add).then_inc(cb, 1)

            @b.gpsimd
            def _(e):
                n = 0
                for s in range(NSL):
                    r = s % 2
                    for m in range(NCH):
                        n += 1
                        e.wait_ge(cb, n)
                        e.dma_start(out=x_dst[m * 128:(m + 1) * 128, s * 512:(s + 1) * 512],
                                    in_=xo[:, (r * NCH + m) * 512:(r * NCH + m + 1) * 512]
                                    ).then_inc(so[r], 16)
                e.wait_ge(so[0], (NSL // 2) * NCH * 16)
                e.wait_ge(so[1], (NSL // 2) * NCH * 16)



LAYER_WEIGHTS = {
    0: ["l0_wq", "l0_wk", "l0_wv", "l0_bqc", "l0_bkc", "l0_bv", "l0_wproj", "l0_bprojc",
        "l0_gate_w", "l0_gate_b", "l0_w1s", "l0_b1c", "l0_w2", "l0_b2"],
    1: ["l1_wqkv", "l1_wproj", "l1_bprojc", "l1_WT", "l1_bc", "l1_lora_down_T",
        "l1_lora_up_T", "l1_down_aux", "l1_up_aux_T"],
    2: ["l2_wq", "l2_wk", "l2_wv", "l2_wo", "l2_gate_wT", "l2_gate_b",
        "l2_w1s", "l2_b1c", "l2_w3s", "l2_b3c", "l2_w2", "l2_b2",
        "l2_sw1s", "l2_sb1c", "l2_sw3s", "l2_sb3c", "l2_sw2", "l2_sb2"],
    3: ["l3_wq", "l3_wk", "l3_wv", "l3_wgq", "l3_wgk", "l3_wgv", "l3_r_out_w",
        "l3_w_out", "l3_glrep", "l3_grrep", "l3_w1s", "l3_b1c",
        "l3_w2s", "l3_b2", "l3_gamma_c", "l3_beta_c"],
}
LAYER_CONSTS = {
    0: ["c_mask_causal2", "c_idx8", "c_ident"],
    1: ["c_phi", "c_ident"],
    2: ["c_idx8", "c_ident", "c_l2ind"],
    3: ["c_mask_win2", "c_mask_lat", "c_ident"],
}


def build(layers, weights_shapes, consts_shapes):
    nc = bass.Bass("TRN2", target_bir_lowering=False)
    with ExitStack() as stack:
        ctx = Ctx(nc, stack)
        x_in = ctx.din("x_in", [D, T])
        x_out = ctx.dout("x_out", [D, T])
        xd = ctx.dtmp("xd", [D, T])
        xn_d = ctx.dtmp("xn_d", [D, T], BF16)
        s_d = ctx.dtmp("s_d", [D, T])
        names = set()
        for L in layers:
            names.update(LAYER_WEIGHTS[L])
            names.update(LAYER_CONSTS[L])
        for n in sorted(names):
            shp = weights_shapes[n] if n in weights_shapes else consts_shapes[n]
            ctx.din(n, shp)

        emit_init(ctx)
        for i, L in enumerate(layers):
            src = x_in if i == 0 else xd
            dst = x_out if i == len(layers) - 1 else xd
            if "ln" in DBG_STAGES:
                emit_ln(ctx, src, xn_d)
            if "attn" in DBG_STAGES:
                (emit_l0_attn, emit_l1_attn, emit_l2_attn, emit_l3_attn)[L](ctx, xn_d, s_d)
            if "ffn" in DBG_STAGES:
                (emit_l0_ffn_combine, emit_l1_ffn_combine, emit_l2_ffn_combine,
                 emit_l3_ffn_combine)[L](ctx, xn_d, s_d, src, dst)
        if DBG_DUMP:
            dxn = ctx.dout("dbg_xn", [D, T], BF16)
            fin = ctx.sem("dbg_fin")
            with nc.Block() as b:
                @b.sync
                def _(e):
                    e.dma_start(out=dxn[:, :], in_=xn_d[:, :]).then_inc(fin, 16)
                    e.wait_ge(fin, 16)
            if "attn" in DBG_STAGES and DBG_SUB >= (3 if 3 in layers else 2):
                dss = ctx.dout("dbg_s", [D, T])
                fin2s = ctx.sem("dbg_fin2s")
                with nc.Block() as b:
                    @b.sync
                    def _(e):
                        e.dma_start(out=dss[:, :], in_=s_d[:, :]).then_inc(fin2s, 16)
                        e.wait_ge(fin2s, 16)
        if "ffn" not in DBG_STAGES:
            fin2 = ctx.sem("dbg_fin2")
            with nc.Block() as b:
                @b.sync
                def _(e):
                    e.dma_start(out=x_out[:, :], in_=x_in[:, :]).then_inc(fin2, 16)
                    e.wait_ge(fin2, 16)
    return nc


def emit_l0_attn(ctx, xn_d, s_d):
    nc = ctx.nc
    H = 16
    for seq in range(NSEQ):
        t0 = seq * SEQ
        ctx.begin_stage()
        with ExitStack() as st:
            qf = ctx.sbuf(st, "l0_q", [128, NCH * SEQ], BF16)
            kf = ctx.sbuf(st, "l0_k", [128, NCH * SEQ], BF16)
            vt = ctx.sbuf(st, "l0_v", [128, 16 * 1024], BF16)
            ao = ctx.sbuf(st, "l0_ao", [128, NCH * SEQ], BF16)
            orow = ctx.g["ones_row"]

            stp = ExitStack()
            wqc = ctx.sbuf(stp, "l0_wq", [128, NCH * 1024], BF16)
            wkc = ctx.sbuf(stp, "l0_wk", [128, NCH * 1024], BF16)
            wvc = ctx.sbuf(stp, "l0_wv", [128, NCH * 1024], BF16)
            bqc = ctx.sbuf(stp, "l0_bqc", [128, 8], F32)
            bkc = ctx.sbuf(stp, "l0_bkc", [128, 8], F32)
            bv = ctx.sbuf(stp, "l0_bv", [1, 1024], BF16)
            xs = ctx.sbuf(stp, "l0_xs", [128, 2 * NCH * 512], BF16)
            pss = [ctx.psum(st, f"l0_ps{i}", [128, 512]) for i in range(2)]

            wl = ctx.sem("l0_wl")
            xl = [ctx.sem("l0_xla"), ctx.sem("l0_xlb")]
            mm = ctx.sem("l0_mm")
            ev = ctx.sem("l0_ev")

            with nc.Block() as b:
                @b.gpsimd
                def _(e):
                    for c in range(NCH):
                        e.dma_start(out=wqc[:, c * 1024:(c + 1) * 1024],
                                    in_=ctx.dram["l0_wq"][c * 128:(c + 1) * 128, :]).then_inc(wl, 16)
                        e.dma_start(out=wkc[:, c * 1024:(c + 1) * 1024],
                                    in_=ctx.dram["l0_wk"][c * 128:(c + 1) * 128, :]).then_inc(wl, 16)
                        e.dma_start(out=wvc[:, c * 1024:(c + 1) * 1024],
                                    in_=ctx.dram["l0_wv"][c * 128:(c + 1) * 128, :]).then_inc(wl, 16)
                    e.dma_start(out=bqc[:], in_=ctx.dram["l0_bqc"][:, :]).then_inc(wl, 16)
                    e.dma_start(out=bkc[:], in_=ctx.dram["l0_bkc"][:, :]).then_inc(wl, 16)
                    e.dma_start(out=bv[:], in_=ctx.dram["l0_bv"][:, :]).then_inc(wl, 16)

                @b.sync
                def _(e):
                    for s in range(4):
                        if s >= 2:
                            e.wait_ge(mm, (s - 1) * 24)
                        for c in range(NCH):
                            e.dma_start(
                                out=xs[:, ((s % 2) * NCH + c) * 512:((s % 2) * NCH + c + 1) * 512],
                                in_=xn_d[c * 128:(c + 1) * 128, t0 + s * 512:t0 + (s + 1) * 512]
                            ).then_inc(xl[s % 2], 16)

                @b.tensor
                def _(e):
                    e.wait_ge(wl, (3 * NCH + 3) * 16)
                    n = 0
                    for s in range(4):
                        e.wait_ge(xl[s % 2], (s // 2 + 1) * NCH * 16)
                        for m in range(16):
                            n += 1
                            if n >= 3:
                                e.wait_ge(ev, n - 2)
                            wmat = wqc if m < 8 else wkc
                            mc = m % 8
                            for c in range(NCH):
                                ii = e.matmul(pss[n % 2][:],
                                              wmat[:, c * 1024 + mc * 128:c * 1024 + (mc + 1) * 128],
                                              xs[:, ((s % 2) * NCH + c) * 512:((s % 2) * NCH + c + 1) * 512],
                                              start=(c == 0), stop=(c == NCH - 1))
                            ii.then_inc(mm, 1)
                        for tt in range(4):
                            for hf in range(2):
                                n += 1
                                if n >= 3:
                                    e.wait_ge(ev, n - 2)
                                for c in range(NCH):
                                    e.matmul(
                                        pss[n % 2][:],
                                        xs[:, ((s % 2) * NCH + c) * 512 + tt * 128:
                                           ((s % 2) * NCH + c) * 512 + (tt + 1) * 128],
                                        wvc[:, c * 1024 + hf * 512:c * 1024 + (hf + 1) * 512],
                                        start=(c == 0), stop=False)
                                e.matmul(pss[n % 2][:], orow[:, 0:128],
                                         bv[:, hf * 512:(hf + 1) * 512],
                                         start=False, stop=True).then_inc(mm, 1)

                @b.scalar
                def _(e):
                    n = 0
                    for s in range(4):
                        for m in range(16):
                            n += 1
                            e.wait_ge(mm, n)
                            dst = qf if m < 8 else kf
                            bcol_t = bqc if m < 8 else bkc
                            mc = m % 8
                            e.activation(dst[:, mc * SEQ + s * 512:mc * SEQ + (s + 1) * 512],
                                         pss[n % 2][:], AF.Identity,
                                         bias=bcol_t[:, mc:mc + 1]).then_inc(ev, 1)
                        for tt in range(4):
                            for hf in range(2):
                                n += 1
                                e.wait_ge(mm, n)
                                kt = s * 4 + tt
                                e.activation(vt[:, kt * 1024 + hf * 512:kt * 1024 + (hf + 1) * 512],
                                             pss[n % 2][:], AF.Copy).then_inc(ev, 1)

            stp.close()
            if DBG_SUB < 1:
                continue
            msk = ctx.sbuf(st, "l0_msk", [128, 2 * 1024], F32)
            pt_sb = ctx.sbuf(st, "l0_pt", [128, 4 * 512], BF16)
            rln = ctx.sbuf(st, "l0_rln", [64, 2 * 512], F32)
            rin = ctx.sbuf(st, "l0_rin", [64, 2 * 512], F32)
            onc = ctx.g["ones_col"]
            ps_st = [ctx.psum(st, f"l0_pst{i}", [128, 1024]) for i in range(2)]
            ps_r = ctx.psum(st, "l0_psr", [64, 512])
            ps_av = ctx.psum(st, "l0_psav", [64, 512])

            ml = ctx.sem("l0_ml")
            sm = ctx.sem("l0_sm")
            pa = ctx.sem("l0_pa")
            ex = ctx.sem("l0_ex")
            cs = ctx.sem("l0_cs")
            rr = ctx.sem("l0_rr")
            fz = ctx.sem("l0_fz")

            groups = [(qt, h) for qt in range(4) for h in range(H)]
            nkp_of = {qt: 2 * qt + 2 for qt in range(4)}

            with nc.Block() as b:
                @b.sync
                def _(e):
                    for pp_ in range(2):
                        e.dma_start(out=msk[:, pp_ * 1024:(pp_ + 1) * 1024],
                                    in_=ctx.dram["c_mask_causal2"][pp_ * 128:(pp_ + 1) * 128, :]
                                    ).then_inc(ml, 16)

                @b.vector
                def _(e):
                    e.wait_ge(ml, 32)
                    np_ = 0
                    for gi, (qt, h) in enumerate(groups):
                        for kp in range(nkp_of[qt]):
                            np_ += 1
                            if kp >= 2 * qt:
                                e.wait_ge(sm, np_)
                                dp = kp - 2 * qt
                                e.tensor_tensor(ps_st[np_ % 2][:], ps_st[np_ % 2][:],
                                                msk[:, dp * 1024:(dp + 1) * 1024],
                                                op=ALU.add).then_inc(pa, 1)
                        e.wait_ge(rr, 2 * (gi + 1))
                        ch, hp = h // 2, (h % 2) * 64
                        e.tensor_tensor(ao[hp:hp + 64, ch * SEQ + qt * 512:ch * SEQ + (qt + 1) * 512],
                                        ps_av[:], rin[:, (gi % 2) * 512:(gi % 2 + 1) * 512],
                                        op=ALU.mult).then_inc(fz, 1)

                @b.tensor
                def _(e):
                    np_ = 0
                    csn = 0
                    for gi, (qt, h) in enumerate(groups):
                        ch, hp = h // 2, (h % 2) * 64
                        nkp = nkp_of[qt]
                        for kp in range(nkp):
                            np_ += 1
                            if kp == 1 and gi >= 1:
                                e.wait_ge(fz, gi)
                            if np_ >= 3:
                                e.wait_ge(ex, np_ - 2)
                            for half in range(2):
                                kt = 2 * kp + half
                                ii = e.matmul(ps_st[np_ % 2][:, half * 512:(half + 1) * 512],
                                              kf[hp:hp + 64, ch * SEQ + kt * 128:ch * SEQ + (kt + 1) * 128],
                                              qf[hp:hp + 64, ch * SEQ + qt * 512:ch * SEQ + (qt + 1) * 512],
                                              start=True, stop=True)
                            ii.then_inc(sm, 1)
                            if kp >= 1:
                                e.wait_ge(ex, np_ - 1)
                                pb = ((np_ - 1) % 2) * 1024
                                for half in range(2):
                                    kt = 2 * (kp - 1) + half
                                    e.matmul(ps_r[:], onc[:],
                                             pt_sb[:, pb + half * 512:pb + (half + 1) * 512],
                                             start=(kp == 1 and half == 0), stop=False)
                                    e.matmul(ps_av[:],
                                             vt[:, kt * 1024 + h * 64:kt * 1024 + (h + 1) * 64],
                                             pt_sb[:, pb + half * 512:pb + (half + 1) * 512],
                                             start=(kp == 1 and half == 0),
                                             stop=False).then_inc(cs, 1)
                        e.wait_ge(ex, np_)
                        pb = (np_ % 2) * 1024
                        for half in range(2):
                            kt = 2 * (nkp - 1) + half
                            e.matmul(ps_r[:], onc[:],
                                     pt_sb[:, pb + half * 512:pb + (half + 1) * 512],
                                     start=False, stop=(half == 1))
                            e.matmul(ps_av[:],
                                     vt[:, kt * 1024 + h * 64:kt * 1024 + (h + 1) * 64],
                                     pt_sb[:, pb + half * 512:pb + (half + 1) * 512],
                                     start=False, stop=(half == 1)).then_inc(cs, 1)
                        csn += 2 * nkp

                @b.scalar
                def _(e):
                    np_ = 0
                    csn = 0
                    npa = 0
                    for gi, (qt, h) in enumerate(groups):
                        for kp in range(nkp_of[qt]):
                            np_ += 1
                            e.wait_ge(sm, np_)
                            if kp >= 2 * qt:
                                npa += 1
                                e.wait_ge(pa, npa)
                            if np_ >= 3:
                                e.wait_ge(cs, 2 * (np_ - 2))
                            e.activation(pt_sb[:, (np_ % 2) * 1024:(np_ % 2 + 1) * 1024],
                                         ps_st[np_ % 2][:], AF.Exp, scale=0.125,
                                         bias=ctx.g["neg_smc"][:]).then_inc(ex, 1)
                        csn += 2 * nkp_of[qt]
                        e.wait_ge(cs, csn)
                        if gi >= 2:
                            e.wait_ge(fz, gi - 1)
                        e.activation(rln[:, (gi % 2) * 512:(gi % 2 + 1) * 512],
                                     ps_r[:], AF.Ln).then_inc(rr, 1)
                        e.drain()
                        e.activation(rin[:, (gi % 2) * 512:(gi % 2 + 1) * 512],
                                     rln[:, (gi % 2) * 512:(gi % 2 + 1) * 512],
                                     AF.Exp, scale=-1.0).then_inc(rr, 1)

            if DBG_SUB < 2:
                continue
            wp = ctx.sbuf(st, "l0_wp", [128, NCH * 1024], BF16)
            bpc = ctx.sbuf(st, "l0_bpc", [128, 8], F32)
            ob = ctx.sbuf(st, "l0_ob", [128, 2 * 512], F32)

            wl2 = ctx.sem("l0_wl2")
            mm2 = ctx.sem("l0_mm2")
            ev2 = ctx.sem("l0_ev2")
            so2 = [ctx.sem("l0_so2a"), ctx.sem("l0_so2b")]

            with nc.Block() as b:
                @b.gpsimd
                def _(e):
                    for c in range(NCH):
                        e.dma_start(out=wp[:, c * 1024:(c + 1) * 1024],
                                    in_=ctx.dram["l0_wproj"][c * 128:(c + 1) * 128, :]
                                    ).then_inc(wl2, 16)
                    e.dma_start(out=bpc[:], in_=ctx.dram["l0_bprojc"][:, :]).then_inc(wl2, 16)

                @b.tensor
                def _(e):
                    e.wait_ge(wl2, (NCH + 1) * 16)
                    n = 0
                    for s in range(4):
                        for m in range(NCH):
                            n += 1
                            if n >= 3:
                                e.wait_ge(ev2, n - 2)
                            for c in range(NCH):
                                ii = e.matmul(pss[n % 2][:],
                                              wp[:, c * 1024 + m * 128:c * 1024 + (m + 1) * 128],
                                              ao[:, c * SEQ + s * 512:c * SEQ + (s + 1) * 512],
                                              start=(c == 0), stop=(c == NCH - 1))
                            ii.then_inc(mm2, 1)

                @b.scalar
                def _(e):
                    n = 0
                    for s in range(4):
                        for m in range(NCH):
                            n += 1
                            e.wait_ge(mm2, n)
                            if n >= 3:
                                e.wait_ge(so2[n % 2], (n // 2 - 1 + (n % 2)) * 16)
                            e.activation(ob[:, (n % 2) * 512:(n % 2 + 1) * 512],
                                         pss[n % 2][:], AF.Identity,
                                         bias=bpc[:, m:m + 1]).then_inc(ev2, 1)

                @b.sync
                def _(e):
                    n = 0
                    for s in range(4):
                        for m in range(NCH):
                            n += 1
                            e.wait_ge(ev2, n)
                            e.dma_start(out=s_d[m * 128:(m + 1) * 128,
                                               t0 + s * 512:t0 + (s + 1) * 512],
                                        in_=ob[:, (n % 2) * 512:(n % 2 + 1) * 512]
                                        ).then_inc(so2[n % 2], 16)

                @b.gpsimd
                def _(e):
                    e.wait_ge(so2[0], (4 * NCH // 2) * 16)
                    e.wait_ge(so2[1], (4 * NCH // 2) * 16)


def emit_l0_ffn_combine(ctx, xn_d, s_d, x_src, x_dst):
    emit_moe_ffn_combine(ctx, xn_d, s_d, x_src, x_dst, kind="l0")


def emit_l2_ffn_combine(ctx, xn_d, s_d, x_src, x_dst):
    emit_moe_ffn_combine(ctx, xn_d, s_d, x_src, x_dst, kind="l2")


def emit_moe_routing(ctx, xn_d, rw_sb, kind, st):
    ctx.begin_stage()
    """Routing pre-pass: writes per-token per-expert combine weights as fp32
    columns rw_sb [128, NTILE*8] (tile-major, 8 expert cols per 128-token tile).

    l0: rw[:, e] = 2.0 * (e_star == e), e_star = max index of top-2 logits
        (logits = xn @ gate_w + gate_b); expert 0 never selected.
    l2: rw[:, e] = softmax(logits)[e] * (e in top2 of (softmax + gate_b)).
    """
    nc = ctx.nc
    NSL = T // 512
    gate_w_name = "l0_gate_w" if kind == "l0" else "l2_gate_wT"
    gate_b_name = "l0_gate_b" if kind == "l0" else "l2_gate_b"

    idx8p = ctx.sbuf(st, "mo_idx", [128, 8], F32)
    gbbc = ctx.sbuf(st, "mo_gbbc", [128, 8], F32)
    t8 = ctx.sbuf(st, "mo_t8", [128, 8], F32)
    p8 = ctx.sbuf(st, "mo_p8", [128, 8], F32)
    v8 = ctx.sbuf(st, "mo_v8", [128, 8], F32)
    col = ctx.sbuf(st, "mo_col", [128, 4], F32)
    gw = ctx.sbuf(st, "mo_gw", [128, NCH * 8], BF16)
    gbr = ctx.sbuf(st, "mo_gb", [1, 8], BF16)
    xs = ctx.sbuf(st, "mo_xs", [128, 2 * NCH * 512], BF16)
    orow = ctx.g["ones_row"]
    ps_l = [ctx.psum(st, f"mo_psl{i}", [128, 512]) for i in range(2)]

    wl = ctx.sem("mo_wl")
    xl = [ctx.sem("mo_xl0"), ctx.sem("mo_xl1")]
    mm = ctx.sem("mo_mm")
    va = ctx.sem("mo_va")
    sa = ctx.sem("mo_sa")
    vr = ctx.sem("mo_vr")
    gbc = ctx.sem("mo_gbc")

    with nc.Block() as b:
        @b.gpsimd
        def _(e):
            for c in range(NCH):
                e.dma_start(out=gw[:, c * 8:(c + 1) * 8],
                            in_=ctx.dram[gate_w_name][c * 128:(c + 1) * 128, :]
                            ).then_inc(wl, 16)
            e.dma_start(out=gbr[:], in_=ctx.dram[gate_b_name][:, :]).then_inc(wl, 16)
            e.dma_start(out=idx8p[:], in_=ctx.dram["c_idx8"][:, :]).then_inc(wl, 16)

        @b.sync
        def _(e):
            for s in range(NSL):
                if s >= 2:
                    e.wait_ge(vr, (s - 1) * 4)
                for c in range(NCH):
                    e.dma_start(
                        out=xs[:, ((s % 2) * NCH + c) * 512:((s % 2) * NCH + c + 1) * 512],
                        in_=xn_d[c * 128:(c + 1) * 128, s * 512:(s + 1) * 512]
                    ).then_inc(xl[s % 2], 16)

        @b.tensor
        def _(e):
            e.wait_ge(wl, (NCH + 2) * 16)
            if kind == "l2":
                e.matmul(ps_l[1][0:128, 0:8], orow[0:1, 0:128], gbr[:],
                         start=True, stop=True).then_inc(mm, 1)
            else:
                e.nop().then_inc(mm, 1)
            n = 0
            pfree = sa if kind == "l2" else vr
            for s in range(NSL):
                e.wait_ge(xl[s % 2], (s // 2 + 1) * NCH * 16)
                for tt in range(4):
                    n += 1
                    if n == 1 and kind == "l2":
                        e.wait_ge(gbc, 1)
                    if n >= 3:
                        e.wait_ge(pfree, n - 2)
                    for c in range(NCH):
                        ii = e.matmul(ps_l[n % 2][0:128, 0:8],
                                      xs[:, ((s % 2) * NCH + c) * 512 + tt * 128:
                                         ((s % 2) * NCH + c) * 512 + (tt + 1) * 128],
                                      gw[:, c * 8:(c + 1) * 8],
                                      start=(c == 0), stop=(c == NCH - 1 and kind == "l2"))
                    if kind == "l0":
                        ii = e.matmul(ps_l[n % 2][0:128, 0:8], orow[:, 0:128], gbr[:],
                                      start=False, stop=True)
                    ii.then_inc(mm, 1)

        if kind == "l2":
            @b.scalar
            def _(e):
                n = 0
                for s in range(NSL):
                    for tt in range(4):
                        n += 1
                        e.wait_ge(va, 2 * n)
                        e.activation(p8[:], ps_l[n % 2][0:128, 0:8], AF.Exp,
                                     bias=col[:, 0:1], accum_out=col[:, 1:2]
                                     ).then_inc(sa, 1)

        @b.vector
        def _(e):
            if kind == "l2":
                e.wait_ge(mm, 1)
                e.tensor_copy(gbbc[:], ps_l[1][0:128, 0:8]).then_inc(gbc, 1)
            n = 0
            for s in range(NSL):
                for tt in range(4):
                    n += 1
                    tile = s * 4 + tt
                    e.wait_ge(mm, n + 1)
                    if kind == "l0":
                        e.max(t8[:], ps_l[n % 2][0:128, 0:8])
                        e.drain()
                        e.tensor_scalar(p8[:], ps_l[n % 2][0:128, 0:8], t8[:, 1:2],
                                        None, op0=ALU.is_ge)
                        e.drain()
                        e.tensor_tensor(v8[:], p8[:], idx8p[:], op=ALU.mult)
                        e.drain()
                        e.reduce_max(col[:, 0:1], v8[:], axis=AX.X)
                        e.drain()
                        e.memset(rw_sb[:, tile * 8:tile * 8 + 1], 0.0)
                        for ex in range(1, 8):
                            ii = e.tensor_scalar(rw_sb[:, tile * 8 + ex:tile * 8 + ex + 1],
                                                 col[:, 0:1], float(ex + 1), 2.0,
                                                 op0=ALU.is_equal, op1=ALU.mult)
                        ii.then_inc(vr, 1)
                    else:
                        e.reduce_max(col[:, 0:1], ps_l[n % 2][0:128, 0:8],
                                     axis=AX.X, negate=True)
                        e.drain()
                        e.nop().then_inc(va, 2)
                        e.wait_ge(sa, n)
                        e.reciprocal(col[:, 2:3], col[:, 1:2])
                        e.drain()
                        e.tensor_scalar(p8[:], p8[:], col[:, 2:3], None, op0=ALU.mult)
                        e.drain()
                        e.tensor_tensor(t8[:], p8[:], gbbc[:], op=ALU.add)
                        e.drain()
                        e.max(v8[:], t8[:])
                        e.drain()
                        e.tensor_scalar(v8[:], t8[:], v8[:, 1:2], None, op0=ALU.is_ge)
                        e.drain()
                        e.tensor_tensor(rw_sb[:, tile * 8:(tile + 1) * 8], p8[:], v8[:],
                                        op=ALU.mult).then_inc(vr, 1)


def emit_moe_experts(ctx, xn_d, rw_sb, y_d, kind, st):
    ctx.begin_stage()
    """Expert sweeps: y_d[t, d] (token-major bf16 DRAM) = sum_e w_e(t) * y_e(t,:).

    Token groups of TG; per (group, expert): stream w1 (and w3) column blocks,
    full w2; h = act(xn@w1+b1) feature-major in SBUF; y accumulated token-major
    in PSUM over hidden chunks then combined into yac with routing columns.
    """
    nc = ctx.nc
    if kind == "l0":
        experts = [("l0_w1s", "l0_b1c", "l0_w2", "l0_b2", e, "rw") for e in range(1, 8)]
        nhc, TG = 32, 1024
    else:
        experts = [("l2_w1s", "l2_b1c", "l2_w2", "l2_b2", e, "rw") for e in range(8)]
        experts.append(("l2_sw1s", "l2_sb1c", "l2_sw2", "l2_sb2", None, "one"))
        nhc, TG = 11, 2048
    NE = len(experts)
    NG = T // TG
    NTS = TG // 512
    NTT = TG // 128
    l2 = kind == "l2"

    xg = ctx.sbuf(st, "me_xg", [128, NCH * TG], BF16)
    hsb = ctx.sbuf(st, "me_h", [128, nhc * TG], BF16)
    w1t = ctx.sbuf(st, "me_w1t", [128, 4 * NCH * 128], BF16)
    w3t = ctx.sbuf(st, "me_w3t", [128, 4 * NCH * 128], BF16) if l2 else None
    w2sb = ctx.sbuf(st, "me_w2sb", [128, nhc * 1024], BF16)
    b1c = ctx.sbuf(st, "me_b1c", [128, 2 * nhc], F32)
    b3c = ctx.sbuf(st, "me_b3c", [128, 2 * nhc], F32) if l2 else None
    b2r = ctx.sbuf(st, "me_b2r", [1, 2 * 1024], BF16)
    yac = ctx.sbuf(st, "me_yac", [128, NTT * 1024], BF16)
    asb = ctx.sbuf(st, "me_asb", [128, 2 * 512], F32) if l2 else None
    psb = ctx.sbuf(st, "me_psb", [128, 2 * 512], F32) if l2 else None
    orow = ctx.g["ones_row"]
    ps_h = [ctx.psum(st, f"me_psh{i}", [128, 512]) for i in range(2)]
    ps_g = [ctx.psum(st, f"me_psg{i}", [128, 512]) for i in range(2)] if l2 else None
    ps_y = [ctx.psum(st, f"me_psy{i}", [128, 512]) for i in range(2)]

    xgl = ctx.sem("me_xgl")
    w1l = [ctx.sem(f"me_w1l{i}") for i in range(4)]
    w2l = ctx.sem("me_w2l")
    b1l = ctx.sem("me_b1l")
    hm = ctx.sem("me_hm")
    he = ctx.sem("me_he")
    hv = ctx.sem("me_hv") if l2 else None
    ym = ctx.sem("me_ym")
    yv = ctx.sem("me_yv")
    yst = [ctx.sem("me_yst0"), ctx.sem("me_yst1")]

    HEPI = 2 if l2 else 1
    NBW = 3 if l2 else 2
    NW1 = 2 if l2 else 1

    def wslices(w1n, b1n, w2n, b2n, enum):
        w1d = ctx.dram[w1n] if enum is None else ctx.dram[w1n][enum]
        b1d = ctx.dram[b1n] if enum is None else ctx.dram[b1n][enum]
        w2d = ctx.dram[w2n] if enum is None else ctx.dram[w2n][enum]
        b2d = ctx.dram[b2n] if enum is None else ctx.dram[b2n][enum:enum + 1]
        return w1d, b1d, w2d, b2d

    with nc.Block() as b:
        @b.gpsimd
        def _(e):
            for gi in range(NG):
                if gi >= 1:
                    e.wait_ge(ym, gi * NE * NTT * 2)
                for c in range(NCH):
                    e.dma_start(out=xg[:, c * TG:(c + 1) * TG],
                                in_=xn_d[c * 128:(c + 1) * 128, gi * TG:(gi + 1) * TG]
                                ).then_inc(xgl, 16)
                for ei, (w1n, b1n, w2n, b2n, enum, _) in enumerate(experts):
                    gg = gi * NE + ei
                    w1d, b1d, w2d, b2d = wslices(w1n, b1n, w2n, b2n, enum)
                    if gg >= 2:
                        e.wait_ge(ym, (gg - 1) * NTT * 2)
                    bo = (gg % 2) * nhc
                    e.dma_start(out=b1c[:, bo:bo + nhc], in_=b1d[:, :]).then_inc(b1l, 16)
                    if l2:
                        b3d = ctx.dram["l2_b3c"][enum] if enum is not None \
                            else ctx.dram["l2_sb3c"]
                        e.dma_start(out=b3c[:, bo:bo + nhc], in_=b3d[:, :]).then_inc(b1l, 16)
                    e.dma_start(out=b2r[:, (gg % 2) * 1024:(gg % 2 + 1) * 1024],
                                in_=b2d[:, :]).then_inc(b1l, 16)
                    for fc in range(nhc):
                        gfc = gg * nhc + fc
                        if fc == nhc // 3:
                            if gg >= 1:
                                e.wait_ge(ym, gg * NTT * 2)
                            e.dma_start(out=w2sb[:, 0:nhc * 1024],
                                        in_=w2d[:, :].rearrange("(c p) d -> p c d", p=128)
                                        ).then_inc(w2l, 16)
                        if gfc >= 4:
                            e.wait_ge(hm, (gfc - 3) * NTS)
                        e.dma_start(
                            out=w1t[:, (gfc % 4) * NCH * 128:(gfc % 4 + 1) * NCH * 128],
                            in_=w1d[fc]).then_inc(w1l[gfc % 4], 16)
                        if l2:
                            w3d = ctx.dram["l2_w3s"][enum] if enum is not None \
                                else ctx.dram["l2_sw3s"]
                            e.dma_start(
                                out=w3t[:, (gfc % 4) * NCH * 128:(gfc % 4 + 1) * NCH * 128],
                                in_=w3d[fc]).then_inc(w1l[gfc % 4], 16)

        @b.tensor
        def _(e):
            nh = 0
            ny = 0
            for gi in range(NG):
                e.wait_ge(xgl, (gi + 1) * NCH * 16)
                for ei, (w1n, b1n, w2n, b2n, enum, wsel) in enumerate(experts):
                    gg = gi * NE + ei
                    for fc in range(nhc):
                        gfc = gg * nhc + fc
                        for ts in range(NTS):
                            nh += 1
                            if ts == 0:
                                e.wait_ge(w1l[gfc % 4], (gfc // 4 + 1) * NW1 * 16)
                                if fc == 0:
                                    e.wait_ge(b1l, (gg + 1) * NBW * 16)
                            if nh >= 3:
                                e.wait_ge(hv if l2 else he, (nh - 2) * (1 if l2 else HEPI))
                            for c in range(NCH):
                                ii = e.matmul(ps_h[nh % 2][:],
                                              w1t[:, (gfc % 4) * NCH * 128 + c * 128:
                                                  (gfc % 4) * NCH * 128 + (c + 1) * 128],
                                              xg[:, c * TG + ts * 512:c * TG + (ts + 1) * 512],
                                              start=(c == 0), stop=(c == NCH - 1))
                            if l2:
                                for c in range(NCH):
                                    ii = e.matmul(ps_g[nh % 2][:],
                                                  w3t[:, (gfc % 4) * NCH * 128 + c * 128:
                                                      (gfc % 4) * NCH * 128 + (c + 1) * 128],
                                                  xg[:, c * TG + ts * 512:c * TG + (ts + 1) * 512],
                                                  start=(c == 0), stop=(c == NCH - 1))
                            ii.then_inc(hm, 1)
                    e.wait_ge(w2l, (gg + 1) * 16)
                    e.wait_ge(he, (gg + 1) * nhc * NTS * HEPI)
                    for tt in range(NTT):
                        for hf in range(2):
                            ny += 1
                            if ny >= 3:
                                e.wait_ge(yv, ny - 2)
                            for fc2 in range(nhc):
                                e.matmul(ps_y[ny % 2][:],
                                         hsb[:, fc2 * TG + tt * 128:fc2 * TG + (tt + 1) * 128],
                                         w2sb[:, fc2 * 1024 + hf * 512:fc2 * 1024 + (hf + 1) * 512],
                                         start=(fc2 == 0), stop=False)
                            e.matmul(ps_y[ny % 2][:], orow[0:1, 0:128],
                                     b2r[:, (gg % 2) * 1024 + hf * 512:
                                         (gg % 2) * 1024 + (hf + 1) * 512],
                                     start=False, stop=True).then_inc(ym, 1)

        @b.scalar
        def _(e):
            nh = 0
            for gi in range(NG):
                for ei in range(NE):
                    gg = gi * NE + ei
                    if gg >= 1:
                        e.wait_ge(ym, gg * NTT * 2)
                    for fc in range(nhc):
                        for ts in range(NTS):
                            nh += 1
                            e.wait_ge(hm, nh)
                            if kind == "l0":
                                e.activation(hsb[:, fc * TG + ts * 512:fc * TG + (ts + 1) * 512],
                                             ps_h[nh % 2][:], AF.Gelu,
                                             bias=b1c[:, (gg % 2) * nhc + fc:
                                                       (gg % 2) * nhc + fc + 1]).then_inc(he, 1)
                            else:
                                e.activation(asb[:, (nh % 2) * 512:(nh % 2 + 1) * 512],
                                             ps_h[nh % 2][:], AF.Identity,
                                             bias=b1c[:, (gg % 2) * nhc + fc:
                                                       (gg % 2) * nhc + fc + 1]).then_inc(he, 1)
                                e.wait_ge(hv, nh)
                                e.activation(hsb[:, fc * TG + ts * 512:fc * TG + (ts + 1) * 512],
                                             psb[:, (nh % 2) * 512:(nh % 2 + 1) * 512],
                                             AF.Silu).then_inc(he, 1)

        if l2:
            @b.vector
            def _(e):
                nh = 0
                ny = 0
                for gi in range(NG):
                    if gi >= 1:
                        e.wait_ge(yst[(gi - 1) % 2], (gi // 2 + (gi % 2)) * 16)
                    e.memset(yac[:, 0:NTT * 1024], 0.0)
                    e.drain()
                    for ei in range(NE):
                        gg = gi * NE + ei
                        for fc in range(nhc):
                            for ts in range(NTS):
                                nh += 1
                                e.wait_ge(he, 2 * nh - 1)
                                e.scalar_tensor_tensor(
                                    psb[:, (nh % 2) * 512:(nh % 2 + 1) * 512],
                                    ps_g[nh % 2][:],
                                    b3c[:, (gg % 2) * nhc + fc:(gg % 2) * nhc + fc + 1],
                                    asb[:, (nh % 2) * 512:(nh % 2 + 1) * 512],
                                    op0=ALU.add, op1=ALU.mult).then_inc(hv, 1)
                        wsel = experts[ei][5]
                        enum = experts[ei][4]
                        for tt in range(NTT):
                            for hf in range(2):
                                ny += 1
                                e.wait_ge(ym, ny)
                                ysl = slice(tt * 1024 + hf * 512, tt * 1024 + (hf + 1) * 512)
                                tile = gi * NTT + tt
                                if wsel == "one":
                                    e.scalar_tensor_tensor(
                                        yac[:, ysl], ps_y[ny % 2][:], 1.0, yac[:, ysl],
                                        op0=ALU.mult, op1=ALU.add).then_inc(yv, 1)
                                else:
                                    e.scalar_tensor_tensor(
                                        yac[:, ysl], ps_y[ny % 2][:],
                                        rw_sb[:, tile * 8 + enum:tile * 8 + enum + 1],
                                        yac[:, ysl], op0=ALU.mult, op1=ALU.add).then_inc(yv, 1)
        else:
            @b.vector
            def _(e):
                ny = 0
                for gi in range(NG):
                    if gi >= 1:
                        e.wait_ge(yst[(gi - 1) % 2], (gi // 2 + (gi % 2)) * 16)
                    e.memset(yac[:, 0:NTT * 1024], 0.0)
                    e.drain()
                    for ei in range(NE):
                        enum = experts[ei][4]
                        for tt in range(NTT):
                            for hf in range(2):
                                ny += 1
                                e.wait_ge(ym, ny)
                                ysl = slice(tt * 1024 + hf * 512, tt * 1024 + (hf + 1) * 512)
                                tile = gi * NTT + tt
                                e.scalar_tensor_tensor(
                                    yac[:, ysl], ps_y[ny % 2][:],
                                    rw_sb[:, tile * 8 + enum:tile * 8 + enum + 1],
                                    yac[:, ysl], op0=ALU.mult, op1=ALU.add).then_inc(yv, 1)

        @b.sync
        def _(e):
            for gi in range(NG):
                e.wait_ge(yv, (gi + 1) * NE * NTT * 2)
                e.dma_start(out=y_d[gi * TG:(gi + 1) * TG, :].rearrange(
                                "(tt p) d -> p tt d", p=128),
                            in_=yac[:, 0:NTT * 1024]).then_inc(yst[gi % 2], 16)
            e.wait_ge(yst[0], ((NG + 1) // 2) * 16)
            if NG > 1:
                e.wait_ge(yst[1], (NG // 2) * 16)


def emit_moe_finalize(ctx, y_d, s_d, x_src, x_dst, st):
    ctx.begin_stage()
    """g = silu(y_d) (token-major) -> PE transpose to feature-major ->
    x_dst = x_src + s_d * g."""
    nc = ctx.nc
    NTILE = T // 128
    NSL = T // 512
    yt = ctx.sbuf(st, "mf_yt", [128, 2 * 1024], BF16)
    gtm = ctx.sbuf(st, "mf_gtm", [128, 2 * 1024], BF16)
    gfm = ctx.sbuf(st, "mf_gfm", [128, NCH * 512], BF16)
    sxr = ctx.sbuf(st, "mf_sxr", [128, NCH * 512], F32)
    xxr = ctx.sbuf(st, "mf_xxr", [128, NCH * 512], F32)
    xov = ctx.sbuf(st, "mf_xov", [128, 2 * NCH * 512], F32)
    identb = ctx.g["ident_bf"]
    ps_t = [ctx.psum(st, f"mf_pst{i}", [128, 128], BF16) for i in range(2)]

    yl = [ctx.sem("mf_yl0"), ctx.sem("mf_yl1")]
    gs = ctx.sem("mf_gs")
    tm = ctx.sem("mf_tm")
    ge = ctx.sem("mf_ge")
    cl = ctx.sem("mf_cl")
    cb = ctx.sem("mf_cb")
    so = [ctx.sem("mf_so0"), ctx.sem("mf_so1")]

    with nc.Block() as b:
        @b.sync
        def _(e):
            for s in range(NSL):
                if s >= 1:
                    e.wait_ge(cb, s * NCH)
                    for c in range(NCH):
                        e.dma_start(out=x_dst[c * 128:(c + 1) * 128,
                                              (s - 1) * 512:s * 512],
                                    in_=xov[:, ((s - 1) % 2) * NCH * 512 + c * 512:
                                            ((s - 1) % 2) * NCH * 512 + (c + 1) * 512]
                                    ).then_inc(so[(s - 1) % 2], 16)
                for tt in range(4 * s, 4 * s + 4):
                    if tt >= 2:
                        e.wait_ge(gs, tt - 1)
                    e.dma_start(out=yt[:, (tt % 2) * 1024:(tt % 2 + 1) * 1024],
                                in_=y_d[tt * 128:(tt + 1) * 128, :]).then_inc(yl[tt % 2], 16)
                for c in range(NCH):
                    e.dma_start(out=sxr[:, c * 512:(c + 1) * 512],
                                in_=s_d[c * 128:(c + 1) * 128, s * 512:(s + 1) * 512]
                                ).then_inc(cl, 16)
                for c in range(NCH):
                    e.dma_start(out=xxr[:, c * 512:(c + 1) * 512],
                                in_=x_src[c * 128:(c + 1) * 128, s * 512:(s + 1) * 512]
                                ).then_inc(cl, 16)
            e.wait_ge(cb, NSL * NCH)
            for c in range(NCH):
                e.dma_start(out=x_dst[c * 128:(c + 1) * 128, (NSL - 1) * 512:NSL * 512],
                            in_=xov[:, ((NSL - 1) % 2) * NCH * 512 + c * 512:
                                    ((NSL - 1) % 2) * NCH * 512 + (c + 1) * 512]
                            ).then_inc(so[(NSL - 1) % 2], 16)
            e.wait_ge(so[0], (NSL // 2) * 16 * NCH)
            e.wait_ge(so[1], (NSL // 2) * 16 * NCH)

        @b.scalar
        def _(e):
            nt = 0
            for tt in range(NTILE):
                e.wait_ge(yl[tt % 2], (tt // 2 + 1) * 16)
                if tt >= 2:
                    e.wait_ge(tm, (tt - 1) * NCH)
                e.activation(gtm[:, (tt % 2) * 1024:(tt % 2 + 1) * 1024],
                             yt[:, (tt % 2) * 1024:(tt % 2 + 1) * 1024],
                             AF.Silu).then_inc(gs, 1)
                for c in range(NCH):
                    nt += 1
                    e.wait_ge(tm, nt)
                    e.activation(gfm[:, c * 512 + (tt % 4) * 128:
                                     c * 512 + (tt % 4) * 128 + 128],
                                 ps_t[nt % 2][:], AF.Copy).then_inc(ge, 1)

        @b.tensor
        def _(e):
            nt = 0
            for tt in range(NTILE):
                e.wait_ge(gs, tt + 1)
                if tt % 4 == 0 and tt >= 4:
                    e.wait_ge(cb, (tt // 4) * NCH)
                for c in range(NCH):
                    nt += 1
                    if nt >= 3:
                        e.wait_ge(ge, nt - 2)
                    e.transpose(ps_t[nt % 2][:],
                                gtm[:, (tt % 2) * 1024 + c * 128:
                                    (tt % 2) * 1024 + (c + 1) * 128],
                                identb[:]).then_inc(tm, 1)

        @b.vector
        def _(e):
            for s in range(NSL):
                e.wait_ge(ge, (s + 1) * 4 * NCH)
                e.wait_ge(cl, (s + 1) * 2 * NCH * 16)
                if s >= 2:
                    e.wait_ge(so[s % 2], (s // 2) * NCH * 16)
                for c in range(NCH):
                    e.tensor_tensor(gfm[:, c * 512:(c + 1) * 512],
                                    gfm[:, c * 512:(c + 1) * 512],
                                    sxr[:, c * 512:(c + 1) * 512], op=ALU.mult)
                    e.drain()
                    e.tensor_tensor(xov[:, (s % 2) * NCH * 512 + c * 512:
                                        (s % 2) * NCH * 512 + (c + 1) * 512],
                                    xxr[:, c * 512:(c + 1) * 512],
                                    gfm[:, c * 512:(c + 1) * 512],
                                    op=ALU.add).then_inc(cb, 1)


def emit_moe_ffn_combine(ctx, xn_d, s_d, x_src, x_dst, kind):
    nc = ctx.nc
    NTILE = T // 128
    y_d = ctx.dtmp(f"y_{kind}_{ctx.uid()}", [T, 1024], BF16)
    with ExitStack() as st:
        rw_sb = ctx.sbuf(st, "mo_rwsb", [128, NTILE * 8], F32)
        with ExitStack() as str_:
            emit_moe_routing(ctx, xn_d, rw_sb, kind, str_)
        with ExitStack() as ste:
            emit_moe_experts(ctx, xn_d, rw_sb, y_d, kind, ste)
        with ExitStack() as stf:
            emit_moe_finalize(ctx, y_d, s_d, x_src, x_dst, stf)


def emit_l2_attn(ctx, xn_d, s_d):
    nc = ctx.nc
    ctx.begin_stage()
    NSL = T // 512
    with ExitStack() as st:
        vsb = ctx.sbuf(st, "l2_v", [128, NCH * T], BF16)
        ckb = ctx.sbuf(st, "l2_ckb", [16, T], BF16)
        skb = ctx.sbuf(st, "l2_skb", [16, T], BF16)
        cqb = ctx.sbuf(st, "l2_cqb", [16, T], BF16)
        sqb = ctx.sbuf(st, "l2_sqb", [16, T], BF16)
        ind = ctx.sbuf(st, "l2_ind", [16, NCH * 128], BF16)
        pss = [ctx.psum(st, f"l2_ps{i}", [128, 512]) for i in range(2)]

        stp2 = ExitStack()
        tq = ctx.sbuf(stp2, "l2_tq", [16, T], F32)
        tk = ctx.sbuf(stp2, "l2_tk", [16, T], F32)
        stp1 = ExitStack()
        wv = ctx.sbuf(stp1, "l2_wv", [128, NCH * 1024], BF16)
        wqk = ctx.sbuf(stp1, "l2_wqk", [128, NCH * 32], BF16)
        xs = ctx.sbuf(stp1, "l2_xs", [128, 2 * NCH * 512], BF16)

        wl = ctx.sem("l2_wl")
        xl = [ctx.sem("l2_xla"), ctx.sem("l2_xlb")]
        mm = ctx.sem("l2_mm")
        ev = ctx.sem("l2_ev")

        with nc.Block() as b:
            @b.gpsimd
            def _(e):
                for c in range(NCH):
                    e.dma_start(out=wv[:, c * 1024:(c + 1) * 1024],
                                in_=ctx.dram["l2_wv"][c * 128:(c + 1) * 128, :]).then_inc(wl, 16)
                    e.dma_start(out=wqk[:, c * 32:c * 32 + 16],
                                in_=ctx.dram["l2_wq"][c * 128:(c + 1) * 128, :]).then_inc(wl, 16)
                    e.dma_start(out=wqk[:, c * 32 + 16:(c + 1) * 32],
                                in_=ctx.dram["l2_wk"][c * 128:(c + 1) * 128, :]).then_inc(wl, 16)
                e.dma_start(out=ind[:], in_=ctx.dram["c_l2ind"][:, :]).then_inc(wl, 16)

            @b.sync
            def _(e):
                for s in range(NSL):
                    if s >= 2:
                        e.wait_ge(mm, (s - 1) * 10)
                    for c in range(NCH):
                        e.dma_start(
                            out=xs[:, ((s % 2) * NCH + c) * 512:((s % 2) * NCH + c + 1) * 512],
                            in_=xn_d[c * 128:(c + 1) * 128, s * 512:(s + 1) * 512]
                        ).then_inc(xl[s % 2], 16)

            @b.tensor
            def _(e):
                e.wait_ge(wl, (3 * NCH + 1) * 16)
                n = 0
                for s in range(NSL):
                    e.wait_ge(xl[s % 2], (s // 2 + 1) * NCH * 16)
                    for m in range(NCH):
                        n += 1
                        if n >= 3:
                            e.wait_ge(ev, n - 2)
                        for c in range(NCH):
                            ii = e.matmul(pss[n % 2][:],
                                          wv[:, c * 1024 + m * 128:c * 1024 + (m + 1) * 128],
                                          xs[:, ((s % 2) * NCH + c) * 512:((s % 2) * NCH + c + 1) * 512],
                                          start=(c == 0), stop=(c == NCH - 1))
                        ii.then_inc(mm, 1)
                    for qk in range(2):
                        n += 1
                        if n >= 3:
                            e.wait_ge(ev, n - 2)
                        for c in range(NCH):
                            ii = e.matmul(pss[n % 2][0:16, :],
                                          wqk[:, c * 32 + qk * 16:c * 32 + qk * 16 + 16],
                                          xs[:, ((s % 2) * NCH + c) * 512:((s % 2) * NCH + c + 1) * 512],
                                          start=(c == 0), stop=(c == NCH - 1))
                        ii.then_inc(mm, 1)

            @b.scalar
            def _(e):
                n = 0
                for s in range(NSL):
                    for m in range(NCH):
                        n += 1
                        e.wait_ge(mm, n)
                        e.activation(vsb[:, m * T + s * 512:m * T + (s + 1) * 512],
                                     pss[n % 2][:], AF.Copy).then_inc(ev, 1)
                    for qk in range(2):
                        n += 1
                        e.wait_ge(mm, n)
                        dst = tq if qk == 0 else tk
                        e.activation(dst[:, s * 512:(s + 1) * 512],
                                     pss[n % 2][0:16, :], AF.Tanh).then_inc(ev, 1)

        stp1.close()

        stA = ExitStack()
        cq = ctx.sbuf(stA, "l2_cq", [16, SEQ], F32)
        sq = ctx.sbuf(stA, "l2_sq", [16, SEQ], F32)
        ck = ctx.sbuf(stA, "l2_ck", [16, SEQ], F32)
        sk = ctx.sbuf(stA, "l2_sk", [16, SEQ], F32)
        den = ctx.sbuf(stA, "l2_den", [16, SEQ], F32)
        t1 = ctx.sbuf(stA, "l2_t1", [16, SEQ], F32)

        PI4 = float(np.pi / 4.0)
        tg = ctx.sem("l2_tg")
        sc = ctx.sem("l2_sc")
        with nc.Block() as b:
            @b.scalar
            def _(e):
                for s in range(NSEQ):
                    sl = slice(s * SEQ, (s + 1) * SEQ)
                    if s >= 1:
                        e.wait_ge(sc, 2 * s)
                    e.activation(cq[:], tq[:, sl], AF.Sin, scale=PI4,
                                 bias=ctx.g["half_pi"][0:16, :])
                    e.activation(sq[:], tq[:, sl], AF.Sin, scale=PI4)
                    e.activation(ck[:], tk[:, sl], AF.Sin, scale=PI4,
                                 bias=ctx.g["half_pi"][0:16, :])
                    e.activation(sk[:], tk[:, sl], AF.Sin, scale=PI4).then_inc(tg, 2)
                    e.wait_ge(sc, 2 * s + 1)
                    e.activation(t1[:], den[:], AF.Ln)
                    e.drain()
                    e.activation(den[:], t1[:], AF.Exp, scale=-1.0).then_inc(tg, 1)

            @b.vector
            def _(e):
                for s in range(NSEQ):
                    sl = slice(s * SEQ, (s + 1) * SEQ)
                    e.wait_ge(tg, 3 * s + 2)
                    e.tensor_tensor_scan(t1[:], ck[:], ck[:], 0.0,
                                         op0=ALU.add, op1=ALU.bypass)
                    e.drain()
                    e.tensor_tensor(den[:], cq[:], t1[:], op=ALU.mult)
                    e.drain()
                    e.tensor_tensor_scan(t1[:], sk[:], sk[:], 0.0,
                                         op0=ALU.add, op1=ALU.bypass)
                    e.drain()
                    e.tensor_tensor(t1[:], sq[:], t1[:], op=ALU.mult)
                    e.drain()
                    e.tensor_tensor(den[:], den[:], t1[:], op=ALU.add)
                    e.drain()
                    e.tensor_scalar(den[:], den[:], 1e-6, None, op0=ALU.add).then_inc(sc, 1)
                    e.wait_ge(tg, 3 * s + 3)
                    e.tensor_tensor(cq[:], cq[:], den[:], op=ALU.mult)
                    e.tensor_tensor(sq[:], sq[:], den[:], op=ALU.mult)
                    e.drain()
                    e.tensor_copy(ckb[:, sl], ck[:])
                    e.tensor_copy(skb[:, sl], sk[:])
                    e.tensor_copy(cqb[:, sl], cq[:])
                    e.tensor_copy(sqb[:, sl], sq[:]).then_inc(sc, 1)

        stA.close()
        stp2.close()

        hd_d = ctx.dtmp(f"l2_hd_{ctx.uid()}", [D, T], BF16)
        hrb = ctx.sbuf(st, "l2_hrb", [128, 2 * 512], BF16)
        bck = ctx.sbuf(st, "l2_bck", [128, 2 * 4 * 512], F32)
        vc = ctx.sbuf(st, "l2_vc", [128, 512], F32)
        vs = ctx.sbuf(st, "l2_vs", [128, 512], F32)
        cvc = ctx.sbuf(st, "l2_cvc", [128, 2 * 512], F32)
        cvs = ctx.sbuf(st, "l2_cvs", [128, 2 * 512], F32)
        th = ctx.sbuf(st, "l2_th", [128, 512], F32)
        ps_b = [ctx.psum(st, f"l2_psb{i}", [128, 512]) for i in range(4)]

        bm = ctx.sem("l2_bm")
        be = ctx.sem("l2_be")
        hv = ctx.sem("l2_hv")
        hst = [ctx.sem("l2_hst0"), ctx.sem("l2_hst1")]

        with nc.Block() as b:
            @b.gpsimd
            def _(e):
                nb = 0
                for c in range(NCH):
                    for s in range(NSEQ):
                        for tt in range(SEQ // 512):
                            t0 = s * SEQ + tt * 512
                            nb += 1
                            e.wait_ge(hv, nb * 5)
                            e.dma_start(out=hd_d[c * 128:(c + 1) * 128, t0:t0 + 512],
                                        in_=hrb[:, (nb % 2) * 512:(nb % 2 + 1) * 512]
                                        ).then_inc(hst[nb % 2], 16)
                e.wait_ge(hst[0], 32 * 16)
                e.wait_ge(hst[1], 32 * 16)

            @b.tensor
            def _(e):
                nb = 0
                for c in range(NCH):
                    for s in range(NSEQ):
                        for tt in range(SEQ // 512):
                            t0 = s * SEQ + tt * 512
                            nb += 1
                            if nb >= 2:
                                e.wait_ge(be, (nb - 1) * 4)
                            for j, row in enumerate((ckb, skb, cqb, sqb)):
                                e.matmul(ps_b[j][:], ind[:, c * 128:(c + 1) * 128],
                                         row[:, t0:t0 + 512],
                                         start=True, stop=True).then_inc(bm, 1)

            @b.scalar
            def _(e):
                nb = 0
                for c in range(NCH):
                    for s in range(NSEQ):
                        for tt in range(SEQ // 512):
                            nb += 1
                            e.wait_ge(bm, nb * 4)
                            if nb >= 3:
                                e.wait_ge(hv, (nb - 2) * 5)
                            par = nb % 2
                            for j in range(4):
                                e.activation(bck[:, par * 2048 + j * 512:par * 2048 + (j + 1) * 512],
                                             ps_b[j][:], AF.Copy).then_inc(be, 1)

            @b.vector
            def _(e):
                nb = 0
                for c in range(NCH):
                    for s in range(NSEQ):
                        for tt in range(SEQ // 512):
                            t0 = s * SEQ + tt * 512
                            nb += 1
                            e.wait_ge(be, nb * 4)
                            e.drain()
                            par = nb % 2
                            bb = par * 2048
                            e.tensor_tensor(vc[:], vsb[:, c * T + t0:c * T + t0 + 512],
                                            bck[:, bb:bb + 512], op=ALU.mult)
                            e.tensor_tensor(vs[:], vsb[:, c * T + t0:c * T + t0 + 512],
                                            bck[:, bb + 512:bb + 1024], op=ALU.mult)
                            e.drain()
                            init_c = 0.0 if tt == 0 else cvc[:, (1 - par) * 512 + 511:(1 - par) * 512 + 512]
                            init_s = 0.0 if tt == 0 else cvs[:, (1 - par) * 512 + 511:(1 - par) * 512 + 512]
                            e.tensor_tensor_scan(cvc[:, par * 512:(par + 1) * 512],
                                                 vc[:], vc[:], init_c,
                                                 op0=ALU.add, op1=ALU.bypass)
                            e.tensor_tensor_scan(cvs[:, par * 512:(par + 1) * 512],
                                                 vs[:], vs[:], init_s,
                                                 op0=ALU.add, op1=ALU.bypass)
                            e.drain()
                            e.tensor_tensor(th[:], bck[:, bb + 1024:bb + 1536],
                                            cvc[:, par * 512:(par + 1) * 512], op=ALU.mult)
                            e.tensor_tensor(vc[:], bck[:, bb + 1536:bb + 2048],
                                            cvs[:, par * 512:(par + 1) * 512], op=ALU.mult)
                            e.drain()
                            if nb >= 3:
                                e.wait_ge(hst[nb % 2], (nb // 2 - 1 + (nb % 2)) * 16)
                            e.tensor_tensor(hrb[:, (nb % 2) * 512:(nb % 2 + 1) * 512],
                                            th[:], vc[:], op=ALU.add).then_inc(hv, 5)

        wo = ctx.sbuf(st, "l2_wo", [128, NCH * 1024], BF16)
        ob = ctx.sbuf(st, "l2_ob", [128, 2 * 512], F32)
        hxs = ctx.sbuf(st, "l2_hxs", [128, 2 * NCH * 512], BF16)
        wl2 = ctx.sem("l2_wl2")
        hxl = [ctx.sem("l2_hxl0"), ctx.sem("l2_hxl1")]
        mm2 = ctx.sem("l2_mm2")
        ev2 = ctx.sem("l2_ev2")
        so2 = [ctx.sem("l2_so2a"), ctx.sem("l2_so2b")]

        with nc.Block() as b:
            @b.gpsimd
            def _(e):
                for c in range(NCH):
                    e.dma_start(out=wo[:, c * 1024:(c + 1) * 1024],
                                in_=ctx.dram["l2_wo"][c * 128:(c + 1) * 128, :]).then_inc(wl2, 16)
                n = 0
                for s in range(NSL):
                    for m in range(NCH):
                        n += 1
                        e.wait_ge(ev2, n)
                        e.dma_start(out=s_d[m * 128:(m + 1) * 128, s * 512:(s + 1) * 512],
                                    in_=ob[:, (n % 2) * 512:(n % 2 + 1) * 512]
                                    ).then_inc(so2[n % 2], 16)
                e.wait_ge(so2[0], (NSL * NCH // 2) * 16)
                e.wait_ge(so2[1], (NSL * NCH // 2) * 16)

            @b.sync
            def _(e):
                for s in range(NSL):
                    if s >= 2:
                        e.wait_ge(mm2, (s - 1) * NCH)
                    for c in range(NCH):
                        e.dma_start(
                            out=hxs[:, ((s % 2) * NCH + c) * 512:((s % 2) * NCH + c + 1) * 512],
                            in_=hd_d[c * 128:(c + 1) * 128, s * 512:(s + 1) * 512]
                        ).then_inc(hxl[s % 2], 16)

            @b.tensor
            def _(e):
                e.wait_ge(wl2, NCH * 16)
                n = 0
                for s in range(NSL):
                    e.wait_ge(hxl[s % 2], (s // 2 + 1) * NCH * 16)
                    for m in range(NCH):
                        n += 1
                        if n >= 3:
                            e.wait_ge(ev2, n - 2)
                        for c in range(NCH):
                            ii = e.matmul(pss[n % 2][:],
                                          wo[:, c * 1024 + m * 128:c * 1024 + (m + 1) * 128],
                                          hxs[:, ((s % 2) * NCH + c) * 512:((s % 2) * NCH + c + 1) * 512],
                                          start=(c == 0), stop=(c == NCH - 1))
                        ii.then_inc(mm2, 1)

            @b.scalar
            def _(e):
                n = 0
                for s in range(NSL):
                    for m in range(NCH):
                        n += 1
                        e.wait_ge(mm2, n)
                        if n >= 3:
                            e.wait_ge(so2[n % 2], (n // 2 - 1 + (n % 2)) * 16)
                        e.activation(ob[:, (n % 2) * 512:(n % 2 + 1) * 512],
                                     pss[n % 2][:], AF.Copy).then_inc(ev2, 1)




def emit_l3_attn(ctx, xn_d, s_d):
    nc = ctx.nc
    H = 16
    LH = 8
    for seq in range(NSEQ):
        t0 = seq * SEQ
        ctx.begin_stage()
        with ExitStack() as st:
            qf = ctx.sbuf(st, "l3_q", [128, NCH * SEQ], BF16)
            kf = ctx.sbuf(st, "l3_k", [128, NCH * SEQ], BF16)
            vt = ctx.sbuf(st, "l3_v", [128, 16 * 1024], BF16)
            gqf = ctx.sbuf(st, "l3_gq", [128, 4 * SEQ], BF16)
            xpb = ctx.sbuf(st, "l3_xp", [128, NCH * 64], BF16)
            gkb = ctx.sbuf(st, "l3_gk", [128, 4 * 64], BF16)
            gvb = ctx.sbuf(st, "l3_gv", [64, 512], BF16)
            glr = ctx.sbuf(st, "l3_glr", [128, 512], BF16)
            grr = ctx.sbuf(st, "l3_grr", [128, 512], BF16)
            stPP = ExitStack()
            pss = [ctx.psum(stPP, f"l3_ps{i}", [128, 512]) for i in range(2)]

            stp = ExitStack()
            wbuf = ctx.sbuf(stp, "l3_wbuf", [128, NCH * 1024], BF16)
            xs = ctx.sbuf(stp, "l3_xs", [128, 2 * NCH * 512], BF16)
            xpr = ctx.sbuf(stp, "l3_xpr", [128, NCH * 64], F32)

            wl = ctx.sem("l3_wl")
            xl = [ctx.sem("l3_xla"), ctx.sem("l3_xlb")]
            mm = ctx.sem("l3_mm")
            ev = ctx.sem("l3_ev")
            pr = ctx.sem("l3_pr")
            pp = ctx.sem("l3_pp")
            cnt = {"wl": 0, "xs": [0, 0], "mm": 0, "ev": 0, "pr": 0, "sl": 0}

            def phase(wname, wcols, mode, dst, pool=False):
                mchunks = wcols // 128
                wl0 = cnt["wl"]
                cnt["wl"] += NCH * 16
                with nc.Block() as b:
                    @b.gpsimd
                    def _(e):
                        for c in range(NCH):
                            e.dma_start(out=wbuf[:, c * wcols:(c + 1) * wcols],
                                        in_=ctx.dram[wname][c * 128:(c + 1) * 128, :]
                                        ).then_inc(wl, 16)

                    @b.sync
                    def _(e):
                        for s in range(4):
                            gsl = cnt["sl"] + s
                            if gsl >= 2:
                                e.wait_ge(mm, cnt["mm0"](gsl - 1))
                                if pool:
                                    e.wait_ge(pr, s - 1)
                            for c in range(NCH):
                                e.dma_start(
                                    out=xs[:, ((gsl % 2) * NCH + c) * 512:((gsl % 2) * NCH + c + 1) * 512],
                                    in_=xn_d[c * 128:(c + 1) * 128, t0 + s * 512:t0 + (s + 1) * 512]
                                ).then_inc(xl[gsl % 2], 16)

                    if pool:
                        @b.vector
                        def _(e):
                            for s in range(4):
                                gsl = cnt["sl"] + s
                                e.wait_ge(xl[gsl % 2], (gsl // 2 + 1) * NCH * 16)
                                for c in range(NCH):
                                    srcv = xs[:, ((gsl % 2) * NCH + c) * 512:((gsl % 2) * NCH + c + 1) * 512]
                                    ii = e.reduce_sum(xpr[:, c * 64 + s * 16:c * 64 + (s + 1) * 16],
                                                      srcv.rearrange("p (l w) -> p l w", w=32),
                                                      axis=AX.X)
                                ii.then_inc(pr, 1)
                            e.drain()
                            for c in range(NCH):
                                e.tensor_scalar(xpb[:, c * 64:(c + 1) * 64],
                                                xpr[:, c * 64:(c + 1) * 64],
                                                1.0 / 32.0, None, op0=ALU.mult)
                            e.drain()
                            e.nop().then_inc(pp, 1)

                    @b.tensor
                    def _(e):
                        e.wait_ge(wl, wl0 + NCH * 16)
                        for s in range(4):
                            gsl = cnt["sl"] + s
                            e.wait_ge(xl[gsl % 2], (gsl // 2 + 1) * NCH * 16)
                            if mode == "fm":
                                for m in range(mchunks):
                                    cnt["mm"] += 1
                                    n = cnt["mm"]
                                    if n >= 3:
                                        e.wait_ge(ev, n - 2)
                                    for c in range(NCH):
                                        ii = e.matmul(pss[n % 2][:],
                                                      wbuf[:, c * wcols + m * 128:c * wcols + (m + 1) * 128],
                                                      xs[:, ((gsl % 2) * NCH + c) * 512:((gsl % 2) * NCH + c + 1) * 512],
                                                      start=(c == 0), stop=(c == NCH - 1))
                                    ii.then_inc(mm, 1)
                            else:
                                for tt in range(4):
                                    for hf in range(2):
                                        cnt["mm"] += 1
                                        n = cnt["mm"]
                                        if n >= 3:
                                            e.wait_ge(ev, n - 2)
                                        for c in range(NCH):
                                            ii = e.matmul(
                                                pss[n % 2][:],
                                                xs[:, ((gsl % 2) * NCH + c) * 512 + tt * 128:
                                                   ((gsl % 2) * NCH + c) * 512 + (tt + 1) * 128],
                                                wbuf[:, c * wcols + hf * 512:c * wcols + (hf + 1) * 512],
                                                start=(c == 0), stop=(c == NCH - 1))
                                        ii.then_inc(mm, 1)

                    @b.scalar
                    def _(e):
                        for s in range(4):
                            if mode == "fm":
                                for m in range(mchunks):
                                    cnt["ev"] += 1
                                    n = cnt["ev"]
                                    e.wait_ge(mm, n)
                                    e.activation(dst[:, m * SEQ + s * 512:m * SEQ + (s + 1) * 512],
                                                 pss[n % 2][:], AF.Copy).then_inc(ev, 1)
                            else:
                                for tt in range(4):
                                    for hf in range(2):
                                        cnt["ev"] += 1
                                        n = cnt["ev"]
                                        e.wait_ge(mm, n)
                                        kt = s * 4 + tt
                                        e.activation(vt[:, kt * 1024 + hf * 512:kt * 1024 + (hf + 1) * 512],
                                                     pss[n % 2][:], AF.Copy).then_inc(ev, 1)
                cnt["sl"] += 4

            per_sl = {"l3_wq": NCH, "l3_wk": NCH, "l3_wv": NCH, "l3_wgq": 4}

            def _mm0(gsl):
                total = 0
                order = [("l3_wq", NCH), ("l3_wk", NCH), ("l3_wv", NCH), ("l3_wgq", 4)]
                g = 0
                for name, per in order:
                    for s in range(4):
                        total += per
                        if g == gsl:
                            return total
                        g += 1
                return total
            cnt["mm0"] = _mm0

            phase("l3_wq", 1024, "fm", qf, pool=True)
            phase("l3_wk", 1024, "fm", kf)
            phase("l3_wv", 1024, "tm", None)
            phase("l3_wgq", 512, "fm", gqf)

            gkv = ctx.sem("l3_gkv")
            with nc.Block() as b:
                @b.gpsimd
                def _(e):
                    for c in range(NCH):
                        e.dma_start(out=wbuf[:, c * 512:(c + 1) * 512],
                                    in_=ctx.dram["l3_wgk"][c * 128:(c + 1) * 128, :]).then_inc(gkv, 16)
                    for c in range(NCH):
                        e.dma_start(out=wbuf[:, NCH * 512 + c * 512:NCH * 512 + (c + 1) * 512],
                                    in_=ctx.dram["l3_wgv"][c * 128:(c + 1) * 128, :]).then_inc(gkv, 16)
                    e.dma_start(out=glr[:], in_=ctx.dram["l3_glrep"][:, :]).then_inc(gkv, 16)
                    e.dma_start(out=grr[:], in_=ctx.dram["l3_grrep"][:, :]).then_inc(gkv, 16)

                @b.tensor
                def _(e):
                    e.wait_ge(gkv, (2 * NCH + 2) * 16)
                    e.wait_ge(pp, 1)
                    for m in range(4):
                        for c in range(NCH):
                            ii = e.matmul(pss[0][0:128, 0:64],
                                          wbuf[:, c * 512 + m * 128:c * 512 + (m + 1) * 128],
                                          xpb[:, c * 64:(c + 1) * 64],
                                          start=(c == 0), stop=(c == NCH - 1))
                        ii.then_inc(mm, 1)
                        e.wait_ge(ev, cnt["ev"] + m + 1)
                    for c in range(NCH):
                        ii = e.matmul(pss[1][0:64, :],
                                      xpb[:, c * 64:(c + 1) * 64],
                                      wbuf[:, NCH * 512 + c * 512:NCH * 512 + (c + 1) * 512],
                                      start=(c == 0), stop=(c == NCH - 1))
                    ii.then_inc(mm, 1)

                @b.scalar
                def _(e):
                    for m in range(4):
                        e.wait_ge(mm, cnt["mm"] + m + 1)
                        e.activation(gkb[:, m * 64:(m + 1) * 64],
                                     pss[0][0:128, 0:64], AF.Copy).then_inc(ev, 1)
                    e.wait_ge(mm, cnt["mm"] + 5)
                    e.activation(gvb[:], pss[1][0:64, :], AF.Copy).then_inc(ev, 1)

            stp.close()
            st2 = st.enter_context(ExitStack())
            att = ctx.sbuf(st2, "l3_att", [128, NCH * SEQ], BF16)
            ra = ctx.sbuf(st2, "l3_ra", [128, 4 * SEQ], BF16)

            stPP.close()
            if DBG_SUB < 1:
                continue
            stFb = ExitStack()
            msk = ctx.sbuf(stFb, "l3_msk", [128, 3 * 1024], F32)
            lmsk = ctx.sbuf(stFb, "l3_lmsk", [64, SEQ], F32)
            pt_sb = ctx.sbuf(stFb, "l3_pt", [128, 4 * 512], BF16)
            rln = ctx.sbuf(stFb, "l3_rln", [64, 2 * 512], F32)
            rin = ctx.sbuf(stFb, "l3_rin", [64, 2 * 512], F32)
            gle = ctx.sbuf(stFb, "l3_gle", [64, 2 * 512], F32)
            onc = ctx.g["ones_col"]
            ps_r = ctx.psum(st, "l3_psr", [64, 512])
            ps_av = ctx.psum(st, "l3_psav", [64, 512])
            stF = ExitStack()
            ps_st = [ctx.psum(stF, f"l3_pst{i}", [128, 1024]) for i in range(2)]
            ps_gl = ctx.psum(stF, "l3_psgl", [64, 512])

            ml = ctx.sem("l3_ml")
            sm = ctx.sem("l3_sm")
            pa = ctx.sem("l3_pa")
            ex = ctx.sem("l3_ex")
            cs = ctx.sem("l3_cs")
            rr = ctx.sem("l3_rr")
            gv2 = ctx.sem("l3_gv2")
            fz = ctx.sem("l3_fz")

            groups = [(qt, h) for qt in range(4) for h in range(H)]

            def kps_of(qt):
                if qt == 0:
                    return [(0, 1), (2, 2)]
                return [(4 * qt - 2, 0), (4 * qt, 1), (4 * qt + 2, 2)]

            with nc.Block() as b:
                @b.sync
                def _(e):
                    for pp_ in range(3):
                        e.dma_start(out=msk[:, pp_ * 1024:(pp_ + 1) * 1024],
                                    in_=ctx.dram["c_mask_win2"][pp_ * 128:(pp_ + 1) * 128, :]
                                    ).then_inc(ml, 16)

                @b.vector
                def _(e):
                    np_ = 0
                    e.wait_ge(ml, 3 * 16)
                    for gi, (qt, h) in enumerate(groups):
                        for k0, mp in kps_of(qt):
                            np_ += 1
                            e.wait_ge(sm, np_)
                            e.tensor_tensor(ps_st[np_ % 2][:], ps_st[np_ % 2][:],
                                            msk[:, mp * 1024:(mp + 1) * 1024],
                                            op=ALU.add).then_inc(pa, 1)
                        e.wait_ge(rr, 3 * (gi + 1))
                        e.tensor_tensor(gle[:, (gi % 2) * 512:(gi % 2 + 1) * 512],
                                        gle[:, (gi % 2) * 512:(gi % 2 + 1) * 512],
                                        rin[:, (gi % 2) * 512:(gi % 2 + 1) * 512],
                                        op=ALU.mult).then_inc(gv2, 1)
                        e.drain()
                        ch, hp = h // 2, (h % 2) * 64
                        e.tensor_tensor(att[hp:hp + 64, ch * SEQ + qt * 512:ch * SEQ + (qt + 1) * 512],
                                        ps_av[:], gle[:, (gi % 2) * 512:(gi % 2 + 1) * 512],
                                        op=ALU.mult).then_inc(fz, 1)

                @b.tensor
                def _(e):
                    np_ = 0
                    for gi, (qt, h) in enumerate(groups):
                        ch, hp = h // 2, (h % 2) * 64
                        kps = kps_of(qt)
                        nkp = len(kps)
                        if gi >= 1:
                            e.wait_ge(rr, 3 * (gi - 1) + 1)
                        e.matmul(ps_gl[:], glr[(h % 2) * 64:(h % 2) * 64 + 64,
                                              (h // 2) * 64:(h // 2) * 64 + 64],
                                 qf[hp:hp + 64, ch * SEQ + qt * 512:ch * SEQ + (qt + 1) * 512],
                                 start=True, stop=True)
                        for ki, (k0, mp) in enumerate(kps):
                            np_ += 1
                            if ki == 1 and gi >= 1:
                                e.wait_ge(fz, gi)
                            if np_ >= 3:
                                e.wait_ge(ex, np_ - 2)
                            for half in range(2):
                                kt = k0 + half
                                ii = e.matmul(ps_st[np_ % 2][:, half * 512:(half + 1) * 512],
                                              kf[hp:hp + 64, ch * SEQ + kt * 128:ch * SEQ + (kt + 1) * 128],
                                              qf[hp:hp + 64, ch * SEQ + qt * 512:ch * SEQ + (qt + 1) * 512],
                                              start=True, stop=True)
                            ii.then_inc(sm, 1)
                            if ki >= 1:
                                e.wait_ge(ex, np_ - 1)
                                pb = ((np_ - 1) % 2) * 1024
                                pk0 = kps[ki - 1][0]
                                for half in range(2):
                                    kt = pk0 + half
                                    e.matmul(ps_r[:], onc[:],
                                             pt_sb[:, pb + half * 512:pb + (half + 1) * 512],
                                             start=(ki == 1 and half == 0), stop=False)
                                    e.matmul(ps_av[:],
                                             vt[:, kt * 1024 + h * 64:kt * 1024 + (h + 1) * 64],
                                             pt_sb[:, pb + half * 512:pb + (half + 1) * 512],
                                             start=(ki == 1 and half == 0),
                                             stop=False).then_inc(cs, 1)
                        e.wait_ge(ex, np_)
                        pb = (np_ % 2) * 1024
                        for half in range(2):
                            kt = kps[-1][0] + half
                            e.matmul(ps_r[:], onc[:],
                                     pt_sb[:, pb + half * 512:pb + (half + 1) * 512],
                                     start=False, stop=(half == 1))
                            e.matmul(ps_av[:],
                                     vt[:, kt * 1024 + h * 64:kt * 1024 + (h + 1) * 64],
                                     pt_sb[:, pb + half * 512:pb + (half + 1) * 512],
                                     start=False, stop=(half == 1)).then_inc(cs, 1)

                @b.scalar
                def _(e):
                    np_ = 0
                    csn = 0
                    for gi, (qt, h) in enumerate(groups):
                        kps = kps_of(qt)
                        for k0, mp in kps:
                            np_ += 1
                            e.wait_ge(pa, np_)
                            if np_ >= 3:
                                e.wait_ge(cs, 2 * (np_ - 2))
                            e.activation(pt_sb[:, (np_ % 2) * 1024:(np_ % 2 + 1) * 1024],
                                         ps_st[np_ % 2][:], AF.Exp, scale=0.125,
                                         bias=ctx.g["neg_smc"][:]).then_inc(ex, 1)
                        csn += 2 * len(kps)
                        e.wait_ge(cs, csn)
                        if gi >= 2:
                            e.wait_ge(fz, gi - 1)
                        e.wait_ge(sm, np_)
                        e.activation(gle[:, (gi % 2) * 512:(gi % 2 + 1) * 512],
                                     ps_gl[:], AF.Sigmoid).then_inc(rr, 1)
                        e.activation(rln[:, (gi % 2) * 512:(gi % 2 + 1) * 512],
                                     ps_r[:], AF.Ln).then_inc(rr, 1)
                        e.drain()
                        e.activation(rin[:, (gi % 2) * 512:(gi % 2 + 1) * 512],
                                     rln[:, (gi % 2) * 512:(gi % 2 + 1) * 512],
                                     AF.Exp, scale=-1.0).then_inc(rr, 1)

            stF.close()
            if DBG_SUB < 2:
                continue
            stL = ExitStack()
            ps_stl = [ctx.psum(stL, f"l3_pstl{i}", [64, 512]) for i in range(2)]

            lml = ctx.sem("l3_lml")
            sml = ctx.sem("l3_sml")
            pal = ctx.sem("l3_pal")
            exl = ctx.sem("l3_exl")
            csl = ctx.sem("l3_csl")
            rrl = ctx.sem("l3_rrl")
            fzl = ctx.sem("l3_fzl")

            lgroups = [(qt, lh) for qt in range(4) for lh in range(LH)]

            with nc.Block() as b:
                @b.sync
                def _(e):
                    e.dma_start(out=lmsk[:], in_=ctx.dram["c_mask_lat"][:, :]).then_inc(lml, 16)

                @b.vector
                def _(e):
                    e.wait_ge(lml, 16)
                    for gi, (qt, lh) in enumerate(lgroups):
                        e.wait_ge(sml, gi + 1)
                        e.tensor_tensor(ps_stl[gi % 2][:], ps_stl[gi % 2][:],
                                        lmsk[:, qt * 512:(qt + 1) * 512],
                                        op=ALU.add).then_inc(pal, 1)
                        e.wait_ge(rrl, 2 * (gi + 1))
                        ch, hp = lh // 2, (lh % 2) * 64
                        e.tensor_tensor(ra[hp:hp + 64, ch * SEQ + qt * 512:ch * SEQ + (qt + 1) * 512],
                                        ps_av[:], rin[:, (gi % 2) * 512:(gi % 2 + 1) * 512],
                                        op=ALU.mult).then_inc(fzl, 1)

                @b.tensor
                def _(e):
                    for gi, (qt, lh) in enumerate(lgroups):
                        if gi >= 2:
                            e.wait_ge(exl, gi - 1)
                        ch, hp = lh // 2, (lh % 2) * 64
                        e.matmul(ps_stl[gi % 2][:],
                                 gkb[hp:hp + 64, ch * 64:(ch + 1) * 64],
                                 gqf[hp:hp + 64, ch * SEQ + qt * 512:ch * SEQ + (qt + 1) * 512],
                                 start=True, stop=True).then_inc(sml, 1)
                        e.wait_ge(exl, gi + 1)
                        if gi >= 1:
                            e.wait_ge(fzl, gi)
                        e.matmul(ps_r[:], onc[0:64, :],
                                 pt_sb[0:64, (gi % 4) * 512:(gi % 4 + 1) * 512],
                                 start=True, stop=True)
                        e.matmul(ps_av[:],
                                 gvb[:, lh * 64:(lh + 1) * 64],
                                 pt_sb[0:64, (gi % 4) * 512:(gi % 4 + 1) * 512],
                                 start=True, stop=True).then_inc(csl, 1)

                @b.scalar
                def _(e):
                    for gi, (qt, lh) in enumerate(lgroups):
                        e.wait_ge(pal, gi + 1)
                        if gi >= 4:
                            e.wait_ge(csl, gi - 3)
                        e.activation(pt_sb[0:64, (gi % 4) * 512:(gi % 4 + 1) * 512],
                                     ps_stl[gi % 2][:], AF.Exp, scale=0.125,
                                     bias=ctx.g["neg_smc"][0:64, :]).then_inc(exl, 1)
                        e.wait_ge(csl, gi + 1)
                        if gi >= 2:
                            e.wait_ge(fzl, gi - 1)
                        e.activation(rln[:, (gi % 2) * 512:(gi % 2 + 1) * 512],
                                     ps_r[:], AF.Ln).then_inc(rrl, 1)
                        e.drain()
                        e.activation(rin[:, (gi % 2) * 512:(gi % 2 + 1) * 512],
                                     rln[:, (gi % 2) * 512:(gi % 2 + 1) * 512],
                                     AF.Exp, scale=-1.0).then_inc(rrl, 1)

            stL.close()
            stFb.close()
            if DBG_SUB < 3:
                continue
            wro = ctx.sbuf(st, "l3_wro", [128, 4 * 1024], BF16)
            grs = ctx.sbuf(st, "l3_grs", [128, 512], F32)
            tmpv = ctx.sbuf(st, "l3_tmpv", [128, 512], F32)
            stR = ExitStack()
            ps_rm = [ctx.psum(stR, f"l3_psrm{i}", [128, 512]) for i in range(2)]
            ps_g2 = ctx.psum(stR, "l3_psg2", [128, 512])

            wl3 = ctx.sem("l3_wl3")
            rm = ctx.sem("l3_rm")
            gm = ctx.sem("l3_gm")
            ge3 = ctx.sem("l3_ge3")
            cv3 = ctx.sem("l3_cv3")

            with nc.Block() as b:
                @b.gpsimd
                def _(e):
                    for c in range(4):
                        e.dma_start(out=wro[:, c * 1024:(c + 1) * 1024],
                                    in_=ctx.dram["l3_r_out_w"][c * 128:(c + 1) * 128, :]
                                    ).then_inc(wl3, 16)

                @b.tensor
                def _(e):
                    e.wait_ge(wl3, 4 * 16)
                    n = 0
                    for sl in range(4):
                        for m in range(NCH):
                            n += 1
                            if n >= 3:
                                e.wait_ge(cv3, n - 2)
                            for c in range(4):
                                e.matmul(ps_rm[n % 2][:],
                                         wro[:, c * 1024 + m * 128:c * 1024 + (m + 1) * 128],
                                         ra[:, c * SEQ + sl * 512:c * SEQ + (sl + 1) * 512],
                                         start=(c == 0), stop=(c == 3))
                            if n >= 2:
                                e.wait_ge(ge3, n - 1)
                            for sub in range(2):
                                h = 2 * m + sub
                                ii = e.matmul(ps_g2[sub * 64:(sub + 1) * 64, :],
                                              grr[sub * 64:(sub + 1) * 64,
                                                  (h // 2) * 64:(h // 2) * 64 + 64],
                                              qf[sub * 64:(sub + 1) * 64,
                                                 m * SEQ + sl * 512:m * SEQ + (sl + 1) * 512],
                                              start=True, stop=True)
                            ii.then_inc(rm, 1)

                @b.scalar
                def _(e):
                    n = 0
                    for sl in range(4):
                        for m in range(NCH):
                            n += 1
                            e.wait_ge(rm, n)
                            if n >= 2:
                                e.wait_ge(cv3, n - 1)
                            e.activation(grs[:], ps_g2[:], AF.Sigmoid).then_inc(ge3, 1)

                @b.vector
                def _(e):
                    n = 0
                    for sl in range(4):
                        for m in range(NCH):
                            n += 1
                            e.wait_ge(ge3, n)
                            e.tensor_tensor(tmpv[:], ps_rm[n % 2][:], grs[:], op=ALU.mult)
                            e.drain()
                            asl = slice(m * SEQ + sl * 512, m * SEQ + (sl + 1) * 512)
                            e.tensor_tensor(att[:, asl], att[:, asl], tmpv[:],
                                            op=ALU.add).then_inc(cv3, 1)
                            e.drain()

            stR.close()
            wo3 = ctx.sbuf(st, "l3_wo3", [128, NCH * 1024], BF16)
            ob3 = ctx.sbuf(st, "l3_ob3", [128, 2 * 512], F32)
            pss = [ctx.psum(st, f"l3_pso{i}", [128, 512]) for i in range(2)]
            wl4 = ctx.sem("l3_wl4")
            mm4 = ctx.sem("l3_mm4")
            ev4 = ctx.sem("l3_ev4")
            so4 = [ctx.sem("l3_so4a"), ctx.sem("l3_so4b")]

            with nc.Block() as b:
                @b.gpsimd
                def _(e):
                    for c in range(NCH):
                        e.dma_start(out=wo3[:, c * 1024:(c + 1) * 1024],
                                    in_=ctx.dram["l3_w_out"][c * 128:(c + 1) * 128, :]
                                    ).then_inc(wl4, 16)

                @b.tensor
                def _(e):
                    e.wait_ge(wl4, NCH * 16)
                    n = 0
                    for s in range(4):
                        for m in range(NCH):
                            n += 1
                            if n >= 3:
                                e.wait_ge(ev4, n - 2)
                            for c in range(NCH):
                                ii = e.matmul(pss[n % 2][:],
                                              wo3[:, c * 1024 + m * 128:c * 1024 + (m + 1) * 128],
                                              att[:, c * SEQ + s * 512:c * SEQ + (s + 1) * 512],
                                              start=(c == 0), stop=(c == NCH - 1))
                            ii.then_inc(mm4, 1)

                @b.scalar
                def _(e):
                    n = 0
                    for s in range(4):
                        for m in range(NCH):
                            n += 1
                            e.wait_ge(mm4, n)
                            if n >= 3:
                                e.wait_ge(so4[n % 2], (n // 2 - 1 + (n % 2)) * 16)
                            e.activation(ob3[:, (n % 2) * 512:(n % 2 + 1) * 512],
                                         pss[n % 2][:], AF.Copy).then_inc(ev4, 1)

                @b.sync
                def _(e):
                    n = 0
                    for s in range(4):
                        for m in range(NCH):
                            n += 1
                            e.wait_ge(ev4, n)
                            e.dma_start(out=s_d[m * 128:(m + 1) * 128,
                                               t0 + s * 512:t0 + (s + 1) * 512],
                                        in_=ob3[:, (n % 2) * 512:(n % 2 + 1) * 512]
                                        ).then_inc(so4[n % 2], 16)

                @b.gpsimd
                def _(e):
                    e.wait_ge(so4[0], (4 * NCH // 2) * 16)
                    e.wait_ge(so4[1], (4 * NCH // 2) * 16)


def emit_l3_ffn_combine(ctx, xn_d, s_d, x_src, x_dst):
    nc = ctx.nc
    ctx.begin_stage()
    NSL = T // 512
    NHC = 32
    with ExitStack() as st:
        xsn = ctx.sbuf(st, "f3_xsn", [128, 2 * NCH * 512], BF16)
        w1t = ctx.sbuf(st, "f3_w1t", [128, 2 * NCH * 128], BF16)
        w2t = ctx.sbuf(st, "f3_w2t", [128, 2 * NHC * 128], BF16)
        hsb = ctx.sbuf(st, "f3_h", [128, NHC * 512], BF16)
        b1c3 = ctx.sbuf(st, "f3_b1c", [128, NHC], F32)
        b2r = ctx.sbuf(st, "f3_b2r", [1, 1024], BF16)
        x2 = ctx.sbuf(st, "f3_x2", [128, NCH * 512], F32)
        xsq = ctx.sbuf(st, "f3_xsq", [128, 2 * 512], F32)
        mrow = ctx.sbuf(st, "f3_mrow", [1, 512], F32)
        qrow = ctx.sbuf(st, "f3_qrow", [1, 512], F32)
        rrow = ctx.sbuf(st, "f3_rrow", [1, 512], F32)
        mu_bc = ctx.sbuf(st, "f3_mubc", [128, 512], F32)
        rs_bc = ctx.sbuf(st, "f3_rsbc", [128, 512], F32)
        gcol = ctx.sbuf(st, "f3_gcol", [128, 8], F32)
        bcol = ctx.sbuf(st, "f3_bcol", [128, 8], F32)
        xhat = ctx.sbuf(st, "f3_xhat", [128, 512], F32)
        gsb = ctx.sbuf(st, "f3_gsb", [128, NCH * 512], BF16)
        sxr = ctx.sbuf(st, "f3_sxr", [128, NCH * 512], F32)
        xxr = ctx.sbuf(st, "f3_xxr", [128, NCH * 512], F32)
        xov = ctx.sbuf(st, "f3_xov", [128, 2 * NCH * 512], F32)
        orow = ctx.g["ones_row"]
        ones128 = ctx.g["ones128_f"]
        onesrow = ctx.g["onesrow_f"]
        ps_h = [ctx.psum(st, f"f3_psh{i}", [128, 512]) for i in range(2)]
        ps_y = [ctx.psum(st, f"f3_psy{i}", [128, 512]) for i in range(2)]
        ps_s1 = ctx.psum(st, "f3_pss1", [128, 512])
        ps_s2 = ctx.psum(st, "f3_pss2", [128, 512])

        wl = ctx.sem("f3_wl")
        w1l = [ctx.sem("f3_w1l0"), ctx.sem("f3_w1l1")]
        w2l = [ctx.sem("f3_w2l0"), ctx.sem("f3_w2l1")]
        xl = [ctx.sem("f3_xl0"), ctx.sem("f3_xl1")]
        cl = ctx.sem("f3_cl")
        hm = ctx.sem("f3_hm")
        he = ctx.sem("f3_he")
        ym = ctx.sem("f3_ym")
        x2v = ctx.sem("f3_x2v")
        sqs = ctx.sem("f3_sqs")
        stm = ctx.sem("f3_stm")
        rowv = ctx.sem("f3_rowv")
        rows = ctx.sem("f3_rows")
        bcm = ctx.sem("f3_bcm")
        bcv = ctx.sem("f3_bcv")
        xhv = ctx.sem("f3_xhv")
        gss = ctx.sem("f3_gss")
        cb = ctx.sem("f3_cb")
        so = [ctx.sem("f3_so0"), ctx.sem("f3_so1")]

        with nc.Block() as b:
            @b.gpsimd
            def _(e):
                e.dma_start(out=b1c3[:], in_=ctx.dram["l3_b1c"][:, :]).then_inc(wl, 16)
                e.dma_start(out=b2r[:], in_=ctx.dram["l3_b2"][:, :]).then_inc(wl, 16)
                e.dma_start(out=gcol[:], in_=ctx.dram["l3_gamma_c"][:, :]).then_inc(wl, 16)
                e.dma_start(out=bcol[:], in_=ctx.dram["l3_beta_c"][:, :]).then_inc(wl, 16)
                for s in range(NSL):
                    for fc in range(NHC):
                        gfc = s * NHC + fc
                        if gfc >= 2:
                            e.wait_ge(hm, gfc - 1)
                        e.dma_start(
                            out=w1t[:, (gfc % 2) * NCH * 128:(gfc % 2 + 1) * NCH * 128],
                            in_=ctx.dram["l3_w1s"][fc]).then_inc(w1l[gfc % 2], 16)
                    for m in range(NCH):
                        gm = s * NCH + m
                        if gm >= 2:
                            e.wait_ge(ym, gm - 1)
                        e.dma_start(
                            out=w2t[:, (gm % 2) * NHC * 128:(gm % 2 + 1) * NHC * 128],
                            in_=ctx.dram["l3_w2s"][m]).then_inc(w2l[gm % 2], 16)

            @b.sync
            def _(e):
                for s in range(NSL):
                    if s >= 1:
                        e.wait_ge(cb, s * NCH)
                        for c in range(NCH):
                            e.dma_start(out=x_dst[c * 128:(c + 1) * 128,
                                                  (s - 1) * 512:s * 512],
                                        in_=xov[:, ((s - 1) % 2) * NCH * 512 + c * 512:
                                                ((s - 1) % 2) * NCH * 512 + (c + 1) * 512]
                                        ).then_inc(so[(s - 1) % 2], 16)
                    if s >= 2:
                        e.wait_ge(x2v, (s - 1) * NCH)
                    for c in range(NCH):
                        e.dma_start(
                            out=xsn[:, ((s % 2) * NCH + c) * 512:((s % 2) * NCH + c + 1) * 512],
                            in_=xn_d[c * 128:(c + 1) * 128, s * 512:(s + 1) * 512]
                        ).then_inc(xl[s % 2], 16)
                    for c in range(NCH):
                        e.dma_start(out=sxr[:, c * 512:(c + 1) * 512],
                                    in_=s_d[c * 128:(c + 1) * 128, s * 512:(s + 1) * 512]
                                    ).then_inc(cl, 16)
                    for c in range(NCH):
                        e.dma_start(out=xxr[:, c * 512:(c + 1) * 512],
                                    in_=x_src[c * 128:(c + 1) * 128, s * 512:(s + 1) * 512]
                                    ).then_inc(cl, 16)
                e.wait_ge(cb, NSL * NCH)
                for c in range(NCH):
                    e.dma_start(out=x_dst[c * 128:(c + 1) * 128, (NSL - 1) * 512:NSL * 512],
                                in_=xov[:, ((NSL - 1) % 2) * NCH * 512 + c * 512:
                                        ((NSL - 1) % 2) * NCH * 512 + (c + 1) * 512]
                                ).then_inc(so[(NSL - 1) % 2], 16)
                e.wait_ge(so[0], (NSL // 2) * NCH * 16)
                e.wait_ge(so[1], (NSL // 2) * NCH * 16)

            @b.tensor
            def _(e):
                e.wait_ge(wl, 4 * 16)
                nh = 0
                for s in range(NSL):
                    e.wait_ge(xl[s % 2], (s // 2 + 1) * NCH * 16)
                    for fc in range(NHC):
                        gfc = s * NHC + fc
                        nh += 1
                        e.wait_ge(w1l[gfc % 2], (gfc // 2 + 1) * 16)
                        if nh >= 3:
                            e.wait_ge(he, nh - 2)
                        for c in range(NCH):
                            ii = e.matmul(ps_h[nh % 2][:],
                                          w1t[:, (gfc % 2) * NCH * 128 + c * 128:
                                              (gfc % 2) * NCH * 128 + (c + 1) * 128],
                                          xsn[:, ((s % 2) * NCH + c) * 512:((s % 2) * NCH + c + 1) * 512],
                                          start=(c == 0), stop=(c == NCH - 1))
                        ii.then_inc(hm, 1)
                    e.wait_ge(he, (s + 1) * NHC)
                    for m in range(NCH):
                        nm = s * NCH + m
                        e.wait_ge(w2l[nm % 2], (nm // 2 + 1) * 16)
                        if nm >= 2:
                            e.wait_ge(x2v, nm - 1)
                        for fc in range(NHC):
                            e.matmul(ps_y[nm % 2][:],
                                     w2t[:, (nm % 2) * NHC * 128 + fc * 128:
                                         (nm % 2) * NHC * 128 + (fc + 1) * 128],
                                     hsb[:, fc * 512:(fc + 1) * 512],
                                     start=(fc == 0), stop=False)
                        e.matmul(ps_y[nm % 2][:], b2r[:, m * 128:(m + 1) * 128],
                                 orow[:], start=False, stop=True).then_inc(ym, 1)
                        e.wait_ge(sqs, nm + 1)
                        e.matmul(ps_s1[0:1, :], ones128[:], x2[:, m * 512:(m + 1) * 512],
                                 start=(m == 0), stop=(m == NCH - 1))
                        e.matmul(ps_s2[0:1, :], ones128[:], xsq[:, (nm % 2) * 512:(nm % 2 + 1) * 512],
                                 start=(m == 0), stop=(m == NCH - 1)).then_inc(stm, 1)
                    e.wait_ge(rows, s + 1)
                    e.matmul(ps_h[0][:], onesrow[:], mrow[:, :], start=True, stop=True)
                    e.matmul(ps_h[1][:], onesrow[:], rrow[:, :],
                             start=True, stop=True).then_inc(bcm, 1)
                    e.wait_ge(bcv, s + 1)

            @b.scalar
            def _(e):
                for s in range(NSL):
                    nh0 = s * NHC
                    for fc in range(NHC):
                        nh = nh0 + fc + 1
                        e.wait_ge(hm, nh)
                        e.activation(hsb[:, fc * 512:(fc + 1) * 512],
                                     ps_h[nh % 2][:], AF.Relu,
                                     bias=b1c3[:, fc:fc + 1]).then_inc(he, 1)
                    for m in range(NCH):
                        nm = s * NCH + m
                        e.wait_ge(x2v, nm + 1)
                        e.activation(xsq[:, (nm % 2) * 512:(nm % 2 + 1) * 512],
                                     x2[:, m * 512:(m + 1) * 512], AF.Square).then_inc(sqs, 1)
                    e.wait_ge(rowv, s * 2 + 2)
                    e.activation(rrow[:], qrow[:], AF.Ln)
                    e.drain()
                    e.activation(rrow[:], rrow[:], AF.Exp, scale=-0.5).then_inc(rows, 1)
                    for c in range(NCH):
                        e.wait_ge(xhv, s * NCH + c + 1)
                        e.activation(gsb[:, c * 512:(c + 1) * 512], xhat[:],
                                     AF.Silu, scale=gcol[:, c:c + 1],
                                     bias=bcol[:, c:c + 1]).then_inc(gss, 1)

            @b.vector
            def _(e):
                for s in range(NSL):
                    for m in range(NCH):
                        nm = s * NCH + m
                        e.wait_ge(ym, nm + 1)
                        e.tensor_tensor(x2[:, m * 512:(m + 1) * 512], ps_y[nm % 2][:],
                                        xsn[:, ((s % 2) * NCH + m) * 512:((s % 2) * NCH + m + 1) * 512],
                                        op=ALU.add).then_inc(x2v, 1)
                    e.wait_ge(stm, (s + 1) * NCH)
                    e.tensor_scalar(mrow[:], ps_s1[0:1, :], 1.0 / D, None, op0=ALU.mult)
                    e.tensor_scalar(qrow[:], ps_s2[0:1, :], 1.0 / D, None, op0=ALU.mult)
                    e.drain()
                    e.tensor_tensor(rrow[:], mrow[:], mrow[:], op=ALU.mult)
                    e.drain()
                    e.tensor_tensor(qrow[:], qrow[:], rrow[:], op=ALU.subtract)
                    e.drain()
                    e.tensor_scalar(qrow[:], qrow[:], 1e-5, None, op0=ALU.add).then_inc(rowv, 2)
                    e.wait_ge(bcm, s + 1)
                    e.tensor_copy(mu_bc[:], ps_h[0][:])
                    e.tensor_copy(rs_bc[:], ps_h[1][:]).then_inc(bcv, 1)
                    e.drain()
                    e.wait_ge(cl, (s + 1) * 2 * NCH * 16)
                    if s >= 2:
                        e.wait_ge(so[s % 2], (s // 2) * NCH * 16)
                    for c in range(NCH):
                        e.tensor_tensor(xhat[:], x2[:, c * 512:(c + 1) * 512],
                                        mu_bc[:], op=ALU.subtract)
                        e.drain()
                        e.tensor_tensor(xhat[:], xhat[:], rs_bc[:],
                                        op=ALU.mult).then_inc(xhv, 1)
                        e.drain()
                        e.wait_ge(gss, s * NCH + c + 1)
                        e.tensor_tensor(xhat[:], gsb[:, c * 512:(c + 1) * 512],
                                        sxr[:, c * 512:(c + 1) * 512], op=ALU.mult)
                        e.drain()
                        e.tensor_tensor(xov[:, (s % 2) * NCH * 512 + c * 512:
                                            (s % 2) * NCH * 512 + (c + 1) * 512],
                                        xxr[:, c * 512:(c + 1) * 512],
                                        xhat[:], op=ALU.add).then_inc(cb, 1)
                        e.drain()



def _run_layers(layers, x_shards, w, c):
    wshapes = {k: list(v.shape) for k, v in w.items()}
    cshapes = {k: list(v.shape) for k, v in c.items()}
    nc = build(layers, wshapes, cshapes)
    names = set()
    for L in layers:
        names.update(LAYER_WEIGHTS[L])
        names.update(LAYER_CONSTS[L])
    shared = {}
    for n in sorted(names):
        shared[n] = w[n] if n in w else c[n]
    in_maps = [dict(shared, x_in=x_shards[i]) for i in range(NCORES)]
    res = run_bass_kernel_spmd(nc, in_maps, core_ids=list(range(NCORES)))
    return [r["x_out"] for r in res.results]


LAYER_GROUPS = [[0, 1, 2, 3]]


def kernel(**inputs):
    w = _prep_weights(inputs)
    c = _prep_consts()
    x_shards = _shard_x(np.asarray(inputs["L0_attn_x"], dtype=np.float32))
    for group in LAYER_GROUPS:
        x_shards = _run_layers(group, x_shards, w, c)
    return _unshard_out(x_shards)
